# version 07
# baseline (speedup 1.0000x reference)
import contextlib
import numpy as np
import concourse.bass as bass
import concourse.mybir as mybir
from concourse.bass_utils import run_bass_kernel_spmd

F32 = mybir.dt.float32
BF16 = mybir.dt.bfloat16
I32 = mybir.dt.int32
AF = mybir.ActivationFunctionType
ALU = mybir.AluOpType
AXX = mybir.AxisListType.X

S = 2048
HID = 2048
PROJ = 10304
INTER = 4096
CONVD = 6144
NH, HD, NG, SS, CH = 64, 64, 8, 128, 256
NCH = S // CH
I_FFN = 5632
O_FFN = 11264
KT = HID // 128
NEG = -1.0e30
R127 = float(np.float32(1.0 / 127.0))

ENGS = ("tensor", "vector", "scalar", "gpsimd", "sync")


class Sem:
    __slots__ = ("h", "c")

    def __init__(self, h):
        self.h = h
        self.c = 0


class Prog:

    def __init__(self, nc, auto_sems):
        self.nc = nc
        self.ops = {e: [] for e in ENGS}
        self.auto = auto_sems

    def _emit(self, eng, fn, w, extra_inc):
        self.ops[eng].append((tuple((s.h, v) for s, v in w), fn, extra_inc))

    def t(self, fn, w=(), inc=False):
        sem = self.auto["tensor"]
        if inc:
            sem.c += 1
            self._emit("tensor", fn, w, (sem.h, 1))
            return sem.c
        self._emit("tensor", fn, w, None)
        return None

    def _auto(self, eng, fn, w):
        sem = self.auto[eng]
        waits = ((sem.h, sem.c),) + tuple((s.h, v) for s, v in w)
        sem.c += 1
        self.ops[eng].append((waits, fn, (sem.h, 1)))
        return sem.c

    def v(self, fn, w=()):
        return self._auto("vector", fn, w)

    def s(self, fn, w=()):
        return self._auto("scalar", fn, w)

    def g(self, fn, w=()):
        return self._auto("gpsimd", fn, w)

    def dma(self, eng, fn, w=(), dsem=None, n=16):
        dsem.c += n
        self._emit(eng, fn, w, (dsem.h, n))
        return dsem.c

    def y(self, fn, w=(), inc=None, n=16):
        if inc is None:
            self._emit("sync", fn, w, None)
            return None
        return self.dma("sync", fn, w, inc, n)

    def barrier(self, bar, dma_sems=()):
        for e in ("vector", "scalar", "gpsimd", "tensor"):
            sem = self.auto[e]
            self._emit(e, lambda en: en.nop(), ((sem, sem.c),), (bar.h, 1))
        self._emit("sync", lambda en: en.nop(),
                   tuple((s, s.c) for s in dma_sems), (bar.h, 1))
        bar.c += 5
        t = bar.c
        for e in ENGS:
            self._emit(e, lambda en: en.nop(), ((bar, t),), None)

    def replay(self, eng):
        lst = self.ops[eng]

        def f(e):
            for waits, fn, inc in lst:
                for (h, v) in waits:
                    e.wait_ge(h, v)
                inst = fn(e)
                if inc is not None:
                    inst.then_inc(inc[0], inc[1])
        return f


def build(dumps=()):
    nc = bass.Bass("TRN2", target_bir_lowering=False, detect_race_conditions=False)

    def scratch(name, shape, dt):
        kind = "ExternalOutput" if name in dumps else "Internal"
        return nc.dram_tensor(name, shape, dt, kind=kind).ap()

    d_x = nc.dram_tensor("x", [S, HID], F32, kind="ExternalInput").ap()
    d_win = nc.dram_tensor("w_in", [HID, PROJ], F32, kind="ExternalInput").ap()
    d_convw = nc.dram_tensor("conv_w", [4, CONVD], F32, kind="ExternalInput").ap()
    d_convb = nc.dram_tensor("conv_b", [CONVD], F32, kind="ExternalInput").ap()
    d_dtb = nc.dram_tensor("dt_bias", [NH], F32, kind="ExternalInput").ap()
    d_alog = nc.dram_tensor("a_log", [NH], F32, kind="ExternalInput").ap()
    d_dd = nc.dram_tensor("d_d", [NH], F32, kind="ExternalInput").ap()
    d_nw = nc.dram_tensor("norm_w", [INTER], F32, kind="ExternalInput").ap()
    d_wout = nc.dram_tensor("w_out", [INTER, HID], F32, kind="ExternalInput").ap()
    d_w1 = nc.dram_tensor("w1", [O_FFN, HID], F32, kind="ExternalInput").ap()
    d_w2 = nc.dram_tensor("w2", [HID, I_FFN], F32, kind="ExternalInput").ap()
    d_drep = nc.dram_tensor("d_rep", [INTER], F32, kind="ExternalInput").ap()

    d_out = nc.dram_tensor("out", [S, HID], F32, kind="ExternalOutput").ap()

    d_xn = scratch("xn_d", [S, HID], F32) if "xn_d" in dumps else None
    d_xqT = scratch("xqT_d", [HID, S], BF16)
    d_gate = scratch("gate_d", [INTER, S], F32)
    d_hbcc = scratch("hbcc_d", [CONVD, S], F32)
    d_dtdump = scratch("dt_d", [NH, S], F32) if "dt_d" in dumps else None
    d_qw1 = scratch("qw1_d", [O_FFN, HID], BF16)
    d_qw2 = scratch("qw2_d", [HID, I_FFN], BF16)
    d_w1t = scratch("w1t_d", [HID, O_FFN], BF16)
    d_w2t = scratch("w2t_d", [I_FFN, HID], BF16)
    d_h = scratch("h_d", [I_FFN, S], F32)
    d_yf = scratch("yf_d", [INTER, S], F32) if "yf_d" in dumps else None
    d_yn = scratch("yn_d", [INTER, S], F32) if "yn_d" in dumps else None
    d_rs = scratch("rs_d", [128, S], F32) if "rs_d" in dumps else None
    d_dbg2 = scratch("dbg2_d", [384, 1024], F32) if "dbg2_d" in dumps else None
    d_g = scratch("g_d", [HID, S], F32)

    with contextlib.ExitStack() as stk:
        E = stk.enter_context

        sm = {k: Sem(E(nc.semaphore(k))) for k in
              ("bar", "dma", "dmx", "dmw", "dmg", "dms", "dss", "dmq", "te", "ve", "se", "ge")}
        bar, dma, dmx, dmw, dms = sm["bar"], sm["dma"], sm["dmx"], sm["dmw"], sm["dms"]
        dmg = sm["dmg"]
        dss, dmq = sm["dss"], sm["dmq"]
        te, ve, se, ge = sm["te"], sm["ve"], sm["se"], sm["ge"]
        p = Prog(nc, {"tensor": te, "vector": ve, "scalar": se, "gpsimd": ge})

        ident = E(nc.sbuf_tensor("ident", [128, 128], F32)).ap()
        identb = E(nc.sbuf_tensor("identb", [128, 128], BF16)).ap()
        convw_c = E(nc.sbuf_tensor("convw_c", [128, 4 * 48], F32)).ap()
        convb_c = E(nc.sbuf_tensor("convb_c", [128, 48], F32)).ap()
        dtb_c = E(nc.sbuf_tensor("dtb_c", [NH, 1], F32)).ap()
        alog_c = E(nc.sbuf_tensor("alog_c", [NH, 1], F32)).ap()
        acol = E(nc.sbuf_tensor("acol", [NH, 1], F32)).ap()
        dd_c = E(nc.sbuf_tensor("dd_c", [NH, 1], F32)).ap()
        nw_c = E(nc.sbuf_tensor("nw_c", [128, 32], F32)).ap()
        sxcols = E(nc.sbuf_tensor("sxcols", [128, 16], F32)).ap()
        sw1c = E(nc.sbuf_tensor("sw1c", [128, 88], F32)).ap()
        sw2c = E(nc.sbuf_tensor("sw2c", [128, 16], F32)).ap()
        shcols = E(nc.sbuf_tensor("shcols", [128, 16], F32)).ap()
        shrcp = E(nc.sbuf_tensor("shrcp", [128, 16], F32)).ap()
        dt_sb = E(nc.sbuf_tensor("dt_sb", [NH, S], F32)).ap()

        def ld(dst, src, slow=False):
            if slow:
                return lambda e: e.dma_start(out=dst, in_=src, allow_slow_non_contiguous=True)
            return lambda e: e.dma_start(out=dst, in_=src)

        p.y(ld(convw_c[:, :], d_convw.rearrange("j (mt p) -> p (j mt)", p=128), True), inc=dma)
        p.y(ld(convb_c[:, :], d_convb.rearrange("(mt p) -> p mt", p=128), True), inc=dma)
        p.y(ld(dtb_c[:, :], d_dtb.rearrange("(h o) -> h o", o=1), True), inc=dma)
        p.y(ld(alog_c[:, :], d_alog.rearrange("(h o) -> h o", o=1), True), inc=dma)
        p.y(ld(dd_c[:, :], d_dd.rearrange("(h o) -> h o", o=1), True), inc=dma)
        p.y(ld(nw_c[:, :], d_nw.rearrange("(mt p) -> p mt", p=128), True), inc=dma)
        v_consts = dma.c

        tmpones = E(nc.sbuf_tensor("tmpones", [128, 128], F32)).ap()
        p.g(lambda e: e.memset(ident[:, :], 0.0))
        p.g(lambda e: e.memset(tmpones[:, :], 1.0))
        g_consts = p.g(lambda e: e.affine_select(ident[:, :], tmpones[:, :], [[1, 128]],
                                                 ALU.is_equal, 0.0, base=0, channel_multiplier=-1))
        identb_done = p.v(lambda e: e.tensor_copy(identb[:, :], ident[:, :]), w=[(ge, g_consts)])

        se_exp = p.s(lambda e: e.activation(acol[:, :], alog_c[:, :], AF.Exp),
                     w=[(dma, v_consts)])
        acol_done = p.v(lambda e: e.tensor_scalar(acol[:, :], acol[:, :], -1.0, None, ALU.mult),
                        w=[(se, se_exp)])

        with contextlib.ExitStack() as ph:
            Eh = ph.enter_context
            xnT = Eh(nc.sbuf_tensor("xnT", [128, KT * S], F32)).ap()
            xin = Eh(nc.sbuf_tensor("xin", [128, HID], F32)).ap()
            xnst = Eh(nc.sbuf_tensor("xnst", [128, HID], F32)).ap()
            xqst = Eh(nc.sbuf_tensor("xqst", [128, HID], BF16)).ap()
            qtmp = Eh(nc.sbuf_tensor("qtmp", [128, HID], I32)).ap()
            f32a = Eh(nc.sbuf_tensor("f32a", [128, HID], F32)).ap()
            f32b = Eh(nc.sbuf_tensor("f32b", [128, HID], F32)).ap()
            f32c = xin
            mcol = Eh(nc.sbuf_tensor("mcol", [128, 1], F32)).ap()
            vcol = Eh(nc.sbuf_tensor("vcol", [128, 1], F32)).ap()
            rcol = Eh(nc.sbuf_tensor("rcol", [128, 1], F32)).ap()
            rcol2 = Eh(nc.sbuf_tensor("rcol2", [128, 1], F32)).ap()
            tb_st = Eh(nc.sbuf_tensor("tb_st", [128, 128], BF16)).ap()
            p0ps = contextlib.ExitStack()
            ps_tr = p0ps.enter_context(nc.psum_tensor("ps_tr", [128, 128], F32)).ap()
            ps_trb = p0ps.enter_context(nc.psum_tensor("ps_trb", [128, 128], BF16)).ap()

            trans_done = 0
            prev_tb_dma = 0
            xq_done = 0
            for i in range(16):
                ldtgt = p.y(ld(xin[:, :], d_x[i * 128:(i + 1) * 128, :]),
                            w=[(ve, xq_done)], inc=dmx)
                p.v(lambda e: e.tensor_reduce(mcol[:, :], xin[:, :], AXX, ALU.add),
                    w=[(dmx, ldtgt)])
                p.v(lambda e: e.tensor_scalar(mcol[:, :], mcol[:, :], 1.0 / HID, None, ALU.mult))
                p.v(lambda e: e.tensor_scalar(f32a[:, :], xin[:, :], mcol[:, :], None, ALU.subtract))
                vq = p.v(lambda e: e.activation(None, None, None) if False else e.engine_nop())
                sq = p.s(lambda e: e.activation(f32b[:, :], f32a[:, :], AF.Square), w=[(ve, vq)])
                p.v(lambda e: e.tensor_reduce(vcol[:, :], f32b[:, :], AXX, ALU.add),
                    w=[(se, sq)])
                p.v(lambda e: e.tensor_scalar(vcol[:, :], vcol[:, :], 1.0 / HID, 1e-5, ALU.mult, ALU.add))
                vr = p.v(lambda e: e.reciprocal(rcol[:, :], vcol[:, :]))
                sr = p.s(lambda e: e.activation(rcol2[:, :], rcol[:, :], AF.Sqrt), w=[(ve, vr)])
                xn_done = p.v(lambda e: e.tensor_scalar(xnst[:, :], f32a[:, :], rcol2[:, :], None,
                                                        ALU.mult),
                              w=[(se, sr), (te, trans_done)])
                if d_xn is not None:
                    p.y(lambda e, i=i: e.dma_start(out=d_xn[i * 128:(i + 1) * 128, :], in_=xnst[:, :]),
                        w=[(ve, xn_done)], inc=dms)

                p.v(lambda e: e.tensor_reduce(rcol[:, :], xnst[:, :], AXX, ALU.max,
                                              apply_absolute_value=True))
                p.v(lambda e: e.tensor_scalar(rcol2[:, :], rcol[:, :], R127, None, ALU.mult))
                p.v(lambda e: e.scalar_tensor_tensor(vcol[:, :], rcol2[:, :], -127.0, rcol[:, :],
                                                     ALU.mult, ALU.add))
                p.v(lambda e, i=i: e.scalar_tensor_tensor(sxcols[:, i:i + 1], vcol[:, :], R127,
                                                          rcol2[:, :], ALU.mult, ALU.add))
                p.v(lambda e, i=i: e.tensor_scalar(sxcols[:, i:i + 1], sxcols[:, i:i + 1], 1e-8,
                                                   None, ALU.max))
                p.v(lambda e, i=i: e.reciprocal(rcol[:, :], sxcols[:, i:i + 1]))
                p.v(lambda e: e.tensor_scalar(f32a[:, :], xnst[:, :], rcol[:, :], 127.0,
                                              ALU.mult, ALU.min))
                p.v(lambda e: e.tensor_scalar(qtmp[:, :], f32a[:, :], -127.0, None, ALU.max))
                p.v(lambda e: e.scalar_tensor_tensor(f32b[:, :], f32a[:, :], -127.0, qtmp[:, :],
                                                     ALU.max, ALU.subtract))
                p.v(lambda e: e.tensor_scalar(f32a[:, :], f32b[:, :], 0.5, None, ALU.is_gt))
                p.v(lambda e: e.tensor_scalar(f32c[:, :], f32b[:, :], -0.5, None, ALU.is_lt))
                p.v(lambda e: e.scalar_tensor_tensor(f32b[:, :], f32a[:, :], 1.0, qtmp[:, :],
                                                     ALU.mult, ALU.add))
                xq_done = p.v(lambda e: e.tensor_tensor(xqst[:, :], f32b[:, :], f32c[:, :],
                                                        ALU.subtract))

                for k in range(KT):
                    w0 = [(ve, xn_done)] if k == 0 else []
                    if i == 0 and k == 0:
                        w0.append((ge, g_consts))
                    t1 = p.t(lambda e, k=k: e.transpose(ps_tr[:, :], xnst[:, k * 128:(k + 1) * 128],
                                                        ident[:, :]),
                             w=w0, inc=True)
                    c1 = p.v(lambda e, k=k, i=i: e.tensor_copy(
                        xnT[:, k * S + i * 128: k * S + (i + 1) * 128], ps_tr[:, :]),
                        w=[(te, t1)])
                    w1l = [(ve, xq_done)] if k == 0 else []
                    if i == 0 and k == 0:
                        w1l.append((ve, identb_done))
                    t2 = p.t(lambda e, k=k: e.transpose(ps_trb[:, :], xqst[:, k * 128:(k + 1) * 128],
                                                        identb[:, :]),
                             w=w1l, inc=True)
                    c2 = p.v(lambda e: e.tensor_copy(tb_st[:, :], ps_trb[:, :]),
                             w=[(te, t2), (dss, prev_tb_dma)])
                    prev_tb_dma = p.dma("scalar", lambda e, k=k, i=i: e.dma_start(
                        out=d_xqT[k * 128:(k + 1) * 128, i * 128:(i + 1) * 128], in_=tb_st[:, :]),
                        w=[(ve, c2)], dsem=dss)
                    trans_done = p.t(lambda e: e.nop(), w=[(ve, c2), (ve, c1)], inc=True)

            p0_done_v = ve.c
            p0ps.close()

            wblk = Eh(nc.sbuf_tensor("wblk", [128, HID], F32)).ap()
            wblk2 = Eh(nc.sbuf_tensor("wblk2", [128, HID], F32)).ap()
            evst = Eh(nc.sbuf_tensor("evst", [128, S + 4], F32)).ap()
            dttmp = evst[0:NH, 0:S]
            evout = f32a
            evout2 = f32b
            ps_mm = Eh(nc.psum_tensor("ps_mm", [128, S], F32)).ap()
            ps_mm2 = Eh(nc.psum_tensor("ps_mm2", [128, S], F32)).ap()

            p.v(lambda e: e.memset(evst[:, 0:1], 0.0))
            evst_zeroed = p.v(lambda e: e.memset(evst[:, S + 1:S + 4], 0.0))

            wqa, wqb, wqo = xnst, xin, xqst
            wqi = qtmp
            qc1, qc2, qc3 = mcol, vcol, rcol
            wqst = {"ld_free": 0, "st_free": 0, "qw1_done": 0, "qw2_done": 0}

            def wq_load(src_ap, width, extra=()):
                lt = p.dma("scalar", lambda e: e.dma_start(out=wqa[:, :width], in_=src_ap),
                           w=[(ve, wqst["ld_free"])] + list(extra), dsem=dmq)
                return lt

            def wq_scale_from_qc1(scol):
                p.v(lambda e: e.tensor_scalar(qc2[:, :], qc1[:, :], R127, None, ALU.mult))
                p.v(lambda e: e.scalar_tensor_tensor(qc3[:, :], qc2[:, :], -127.0, qc1[:, :],
                                                     ALU.mult, ALU.add))
                p.v(lambda e: e.scalar_tensor_tensor(scol, qc3[:, :], R127, qc2[:, :],
                                                     ALU.mult, ALU.add))
                p.v(lambda e: e.tensor_scalar(scol, scol, 1e-8, None, ALU.max))
                p.v(lambda e: e.reciprocal(qc3[:, :], scol))

            def wq_quant_sub(dst, width, key):
                p.v(lambda e: e.tensor_scalar(wqb[:, :width], wqa[:, :width], qc3[:, :], 127.0,
                                              ALU.mult, ALU.min))
                p.v(lambda e: e.tensor_scalar(wqi[:, :width], wqb[:, :width],
                                              -127.0, None, ALU.max))
                p.v(lambda e: e.scalar_tensor_tensor(wqa[:, :width], wqb[:, :width], -127.0,
                                                     wqi[:, :width], ALU.max, ALU.subtract))
                p.v(lambda e: e.tensor_scalar(wqb[:, :width], wqa[:, :width], 0.5, None, ALU.is_gt))
                p.v(lambda e: e.tensor_scalar(wqa[:, :width], wqa[:, :width], -0.5, None, ALU.is_lt))
                p.v(lambda e: e.scalar_tensor_tensor(wqb[:, :width], wqb[:, :width], 1.0,
                                                     wqi[:, :width], ALU.mult, ALU.add))
                cq = p.v(lambda e: e.tensor_tensor(wqo[:, :width], wqb[:, :width], wqa[:, :width],
                                                   ALU.subtract), w=[(dss, wqst["st_free"])])
                wqst["ld_free"] = cq
                wqst["st_free"] = p.dma("scalar", lambda e: e.dma_start(out=dst, in_=wqo[:, :width]),
                                        w=[(ve, cq)], dsem=dss)
                wqst[key] = wqst["st_free"]

            def emit_wq_w1(mp, extra=()):
                lt = wq_load(d_w1[mp * 128:(mp + 1) * 128, :], HID, extra)
                p.v(lambda e: e.tensor_reduce(qc1[:, :], wqa[:, :HID], AXX, ALU.max,
                                              apply_absolute_value=True), w=[(dmq, lt)])
                wq_scale_from_qc1(sw1c[:, mp:mp + 1])
                wq_quant_sub(d_qw1[mp * 128:(mp + 1) * 128, :], HID, "qw1_done")

            W2SUB = [(0, 2048), (2048, 2048), (4096, 1536)]

            def emit_wq_w2(m2):
                for si, (off, wd) in enumerate(W2SUB):
                    lt = wq_load(d_w2[m2 * 128:(m2 + 1) * 128, off:off + wd], wd)
                    p.v(lambda e, wd=wd: e.tensor_reduce(qc2[:, :], wqa[:, :wd], AXX, ALU.max,
                                                         apply_absolute_value=True),
                        w=[(dmq, lt)])
                    if si == 0:
                        p.v(lambda e: e.tensor_copy(qc1[:, :], qc2[:, :]))
                    else:
                        p.v(lambda e: e.tensor_tensor(qc1[:, :], qc1[:, :], qc2[:, :], ALU.max))
                    wqst["ld_free"] = ve.c
                wq_scale_from_qc1(sw2c[:, m2:m2 + 1])
                for (off, wd) in W2SUB:
                    lt = wq_load(d_w2[m2 * 128:(m2 + 1) * 128, off:off + wd], wd)
                    p.v(lambda e: e.nop(), w=[(dmq, lt)])
                    wq_quant_sub(d_qw2[m2 * 128:(m2 + 1) * 128, off:off + wd], wd, "qw2_done")

            wq_work = [("w1", mp) for mp in range(O_FFN // 128)] +                       [("w2", m2) for m2 in range(HID // 128)]
            wq_idx = [0]

            def drain_wq(n):
                while n > 0 and wq_idx[0] < len(wq_work):
                    kind, idx = wq_work[wq_idx[0]]
                    extra = [(ve, p0_done_v), (te, trans_done)] if wq_idx[0] == 0 else ()
                    if kind == "w1":
                        emit_wq_w1(idx, extra)
                    else:
                        emit_wq_w2(idx)
                    wq_idx[0] += 1
                    n -= 1

            NM = 81
            ev_done = [0, 0]
            wb_free = [0, 0]
            evout_free = 0
            evout_free_sem = None
            evout2_free = 0
            for m in range(NM):
                mw = 128 if m < 80 else 64
                par = m % 2
                wb = wblk if par == 0 else wblk2
                ps = ps_mm if par == 0 else ps_mm2
                wtgts = []
                for kq in range(4):
                    wl = [(te, wb_free[par])] if kq == 0 else []
                    if m == 0 and kq == 0:
                        wl.append((ve, p0_done_v))
                    wtgts.append(p.y(lambda e, m=m, kq=kq, mw=mw, wb=wb: e.dma_start(
                        out=wb[:, kq * 4 * mw:(kq + 1) * 4 * mw].rearrange("p (k c) -> p k c", k=4),
                        in_=d_win[kq * 512:(kq + 1) * 512, m * 128:m * 128 + mw]
                            .rearrange("(k p) c -> p k c", p=128)),
                        w=wl, inc=dmw))
                last = None
                for k in range(KT):
                    for nn in range(4):
                        wlist = []
                        if nn == 0 and k % 4 == 0:
                            wlist.append((dmw, wtgts[k // 4]))
                        if k == 0 and nn == 0:
                            wlist.append((ve, p0_done_v))
                            wlist.append((ve, ev_done[par]))
                        last = p.t(lambda e, k=k, nn=nn, mw=mw, wb=wb, ps=ps: e.matmul(
                            ps[0:mw, nn * 512:(nn + 1) * 512],
                            wb[:, k * mw:(k + 1) * mw],
                            xnT[:, k * S + nn * 512: k * S + (nn + 1) * 512],
                            start=(k == 0), stop=(k == KT - 1)),
                            w=wlist, inc=True)
                wb_free[par] = last

                if m < 32:
                    wl = [(te, last)]
                    if evout_free_sem is not None:
                        wl.append((evout_free_sem, evout_free))
                    c = p.v(lambda e, ps=ps: e.tensor_copy(evout[:, :], ps[:, :]), w=wl)
                    ev_done[par] = c
                    evout_free = p.dma("scalar", lambda e, m=m: e.dma_start(
                        out=d_gate[m * 128:(m + 1) * 128, :], in_=evout[:, :]),
                        w=[(ve, c)], dsem=dss)
                    evout_free_sem = dss
                elif m < 80:
                    mt = m - 32
                    c = p.v(lambda e, ps=ps: e.tensor_copy(evst[:, 1:S + 1], ps[:, :]),
                            w=[(te, last), (ve, evst_zeroed)])
                    ev_done[par] = c
                    wl = []
                    if evout_free_sem is not None:
                        wl.append((evout_free_sem, evout_free))
                    p.v(lambda e, mt=mt: e.tensor_scalar(
                        evout[:, :], evst[:, 0:S], convw_c[:, mt:mt + 1], None, ALU.mult),
                        w=wl)
                    for j in range(1, 4):
                        p.v(lambda e, mt=mt, j=j: e.scalar_tensor_tensor(
                            evout[:, :], evst[:, j:j + S], convw_c[:, j * 48 + mt:j * 48 + mt + 1],
                            evout[:, :], ALU.mult, ALU.add))
                    cv = ve.c
                    evout_free_sem = se
                    sv = p.s(lambda e, mt=mt: e.activation(
                        evout2[:, :], evout[:, :], AF.Silu, bias=convb_c[:, mt:mt + 1]),
                        w=[(ve, cv), (dss, evout2_free)])
                    evout_free = sv
                    evout2_free = p.dma("scalar", lambda e, mt=mt: e.dma_start(
                        out=d_hbcc[mt * 128:(mt + 1) * 128, :], in_=evout2[:, :]),
                        w=[(se, sv)], dsem=dss)
                else:
                    wl = [(te, last)]
                    if evout_free_sem is not None:
                        wl.append((evout_free_sem, evout_free))
                    c = p.v(lambda e, ps=ps: e.tensor_scalar(evout[0:NH, :], ps[0:NH, :],
                                                             dtb_c[:, :], None, ALU.add), w=wl)
                    ev_done[par] = c
                    p.s(lambda e: e.activation(dttmp[:, :], evout[0:NH, :], AF.Abs),
                        w=[(ve, c)])
                    p.s(lambda e: e.activation(dt_sb[:, :], dttmp[:, :], AF.Exp, scale=-1.0))
                    sv = p.s(lambda e: e.activation(dttmp[:, :], dt_sb[:, :], AF.Ln, bias=1.0))
                    dv = p.v(lambda e: e.scalar_tensor_tensor(dt_sb[:, :], evout[0:NH, :], 0.0,
                                                              dttmp[:, :], ALU.max, ALU.add),
                             w=[(se, sv)])
                    if d_dtdump is not None:
                        p.y(lambda e: e.dma_start(out=d_dtdump[:, :], in_=dt_sb[:, :]),
                            w=[(ve, dv)], inc=dms)
                if m >= 2:
                    drain_wq(2)


            drain_wq(len(wq_work))
            qw1_done = wqst["qw1_done"]
            qw2_done = wqst["qw2_done"]

        xqT_written = prev_tb_dma
        p.barrier(bar, dma_sems=(dma, dmx, dmw, dmg, dms, dss, dmq))

        sh_done = None
        with contextlib.ExitStack() as ph:
            Eh = ph.enter_context
            trst = Eh(nc.sbuf_tensor("trst", [128, O_FFN], BF16)).ap()
            xqT_sb = Eh(nc.sbuf_tensor("xqT_sb", [128, KT * S], BF16)).ap()
            sx_rep = Eh(nc.sbuf_tensor("sx_rep", [128, S], F32)).ap()
            diagst = Eh(nc.sbuf_tensor("diagst", [128, 128], F32)).ap()
            wst_g = Eh(nc.sbuf_tensor("wst_g", [128, 2048], BF16)).ap()
            wst_u = Eh(nc.sbuf_tensor("wst_u", [128, 2048], BF16)).ap()
            evG = Eh(nc.sbuf_tensor("evG", [128, 1024], F32)).ap()
            evU = Eh(nc.sbuf_tensor("evU", [128, 1024], F32)).ap()
            sgout = Eh(nc.sbuf_tensor("sgout", [128, 1024], F32)).ap()
            hstage = Eh(nc.sbuf_tensor("hstage", [128, 1024], F32)).ap()
            habs = Eh(nc.sbuf_tensor("habs", [128, 1024], F32)).ap()
            hmax_acc = Eh(nc.sbuf_tensor("hmax_acc", [128, S], F32)).ap()
            hmcol = Eh(nc.sbuf_tensor("hmcol", [128, 1], F32)).ap()
            hmc2 = Eh(nc.sbuf_tensor("hmc2", [128, 1], F32)).ap()
            hmc3 = Eh(nc.sbuf_tensor("hmc3", [128, 1], F32)).ap()
            p2ps = contextlib.ExitStack()
            ps_g = [p2ps.enter_context(nc.psum_tensor(f"ps_g{i}", [128, 1024], F32)).ap()
                    for i in range(2)]
            ps_u = [p2ps.enter_context(nc.psum_tensor(f"ps_u{i}", [128, 1024], F32)).ap()
                    for i in range(2)]
            ps_rep = ps_g[0][:, 0:128]

            trdone = 0
            for k in range(KT):
                tt_ = p.y(lambda e, k=k: e.dma_start_transpose(
                    trst[:, :O_FFN], d_qw1[:, k * 128:(k + 1) * 128]),
                    w=[(dss, qw1_done), (dms, trdone)], inc=dma)
                trdone = p.y(lambda e, k=k: e.dma_start(
                    out=d_w1t[k * 128:(k + 1) * 128, :], in_=trst[:, :O_FFN]),
                    w=[(dma, tt_)], inc=dms)
            wT_done = trdone

            xq_lds = 0
            for k in range(KT):
                xq_lds = p.y(lambda e, k=k: e.dma_start(
                    out=xqT_sb[:, k * S:(k + 1) * S], in_=d_xqT[k * 128:(k + 1) * 128, :]),
                    w=([(dss, xqT_written)] if k == 0 else []), inc=dma)

            rep_done = 0
            for t in range(16):
                dg = p.v(lambda e, t=t: e.tensor_scalar(diagst[:, :], ident[:, :],
                                                        sxcols[:, t:t + 1], None, ALU.mult),
                         w=[(te, rep_done)] if t else [])
                tm = p.t(lambda e: e.matmul(ps_rep[:, :], tmpones[:, :], diagst[:, :],
                                            start=True, stop=True),
                         w=[(ve, dg)], inc=True)
                rep_done = tm
                p.v(lambda e, t=t: e.tensor_copy(sx_rep[:, t * 128:(t + 1) * 128], ps_rep[:, :]),
                    w=[(te, tm)])
            sxrep_done = p.v(lambda e: e.memset(hmax_acc[:, :], 0.0))

            ev_free = [[0, 0], [0, 0]]

            wst_free = [0, 0]
            hst_free = 0
            habs_free = 0
            sg_free = 0
            evG_free = 0
            wst_dump_gate = 0
            for mp in range(44):
                wts = []
                for half, mrow in ((0, mp), (1, 44 + mp)):
                    wst = wst_g if half == 0 else wst_u
                    wl0 = [(te, wst_free[half])]
                    if mp == 0:
                        wl0.append((dms, wT_done))
                    if mp == 1 and half == 0 and wst_dump_gate:
                        wl0.append((dms, wst_dump_gate))
                    wts.append(p.y(lambda e, mrow=mrow, wst=wst: e.dma_start(
                        out=wst[:, :].rearrange("p (k c) -> p k c", k=KT),
                        in_=d_w1t[:, mrow * 128:(mrow + 1) * 128]
                            .rearrange("(k p) c -> p k c", p=128)),
                        w=wl0, inc=dmw))
                if d_dbg2 is not None and mp == 0:
                    crw = p.v(lambda e: e.tensor_copy(evU[:, :], wst_g[:, 0:1024]),
                              w=[(dmw, wts[0])])
                    wst_dump_gate = p.y(lambda e: e.dma_start(out=d_dbg2[128:256, :], in_=evU[:, :]),
                                        w=[(ve, crw)], inc=dms)
                mm_done = [[None, None], [None, None]]
                for half in range(2):
                    wst = wst_g if half == 0 else wst_u
                    psl = ps_g if half == 0 else ps_u
                    for sc in range(2):
                        for k in range(KT):
                            wl = []
                            if k == 0:
                                wl.append((dmw, wts[half]))
                                wl.append((ve, ev_free[half][sc]))
                                if mp == 0 and half == 0 and sc == 0:
                                    wl.append((dma, xq_lds))
                                    wl.append((ve, sxrep_done))
                            for half2 in range(2):
                                mm_done[half][sc] = p.t(lambda e, k=k, sc=sc, wst=wst, psl=psl,
                                                        h2=half2: e.matmul(
                                    psl[sc][:, h2 * 512:(h2 + 1) * 512],
                                    wst[:, k * 128:(k + 1) * 128],
                                    xqT_sb[:, k * S + sc * 1024 + h2 * 512:
                                           k * S + sc * 1024 + (h2 + 1) * 512],
                                    start=(k == 0), stop=(k == KT - 1)),
                                    w=(wl if half2 == 0 else []), inc=True)
                    wst_free[half] = mm_done[half][1]

                for sc in range(2):
                    ssl = slice(sc * 1024, (sc + 1) * 1024)
                    if d_dbg2 is not None and mp == 0 and sc == 0:
                        cr = p.v(lambda e: e.tensor_copy(evU[:, :], ps_g[0][:, :]),
                                 w=[(te, mm_done[0][0])])
                        p.y(lambda e: e.dma_start(out=d_dbg2[0:128, :], in_=evU[:, :]),
                            w=[(ve, cr)], inc=dms)
                        cr3 = p.v(lambda e: e.tensor_copy(evU[:, :], xqT_sb[:, 0:1024]))
                        p.y(lambda e: e.dma_start(out=d_dbg2[256:384, :], in_=evU[:, :]),
                            w=[(ve, cr3)], inc=dms)
                        p.v(lambda e: e.nop(), w=[(dms, dms.c)])
                    cG = p.v(lambda e, sc=sc, mp=mp, ssl=ssl: e.scalar_tensor_tensor(
                        evG[:, :], ps_g[sc][:, :], sw1c[:, mp:mp + 1], sx_rep[:, ssl],
                        ALU.mult, ALU.mult),
                        w=[(te, mm_done[0][sc]), (se, evG_free)])
                    ev_free[0][sc] = cG
                    sg = p.s(lambda e: e.activation(sgout[:, :], evG[:, :], AF.Silu),
                             w=[(ve, cG), (ve, sg_free)])
                    evG_free = sg
                    cU = p.v(lambda e, sc=sc, mp=mp, ssl=ssl: e.scalar_tensor_tensor(
                        evU[:, :], ps_u[sc][:, :], sw1c[:, 44 + mp:45 + mp], sx_rep[:, ssl],
                        ALU.mult, ALU.mult),
                        w=[(te, mm_done[1][sc])])
                    ev_free[1][sc] = cU
                    hh = p.v(lambda e: e.tensor_tensor(hstage[:, :], sgout[:, :], evU[:, :],
                                                       ALU.mult),
                             w=[(se, sg), (dss, hst_free)])
                    sg_free = hh
                    ab = p.s(lambda e: e.activation(habs[:, :], hstage[:, :], AF.Abs),
                             w=[(ve, hh), (ve, habs_free)])
                    habs_free = p.v(lambda e, ssl=ssl: e.tensor_tensor(
                        hmax_acc[:, ssl], hmax_acc[:, ssl], habs[:, :], ALU.max),
                        w=[(se, ab)])
                    hst_free = p.dma("scalar", lambda e, mp=mp, sc=sc: e.dma_start(
                        out=d_h[mp * 128:(mp + 1) * 128, sc * 1024:(sc + 1) * 1024],
                        in_=hstage[:, :]),
                        w=[(ve, hh)], dsem=dss)

            for k in range(I_FFN // 128):
                tt_ = p.y(lambda e, k=k: e.dma_start_transpose(
                    trst[:, :HID], d_qw2[:, k * 128:(k + 1) * 128]),
                    w=[(dss, qw2_done), (dms, trdone)], inc=dma)
                trdone = p.y(lambda e, k=k: e.dma_start(
                    out=d_w2t[k * 128:(k + 1) * 128, :], in_=trst[:, :HID]),
                    w=[(dma, tt_)], inc=dms)
            w2T_done = trdone

            trp_done = 0
            for t in range(16):
                tq = p.t(lambda e, t=t: e.transpose(ps_rep[:, :], hmax_acc[:, t * 128:(t + 1) * 128],
                                                    ident[:, :]),
                         w=[(ve, habs_free), (te, rep_done)] if t == 0 else [(ve, trp_done)],
                         inc=True)
                p.v(lambda e: e.tensor_reduce(hmcol[:, :], ps_rep[:, :], AXX, ALU.max),
                    w=[(te, tq)])
                p.v(lambda e: e.tensor_scalar(hmc2[:, :], hmcol[:, :], R127, None, ALU.mult))
                p.v(lambda e: e.scalar_tensor_tensor(hmc3[:, :], hmc2[:, :], -127.0, hmcol[:, :],
                                                     ALU.mult, ALU.add))
                p.v(lambda e, t=t: e.scalar_tensor_tensor(shcols[:, t:t + 1], hmc3[:, :], R127,
                                                          hmc2[:, :], ALU.mult, ALU.add))
                p.v(lambda e, t=t: e.tensor_scalar(shcols[:, t:t + 1], shcols[:, t:t + 1], 1e-8,
                                                   None, ALU.max))
                trp_done = p.v(lambda e, t=t: e.reciprocal(shrcp[:, t:t + 1], shcols[:, t:t + 1]))
            sh_done = trp_done
            p2ps.close()

        p.barrier(bar, dma_sems=(dma, dmx, dmw, dmg, dms, dss, dmq))

        with contextlib.ExitStack() as ph:
            Eh = ph.enter_context
            hq_sb = Eh(nc.sbuf_tensor("hq_sb", [128, 44 * 1024], BF16)).ap()
            sh_rep = Eh(nc.sbuf_tensor("sh_rep", [128, S], F32)).ap()
            rcph_rep = Eh(nc.sbuf_tensor("rcph_rep", [128, S], F32)).ap()
            diag2 = Eh(nc.sbuf_tensor("diag2", [128, 128], F32)).ap()
            htile = Eh(nc.sbuf_tensor("htile", [128, 1024], F32)).ap()
            fqa = Eh(nc.sbuf_tensor("fqa", [128, 1024], F32)).ap()
            fqb = Eh(nc.sbuf_tensor("fqb", [128, 1024], F32)).ap()
            hqi = Eh(nc.sbuf_tensor("hqi", [128, 1024], I32)).ap()
            w2st = Eh(nc.sbuf_tensor("w2st", [128, 44 * 128], BF16)).ap()
            w2st2 = Eh(nc.sbuf_tensor("w2st2", [128, 44 * 128], BF16)).ap()
            evf = Eh(nc.sbuf_tensor("evf", [128, 1024], F32)).ap()
            gout = Eh(nc.sbuf_tensor("gout", [128, 1024], F32)).ap()
            p2bps = contextlib.ExitStack()
            ps_rep2 = p2bps.enter_context(nc.psum_tensor("ps_rep2", [128, 128], F32)).ap()
            ps_f = [p2bps.enter_context(nc.psum_tensor(f"ps_f{i}", [128, 1024], F32)).ap()
                    for i in range(2)]

            rep_done = 0
            for t in range(16):
                dg = p.v(lambda e, t=t: e.tensor_scalar(diag2[:, :], ident[:, :],
                                                        shcols[:, t:t + 1], None, ALU.mult),
                         w=[(te, rep_done)])
                tm = p.t(lambda e: e.matmul(ps_rep2[:, :], tmpones[:, :], diag2[:, :],
                                            start=True, stop=True), w=[(ve, dg)], inc=True)
                p.v(lambda e, t=t: e.tensor_copy(sh_rep[:, t * 128:(t + 1) * 128], ps_rep2[:, :]),
                    w=[(te, tm)])
                dg = p.v(lambda e, t=t: e.tensor_scalar(diag2[:, :], ident[:, :],
                                                        shrcp[:, t:t + 1], None, ALU.mult))
                tm = p.t(lambda e: e.matmul(ps_rep2[:, :], tmpones[:, :], diag2[:, :],
                                            start=True, stop=True), w=[(ve, dg)], inc=True)
                rep_done = tm
                p.v(lambda e, t=t: e.tensor_copy(rcph_rep[:, t * 128:(t + 1) * 128], ps_rep2[:, :]),
                    w=[(te, tm)])

            g_stored = 0
            for sc in range(2):
                ssl = slice(sc * 1024, (sc + 1) * 1024)
                ht_free = 0
                hq_done = 0
                hq_done_k = [0] * 44
                for mp in range(44):
                    lt = p.y(lambda e, mp=mp, sc=sc: e.dma_start(
                        out=htile[:, :], in_=d_h[mp * 128:(mp + 1) * 128, sc * 1024:(sc + 1) * 1024]),
                        w=[(ve, ht_free)], inc=dma)
                    p.v(lambda e, ssl=ssl: e.tensor_tensor(fqa[:, :], htile[:, :], rcph_rep[:, ssl],
                                                           ALU.mult), w=[(dma, lt)])
                    ht_free = ve.c
                    p.v(lambda e: e.tensor_scalar(fqa[:, :], fqa[:, :], 127.0, None, ALU.min))
                    p.v(lambda e: e.tensor_scalar(hqi[:, :], fqa[:, :], -127.0, None, ALU.max))
                    p.v(lambda e: e.scalar_tensor_tensor(fqb[:, :], fqa[:, :], -127.0, hqi[:, :],
                                                         ALU.max, ALU.subtract))
                    p.v(lambda e: e.tensor_scalar(fqa[:, :], fqb[:, :], 0.5, None, ALU.is_gt))
                    p.v(lambda e: e.tensor_scalar(fqb[:, :], fqb[:, :], -0.5, None, ALU.is_lt))
                    p.v(lambda e: e.scalar_tensor_tensor(fqa[:, :], fqa[:, :], 1.0, hqi[:, :],
                                                         ALU.mult, ALU.add))
                    hq_done = p.v(lambda e, mp=mp: e.tensor_tensor(
                        hq_sb[:, mp * 1024:(mp + 1) * 1024], fqa[:, :], fqb[:, :], ALU.subtract))
                    hq_done_k[mp] = hq_done

                evf_free = 0
                gout_free = 0
                w2_free = [0, 0]
                mmf_prev = [0, 0]
                for m2 in range(16):
                    par = m2 % 2
                    w2b = w2st if par == 0 else w2st2
                    psf = ps_f[par]
                    wt = p.dma("scalar", lambda e, m2=m2, w2b=w2b: e.dma_start(
                        out=w2b[:, :].rearrange("p (k c) -> p k c", k=44),
                        in_=d_w2t[:, m2 * 128:(m2 + 1) * 128].rearrange("(k p) c -> p k c", p=128)),
                        w=[(te, w2_free[par])] + ([(dms, w2T_done)] if (sc == 0 and m2 < 2) else []),
                        dsem=dmq)
                    mmf = None
                    for k in range(44):
                        wl = []
                        if m2 < 2:
                            wl.append((ve, hq_done_k[k]))
                        elif k == 0:
                            wl.append((ve, hq_done))
                        if k == 0:
                            wl.append((dmq, wt))
                            wl.append((ve, mmf_prev[par]))
                        for half2 in range(2):
                            mmf = p.t(lambda e, k=k, w2b=w2b, psf=psf, h2=half2: e.matmul(
                                psf[:, h2 * 512:(h2 + 1) * 512],
                                w2b[:, k * 128:(k + 1) * 128],
                                hq_sb[:, k * 1024 + h2 * 512: k * 1024 + (h2 + 1) * 512],
                                start=(k == 0), stop=(k == 43)),
                                w=(wl if half2 == 0 else []), inc=True)
                    w2_free[par] = mmf
                    cf = p.v(lambda e, m2=m2, ssl=ssl, psf=psf: e.scalar_tensor_tensor(
                        evf[:, :], psf[:, :], sw2c[:, m2:m2 + 1], sh_rep[:, ssl],
                        ALU.mult, ALU.mult),
                        w=[(te, mmf), (se, evf_free)])
                    mmf_prev[par] = cf
                    sg2 = p.s(lambda e: e.activation(gout[:, :], evf[:, :], AF.Silu),
                              w=[(ve, cf), (dms, gout_free)])
                    evf_free = sg2
                    gout_free = p.y(lambda e, m2=m2, sc=sc: e.dma_start(
                        out=d_g[m2 * 128:(m2 + 1) * 128, sc * 1024:(sc + 1) * 1024],
                        in_=gout[:, :]),
                        w=[(se, sg2)], inc=dms)
                g_stored = gout_free
            p2bps.close()

        p.barrier(bar, dma_sems=(dma, dmx, dmw, dmg, dms, dss, dmq))

        with contextlib.ExitStack() as ph:
            Eh = ph.enter_context
            s_run = Eh(nc.sbuf_tensor("s_run", [128, NH * HD], F32)).ap()
            repbank = Eh(nc.sbuf_tensor("repbank", [64, NH * 64], F32)).ap()
            mask0 = Eh(nc.sbuf_tensor("mask0", [128, 256], F32)).ap()
            mask1 = Eh(nc.sbuf_tensor("mask1", [128, 256], F32)).ap()
            ones64 = Eh(nc.sbuf_tensor("ones64", [64, 256], F32)).ap()
            ddrepc = Eh(nc.sbuf_tensor("ddrepc", [128, 32], F32)).ap()
            ab_c = Eh(nc.sbuf_tensor("ab_c", [64, 256], F32)).ap()
            acum = Eh(nc.sbuf_tensor("acum", [64, 256], F32)).ap()
            eexp = Eh(nc.sbuf_tensor("eexp", [64, 256], F32)).ap()
            decayc = Eh(nc.sbuf_tensor("decayc", [64, 256], F32)).ap()
            acT = Eh(nc.sbuf_tensor("acT", [128, 128], F32)).ap()
            nacT = Eh(nc.sbuf_tensor("nacT", [128, 128], F32)).ap()
            dcT = Eh(nc.sbuf_tensor("dcT", [128, 128], F32)).ap()
            cdecbc = Eh(nc.sbuf_tensor("cdecbc", [128, 64], F32)).ap()
            diag3 = Eh(nc.sbuf_tensor("diag3", [128, 832], F32)).ap()
            hsdt_ct = Eh(nc.sbuf_tensor("hsdt_ct", [128, 2 * 256], F32)).ap()
            hsdtT = Eh(nc.sbuf_tensor("hsdtT", [128, 2 * INTER], F32)).ap()
            hsr = Eh(nc.sbuf_tensor("hsr", [128, 2 * 256], F32)).ap()
            hse = Eh(nc.sbuf_tensor("hse", [128, 2 * 256], F32)).ap()
            b_ch = Eh(nc.sbuf_tensor("b_ch", [128, 8 * 256], F32)).ap()
            c_ch = Eh(nc.sbuf_tensor("c_ch", [128, 8 * 256], F32)).ap()
            btsb = Eh(nc.sbuf_tensor("btsb", [128, 2 * 128], F32)).ap()
            gsb = Eh(nc.sbuf_tensor("gsb", [128, 512], F32)).ap()
            dmst = Eh(nc.sbuf_tensor("dmst", [128, 2 * 512], F32)).ap()
            ltsb = Eh(nc.sbuf_tensor("ltsb", [128, 2 * 512], F32)).ap()
            mtsb = Eh(nc.sbuf_tensor("mtsb", [128, 2 * 512], F32)).ap()
            bdst = Eh(nc.sbuf_tensor("bdst", [128, 2 * 256], F32)).ap()
            yfall = Eh(nc.sbuf_tensor("yfall", [128, 32 * 256], F32)).ap()
            gate_ct = Eh(nc.sbuf_tensor("gate_ct", [128, 2 * 256], F32)).ap()
            ebsb = Eh(nc.sbuf_tensor("ebsb", [128, 256], F32)).ap()
            yofe = Eh(nc.sbuf_tensor("yofe", [128, 256], F32)).ap()
            ysb = Eh(nc.sbuf_tensor("ysb", [128, 256], F32)).ap()
            sgate = Eh(nc.sbuf_tensor("sgate", [128, 256], F32)).ap()
            rstdb = Eh(nc.sbuf_tensor("rstdb", [128, 256], F32)).ap()
            wopst = Eh(nc.sbuf_tensor("wopst", [128, 32 * 128], F32)).ap()
            wopst2 = Eh(nc.sbuf_tensor("wopst2", [128, 32 * 128], F32)).ap()
            g_m = Eh(nc.sbuf_tensor("g_m", [128, 256], F32)).ap()
            x_m = Eh(nc.sbuf_tensor("x_m", [128, 2 * 128], F32)).ap()
            zsb = Eh(nc.sbuf_tensor("zsb", [128, 256], F32)).ap()
            osb = Eh(nc.sbuf_tensor("osb", [128, 2 * 128], F32)).ap()

            p3ps = contextlib.ExitStack()
            PS_dt = p3ps.enter_context(nc.psum_tensor("PS_dt", [128, 512], F32)).ap()
            PS_trA = p3ps.enter_context(nc.psum_tensor("PS_trA", [128, 512], F32)).ap()
            PS_trB = p3ps.enter_context(nc.psum_tensor("PS_trB", [128, 512], F32)).ap()
            PS_G = p3ps.enter_context(nc.psum_tensor("PS_G", [128, 512], F32)).ap()
            PS_sel = p3ps.enter_context(nc.psum_tensor("PS_sel", [128, 512], F32)).ap()
            PS_acc = p3ps.enter_context(nc.psum_tensor("PS_acc", [128, 512], F32)).ap()
            PS_yd = p3ps.enter_context(nc.psum_tensor("PS_yd", [128, 512], F32)).ap()
            PS_yo = p3ps.enter_context(nc.psum_tensor("PS_yo", [128, 512], F32)).ap()

            p.v(lambda e: e.memset(s_run[:, :], 0.0))
            p.v(lambda e: e.memset(ones64[:, :], 1.0))
            p.g(lambda e: e.memset(repbank[:, :], 1.0))
            p.g(lambda e: e.affine_select(repbank[:, :], repbank[:, :], [[1, 64], [0, 64]],
                                          ALU.is_equal, 0.0, base=0, channel_multiplier=-1))
            p.g(lambda e: e.memset(mask0[:, :], 0.0))
            p.g(lambda e: e.affine_select(mask0[:, :], mask0[:, :], [[1, 256]],
                                          ALU.is_ge, NEG, base=0, channel_multiplier=-1))
            p.g(lambda e: e.memset(mask1[:, :], 0.0))
            gcst = p.g(lambda e: e.affine_select(mask1[:, :], mask1[:, :], [[1, 256]],
                                                 ALU.is_ge, NEG, base=-128, channel_multiplier=-1))
            ddrep_ld = p.y(ld(ddrepc[:, :], d_drep.rearrange("(mt p) -> p mt", p=128), True),
                           inc=dma)
            consts3 = p.v(lambda e: e.nop(), w=[(ge, gcst), (dma, ddrep_ld)])

            diag_cons = 0
            dg6_cons = [0] * 6
            dgv_done = [0] * 6
            lt2_last = [0, 0]
            yd2_last = [0, 0]
            sel2_free = [0, 0]
            yd_evac2 = [0, 0]
            st2_cons = [0, 0]
            mt2_done = [0, 0]
            bd2_done = [0, 0]
            hse_free = [0, 0]
            sqrd_done0 = 0
            sqrd_done1 = 0
            bt_done = 0
            lt_last = 0
            yd_last = 0
            hs_free = 0
            gate_free = [0, 0]
            gm_free = [0, 0]
            xm_free = [0, 0]
            bc_free = 0
            acum_free = 0
            hsdtT_cons = 0
            yo_evac = 0
            st_upd = 0
            sel_free = 0
            acc_free = 0
            tr_free = [0, 0]
            out_dma = 0
            for c in range(NCH):
                csl = slice(c * CH, (c + 1) * CH)
                wl = [(te, acum_free)]
                if c == 0:
                    wl.append((ve, consts3))
                av = p.v(lambda e, csl=csl: e.tensor_scalar(ab_c[:, :], dt_sb[:, csl], acol[:, :],
                                                            None, ALU.mult), w=wl)
                scan_done = p.v(lambda e: e.tensor_tensor_scan(acum[:, :], ones64[:, :], ab_c[:, :],
                                                               0.0, ALU.mult, ALU.add))
                sc1 = p.s(lambda e: e.activation(eexp[:, :], acum[:, :], AF.Exp), w=[(ve, scan_done)])
                dv1 = p.v(lambda e: e.tensor_scalar(decayc[:, :], acum[:, :],
                                                    acum[:, 255:256], None, ALU.subtract))
                sc2 = p.s(lambda e: e.activation(decayc[:, :], decayc[:, :], AF.Exp, scale=-1.0),
                          w=[(ve, dv1)])
                for t in range(2):
                    tq = p.t(lambda e, t=t: e.transpose(PS_trA[:, 0:64],
                                                        acum[0:64, t * 128:(t + 1) * 128],
                                                        ident[0:64, 0:64]),
                             w=[(ve, tr_free[0]), (ve, scan_done)], inc=True)
                    tr_free[0] = p.v(lambda e, t=t: e.tensor_copy(acT[:, t * 64:(t + 1) * 64],
                                                                  PS_trA[:, 0:64]),
                                     w=[(te, tq)])
                    tq = p.t(lambda e, t=t: e.transpose(PS_trB[:, 0:64],
                                                        decayc[0:64, t * 128:(t + 1) * 128],
                                                        ident[0:64, 0:64]),
                             w=[(ve, tr_free[1]), (se, sc2)], inc=True)
                    tr_free[1] = p.v(lambda e, t=t: e.tensor_copy(dcT[:, t * 64:(t + 1) * 64],
                                                                  PS_trB[:, 0:64]),
                                     w=[(te, tq)])
                acT_ok = p.v(lambda e: e.tensor_scalar(nacT[:, :], acT[:, :], -1.0, None, ALU.mult))
                dgv = p.v(lambda e: e.tensor_scalar(diag3[0:64, 768:832], ident[0:64, 0:64],
                                                    eexp[:, 255:256], None, ALU.mult),
                          w=[(se, sc1), (te, diag_cons)])
                tcd = p.t(lambda e: e.matmul(PS_acc[:, 256:320], tmpones[0:64, :],
                                             diag3[0:64, 768:832], start=True, stop=True),
                          w=[(ve, dgv), (ve, acc_free)], inc=True)
                diag_cons = tcd
                cdec_done = p.v(lambda e: e.tensor_copy(cdecbc[:, :], PS_acc[:, 256:320]),
                                w=[(te, tcd)])

                def emit_dg(hx):
                    for t in range(2):
                        sl = (hx % 3) * 2 + t
                        dgv_done[sl] = p.v(lambda e, hx=hx, t=t, sl=sl: e.tensor_scalar(
                            diag3[:, sl * 128:(sl + 1) * 128], ident[:, :],
                            acT[:, t * 64 + hx: t * 64 + hx + 1], None, ALU.mult),
                            w=[(te, dg6_cons[sl])])
                emit_dg(0)
                emit_dg(1)
                ldb = p.y(lambda e, c=c: e.dma_start(
                    out=b_ch[:, :].rearrange("p (g l) -> p g l", g=8),
                    in_=d_hbcc[INTER:INTER + 1024, c * CH:(c + 1) * CH]
                        .rearrange("(g p) l -> p g l", p=128)),
                    w=[(te, bc_free)], inc=dma)
                ldc = p.y(lambda e, c=c: e.dma_start(
                    out=c_ch[:, :].rearrange("p (g l) -> p g l", g=8),
                    in_=d_hbcc[INTER + 1024:INTER + 2048, c * CH:(c + 1) * CH]
                        .rearrange("(g p) l -> p g l", p=128)),
                    inc=dma)
                hdv_hist = [0, 0]
                trc_hist = [0, 0]
                for ct in range(32):
                    par = ct % 2
                    psdt = PS_dt if par == 0 else PS_yo
                    pstr3 = PS_trA if par == 0 else PS_trB
                    lh = p.y(lambda e, ct=ct, c=c, par=par: e.dma_start(
                        out=hsr[:, par * 256:(par + 1) * 256],
                        in_=d_hbcc[ct * 128:(ct + 1) * 128, c * CH:(c + 1) * CH]),
                        w=[(ve, hdv_hist[par])], inc=dma)
                    tdt = p.t(lambda e, ct=ct, c=c, psdt=psdt: e.matmul(
                        psdt[:, 0:256], repbank[:, ct * 128:(ct + 1) * 128],
                        dt_sb[0:64, c * CH:(c + 1) * CH],
                        start=True, stop=True),
                        w=[(ve, hdv_hist[par]), (ve, av)], inc=True)
                    hdv = p.v(lambda e, ct=ct, par=par, psdt=psdt: e.tensor_tensor(
                        hsdt_ct[:, par * 256:(par + 1) * 256],
                        hsr[:, par * 256:(par + 1) * 256], psdt[:, 0:256],
                        ALU.mult),
                        w=[(te, tdt), (dma, lh)])
                    hdv_hist[par] = hdv
                    tqs = []
                    for t in range(2):
                        wl0 = [(ve, trc_hist[par]), (ve, hdv), (te, hsdtT_cons)]
                        if ct < 2:
                            wl0.append((ve, sel2_free[1] if par == 0 else yd_evac2[1]))
                        tqs.append(p.t(lambda e, t=t, par=par, pstr3=pstr3: e.transpose(
                            pstr3[:, 128 + t * 128:256 + t * 128],
                            hsdt_ct[:, par * 256 + t * 128: par * 256 + (t + 1) * 128],
                            ident[:, :]),
                            w=(wl0 if t == 0 else []), inc=True))
                    cps = None
                    for t in range(2):
                        cps = p.v(lambda e, ct=ct, t=t, pstr3=pstr3: e.tensor_copy(
                            hsdtT[:, t * INTER + ct * 128: t * INTER + (ct + 1) * 128],
                            pstr3[:, 128 + t * 128:256 + t * 128]),
                            w=[(te, tqs[1])])
                    trc_hist[par] = cps
                hsdtT_done = p.v(lambda e: e.nop())

                yf_lasts = [0] * 32
                last_states_mm = 0
                last_yoff_mm = 0
                last_ebig_mm = 0
                gv_done = 0
                pair_idx = 0
                for g in range(8):
                    wl = [(ve, st_upd), (dma, ldc)]
                    tg_ = None
                    for t in range(2):
                        tg_ = p.t(lambda e, g=g, t=t: e.matmul(
                            PS_G[:, t * 256:(t + 1) * 256],
                            b_ch[:, g * 256 + t * 128: g * 256 + (t + 1) * 128],
                            c_ch[:, g * 256:(g + 1) * 256], start=True, stop=True),
                            w=(wl if t == 0 else []), inc=True)
                    gv_done = p.v(lambda e: e.tensor_copy(gsb[:, :], PS_G[:, :]),
                                  w=[(te, tg_)])
                    tq2 = None
                    for t in range(2):
                        tq2 = p.t(lambda e, g=g, t=t: e.transpose(
                            PS_G[:, t * 128:(t + 1) * 128],
                            b_ch[:, g * 256 + t * 128: g * 256 + (t + 1) * 128],
                            ident[:, :]),
                            w=[(ve, gv_done)] if t == 0 else [], inc=True)
                    bt_done = None
                    for t in range(2):
                        bt_done = p.v(lambda e, t=t: e.tensor_copy(
                            btsb[:, t * 128:(t + 1) * 128], PS_G[:, t * 128:(t + 1) * 128]),
                            w=[(te, tq2)])

                    def emit_lchain(h):
                        hp = h % 2
                        ps_ar = PS_sel if hp == 0 else PS_trA
                        ta = None
                        if h + 2 < 64:
                            emit_dg(h + 2)
                        for t in range(2):
                            sl = (h % 3) * 2 + t
                            ta = p.t(lambda e, t=t, hp=hp, sl=sl, ps_ar=ps_ar: e.matmul(
                                ps_ar[:, t * 128:(t + 1) * 128], tmpones[:, :],
                                diag3[:, sl * 128:(sl + 1) * 128],
                                start=True, stop=True),
                                w=[(ve, dgv_done[sl])] + ([(ve, sel2_free[hp])] if t == 0 else []),
                                inc=True)
                            dg6_cons[sl] = ta
                        p.v(lambda e, h=h, hp=hp: e.tensor_scalar(
                            bdst[:, hp * 256: hp * 256 + 128], btsb[:, 0:128],
                            dcT[:, 0 * 64 + h: 0 * 64 + h + 1], None, ALU.mult),
                            w=[(te, st2_cons[hp])] + ([(ve, bt_done)] if h % 8 == 0 else []))
                        bd2_done[hp] = p.v(lambda e, h=h, hp=hp: e.tensor_scalar(
                            bdst[:, hp * 256 + 128: hp * 256 + 256], btsb[:, 128:256],
                            dcT[:, 1 * 64 + h: 1 * 64 + h + 1], None, ALU.mult))
                        mt_done = 0
                        for t in range(2):
                            msk = mask0 if t == 0 else mask1
                            dm = p.v(lambda e, h=h, t=t, msk=msk, hp=hp, ps_ar=ps_ar:
                                     e.scalar_tensor_tensor(
                                dmst[:, hp * 512 + t * 256: hp * 512 + (t + 1) * 256],
                                ps_ar[:, 0:256],
                                nacT[:, t * 64 + h: t * 64 + h + 1],
                                msk[:, :], ALU.add, ALU.add),
                                w=[(te, ta), (se, lt2_last[hp])])
                            sel2_free[hp] = dm
                            lt_ = p.s(lambda e, t=t, hp=hp: e.activation(
                                ltsb[:, hp * 512 + t * 256: hp * 512 + (t + 1) * 256],
                                dmst[:, hp * 512 + t * 256: hp * 512 + (t + 1) * 256],
                                AF.Exp), w=[(ve, dm)])
                            lt2_last[hp] = lt_
                            mt_done = p.v(lambda e, t=t, hp=hp, g=h // 8: e.tensor_tensor(
                                mtsb[:, hp * 512 + t * 256: hp * 512 + (t + 1) * 256],
                                ltsb[:, hp * 512 + t * 256: hp * 512 + (t + 1) * 256],
                                gsb[:, t * 256:(t + 1) * 256], ALU.mult),
                                w=[(se, lt_), (ve, gv_done), (te, yd2_last[hp])])
                        mt2_done[hp] = mt_done

                    def emit_consume(h):
                        nonlocal last_states_mm, last_yoff_mm, last_ebig_mm, yo_evac
                        nonlocal pair_idx, cdec_done
                        g = h // 8
                        hh = h % 8
                        hp = h % 2
                        pp = (h // 2) % 2
                        ps_ydx = PS_yd if pp == 0 else PS_trB
                        for t in range(2):
                            yd2_last[hp] = p.t(lambda e, h=h, t=t, hh=hh, hp=hp, ps_ydx=ps_ydx:
                                               e.matmul(
                                ps_ydx[(hh % 2) * 64:(hh % 2) * 64 + 64, 0:256],
                                hsdtT[:, t * INTER + h * 64:t * INTER + (h + 1) * 64],
                                mtsb[:, hp * 512 + t * 256: hp * 512 + (t + 1) * 256],
                                start=(t == 0), stop=(t == 1)),
                                w=([(ve, mt2_done[hp])] if t == 0 else []) +
                                  ([(ve, yd_evac2[pp])] if (hh % 2 == 0 and t == 0) else []),
                                inc=True)
                        for t in range(2):
                            last_states_mm = p.t(lambda e, h=h, t=t, hh=hh, hp=hp: e.matmul(
                                PS_G[:, hh * 64:(hh + 1) * 64],
                                bdst[:, hp * 256 + t * 128: hp * 256 + (t + 1) * 128],
                                hsdtT[:, t * INTER + h * 64:t * INTER + (h + 1) * 64],
                                start=(hh == 0 and t == 0), stop=(hh == 7 and t == 1)),
                                w=([(ve, bd2_done[hp])] if t == 0 else []), inc=True)
                        st2_cons[hp] = last_states_mm

                        if hh % 2 == 1:
                            ct = g * 4 + hh // 2
                            last_yoff_mm = p.t(lambda e, ct=ct, g=g: e.matmul(
                                PS_yo[:, 0:256], s_run[:, ct * 128:(ct + 1) * 128],
                                c_ch[:, g * 256:(g + 1) * 256], start=True, stop=True),
                                w=[(ve, yo_evac)] + ([(ve, hsdtT_done)] if ct == 0 else []),
                                inc=True)
                            last_ebig_mm = p.t(lambda e, ct=ct, c=c: e.matmul(
                                PS_dt[:, 256:512], repbank[:, ct * 128:(ct + 1) * 128],
                                eexp[:, :], start=True, stop=True),
                                w=[(ve, yo_evac)] + ([(ve, hsdtT_done)] if ct == 0 else []),
                                inc=True)
                            ev1 = p.v(lambda e: e.tensor_copy(ebsb[:, :], PS_dt[:, 256:512]),
                                      w=[(te, last_ebig_mm)])
                            yo_evac = p.v(lambda e: e.tensor_tensor(yofe[:, :], PS_yo[:, 0:256],
                                                                    ebsb[:, :], ALU.mult),
                                          w=[(te, last_yoff_mm), (te, sqrd_done1)])
                            gring = ct % 2
                            lg = p.y(lambda e, ct=ct, c=c, gring=gring: e.dma_start(
                                out=gate_ct[:, gring * 256:(gring + 1) * 256],
                                in_=d_gate[ct * 128:(ct + 1) * 128, c * CH:(c + 1) * CH]),
                                w=[(se, gate_free[gring])], inc=dma)
                            gate_free[gring] = p.s(lambda e, gring=gring: e.activation(
                                sgate[:, :], gate_ct[:, gring * 256:(gring + 1) * 256], AF.Silu),
                                w=[(dma, lg), (ve, ve.c)])
                            lhs2 = p.y(lambda e, ct=ct, c=c, gring=gring: e.dma_start(
                                out=hse[:, gring * 256:(gring + 1) * 256],
                                in_=d_hbcc[ct * 128:(ct + 1) * 128, c * CH:(c + 1) * CH]),
                                w=[(ve, hse_free[gring])], inc=dma)
                            t1 = p.v(lambda e, ct=ct, ps_ydx=ps_ydx, gring=gring:
                                     e.scalar_tensor_tensor(
                                ysb[:, :], hse[:, gring * 256:(gring + 1) * 256],
                                ddrepc[:, ct:ct + 1], ps_ydx[:, 0:256], ALU.mult, ALU.add),
                                w=[(te, te.c), (te, sqrd_done0), (dma, lhs2)])
                            yd_evac2[pp] = t1
                            hse_free[gring] = t1
                            p.v(lambda e: e.tensor_tensor(ysb[:, :], ysb[:, :], yofe[:, :],
                                                          ALU.add))
                            yf = p.v(lambda e, ct=ct: e.tensor_tensor(
                                yfall[:, ct * 256:(ct + 1) * 256], ysb[:, :], sgate[:, :],
                                ALU.mult),
                                w=[(se, gate_free[gring])])
                            yf_lasts[ct] = yf
                            pair_idx += 1

                    for hh in range(8):
                        emit_lchain(g * 8 + hh)
                        if hh > 0:
                            emit_consume(g * 8 + hh - 1)
                    emit_consume(g * 8 + 7)
                    for hh in range(8):
                        h = g * 8 + hh
                        st_upd = p.v(lambda e, h=h, hh=hh: e.scalar_tensor_tensor(
                            s_run[:, h * 64:(h + 1) * 64], s_run[:, h * 64:(h + 1) * 64],
                            cdecbc[:, (h % 64):(h % 64) + 1], PS_G[:, hh * 64:(hh + 1) * 64],
                            ALU.mult, ALU.add),
                            w=[(te, last_states_mm), (ve, cdec_done)] if hh == 0 else
                              [(te, last_states_mm)])
                bc_free = last_yoff_mm
                acum_free = last_ebig_mm
                hsdtT_cons = last_states_mm

                sq_reads = [ysb, yofe]
                sqrd_done = [sqrd_done0, sqrd_done1]
                for ct in range(32):
                    buf = sq_reads[ct % 2]
                    wl5 = [(ve, yf_lasts[ct]), (te, sqrd_done[ct % 2])]
                    if ct < 2:
                        wl5.append((ve, yf_lasts[31]))
                    sq = p.s(lambda e, ct=ct, buf=buf: e.activation(
                        buf[:, :], yfall[:, ct * 256:(ct + 1) * 256], AF.Square),
                        w=wl5)
                    sqrd_done[ct % 2] = sq_mm = p.t(lambda e, ct=ct, buf=buf: e.matmul(
                        PS_acc[:, 0:256], tmpones[:, :], buf[:, :],
                        start=(ct == 0), stop=(ct == 31)),
                        w=[(se, sq)] + ([(ve, acc_free)] if ct == 0 else []),
                        inc=True)

                sqrd_done0, sqrd_done1 = sqrd_done

                yf_dump = 0
                if d_yf is not None:
                    for ct in range(32):
                        yf_dump = p.y(lambda e, ct=ct, c=c: e.dma_start(
                            out=d_yf[ct * 128:(ct + 1) * 128, c * CH:(c + 1) * CH],
                            in_=yfall[:, ct * 256:(ct + 1) * 256]),
                            w=[(ve, ve.c)], inc=dms)
                sumsq_mm = te.c
                rv = p.v(lambda e: e.tensor_scalar(rstdb[:, :], PS_acc[:, 0:256],
                                                   1.0 / INTER, 1e-6, ALU.mult, ALU.add),
                         w=[(te, sumsq_mm)])
                acc_free = rv
                p.v(lambda e: e.reciprocal(rstdb[:, :], rstdb[:, :]))
                sqr = p.s(lambda e: e.activation(rstdb[:, :], rstdb[:, :], AF.Sqrt),
                          w=[(ve, ve.c)])
                yn_done = 0
                for ct in range(32):
                    wl4 = [(se, sqr)] if ct == 0 else []
                    if d_yf is not None and ct == 0:
                        wl4.append((dms, yf_dump))
                    yn_done = p.v(lambda e, ct=ct: e.scalar_tensor_tensor(
                        yfall[:, ct * 256:(ct + 1) * 256], yfall[:, ct * 256:(ct + 1) * 256],
                        nw_c[:, ct:ct + 1], rstdb[:, :], ALU.mult, ALU.mult),
                        w=wl4)

                yn_dump = 0
                if d_yn is not None:
                    for ct in range(32):
                        yn_dump = p.y(lambda e, ct=ct, c=c: e.dma_start(
                            out=d_yn[ct * 128:(ct + 1) * 128, c * CH:(c + 1) * CH],
                            in_=yfall[:, ct * 256:(ct + 1) * 256]),
                            w=[(ve, yn_done)], inc=dms)
                    yn_done = p.v(lambda e: e.nop(), w=[(dms, yn_dump)])
                if d_rs is not None:
                    p.y(lambda e, c=c: e.dma_start(
                        out=d_rs[:, c * CH:(c + 1) * CH], in_=rstdb[:, :]),
                        w=[(ve, yn_done)], inc=dms)

                op_banks = [PS_sel, PS_yo]
                op_evac2 = [sel2_free[0], yo_evac]
                wop_bufs = [wopst, wopst2]
                wop_cons = [0, 0]
                for m in range(16):
                    psop = op_banks[m % 2]
                    wop = wop_bufs[m % 2]
                    wts2 = []
                    for half3 in range(2):
                        wts2.append(p.dma("scalar", lambda e, m=m, half3=half3, wop=wop: e.dma_start(
                            out=wop[:, half3 * 16 * 128:(half3 + 1) * 16 * 128]
                                .rearrange("p (k c) -> p k c", k=16),
                            in_=d_wout[half3 * 2048:(half3 + 1) * 2048, m * 128:(m + 1) * 128]
                                .rearrange("(k p) c -> p k c", p=128)),
                            w=[(te, wop_cons[m % 2])] if half3 == 0 else [], dsem=dmg))
                    lgm = p.y(lambda e, m=m, c=c: e.dma_start(
                        out=g_m[:, :],
                        in_=d_g[m * 128:(m + 1) * 128, c * CH:(c + 1) * CH]),
                        w=[(ve, gm_free[0])], inc=dma)
                    lxm = p.y(lambda e, m=m, c=c: e.dma_start(
                        out=x_m[:, :].rearrange("p (lb cc) -> p lb cc", lb=2),
                        in_=d_x[c * CH:(c + 1) * CH, m * 128:(m + 1) * 128]
                            .rearrange("(lb p) cc -> p lb cc", p=128)),
                        w=[(ve, xm_free[0])], inc=dma)
                    mmop = None
                    for kt in range(32):
                        wl3 = []
                        if kt == 0:
                            wl3 = [(dmg, wts2[0]), (ve, op_evac2[m % 2]), (ve, yn_done)]
                        elif kt == 16:
                            wl3 = [(dmg, wts2[1])]
                        mmop = p.t(lambda e, kt=kt, m=m, psop=psop, wop=wop: e.matmul(
                            psop[:, 0:256], wop[:, kt * 128:(kt + 1) * 128],
                            yfall[:, kt * 256:(kt + 1) * 256],
                            start=(kt == 0), stop=(kt == 31)),
                            w=wl3, inc=True)
                    wop_cons[m % 2] = mmop
                    zv = p.v(lambda e, m=m, psop=psop: e.tensor_tensor(
                        zsb[:, :], psop[:, 0:256], g_m[:, :], ALU.mult),
                        w=[(te, mmop), (dma, lgm)])
                    op_evac2[m % 2] = zv
                    gm_free[0] = zv
                    ob_done = 0
                    for lb in range(2):
                        par = lb
                        pstr3 = PS_trA if par == 0 else PS_trB
                        tz = p.t(lambda e, lb=lb, pstr3=pstr3: e.transpose(
                            pstr3[:, 256:384], zsb[:, lb * 128:(lb + 1) * 128], ident[:, :]),
                            w=[(ve, tr_free[par]), (ve, zv)], inc=True)
                        tr_free[par] = p.v(lambda e, lb=lb, pstr3=pstr3: e.tensor_tensor(
                            osb[:, lb * 128:(lb + 1) * 128], pstr3[:, 256:384],
                            x_m[:, lb * 128:(lb + 1) * 128], ALU.add),
                            w=[(te, tz), (dma, lxm), (dms, out_dma)])
                        ob_done = tr_free[par]
                    xm_free[0] = ob_done
                    out_dma = p.y(lambda e, m=m, c=c: e.dma_start(
                        out=d_out[c * CH:(c + 1) * CH, m * 128:(m + 1) * 128]
                            .rearrange("(lb p) cc -> p lb cc", p=128),
                        in_=osb[:, :].rearrange("p (lb cc) -> p lb cc", lb=2)),
                        w=[(ve, ob_done)], inc=dms)
                sel2_free[0] = op_evac2[0]
                yo_evac = op_evac2[1]

        p.y(lambda e: e.nop(), w=[(dms, dms.c), (dma, dma.c), (dmw, dmw.c), (dmx, dmx.c), (dmg, dmg.c), (dss, dss.c), (dmq, dmq.c)], inc=None)

        with nc.Block() as block:
            block.tensor(p.replay("tensor"))
            block.vector(p.replay("vector"))
            block.scalar(p.replay("scalar"))
            block.gpsimd(p.replay("gpsimd"))
            block.sync(p.replay("sync"))

    return nc


def _shard_inputs(inputs):
    x = np.ascontiguousarray(inputs["L0_attn_input_states"], dtype=np.float32)
    common = {
        "w_in": np.ascontiguousarray(inputs["L0_attn_in_proj_w"], dtype=np.float32),
        "conv_w": np.ascontiguousarray(np.asarray(inputs["L0_attn_conv_w"]).reshape(4, CONVD), dtype=np.float32),
        "conv_b": np.ascontiguousarray(inputs["L0_attn_conv_b"], dtype=np.float32),
        "dt_bias": np.ascontiguousarray(inputs["L0_attn_dt_bias"], dtype=np.float32),
        "a_log": np.ascontiguousarray(inputs["L0_attn_A_log"], dtype=np.float32),
        "d_d": np.ascontiguousarray(inputs["L0_attn_D"], dtype=np.float32),
        "norm_w": np.ascontiguousarray(inputs["L0_attn_norm_w"], dtype=np.float32),
        "w_out": np.ascontiguousarray(inputs["L0_attn_out_proj_w"], dtype=np.float32),
        "w1": np.ascontiguousarray(inputs["L0_ffn_w_gate_up"], dtype=np.float32),
        "w2": np.ascontiguousarray(inputs["L0_ffn_w_down"], dtype=np.float32),
        "d_rep": np.ascontiguousarray(np.repeat(np.asarray(inputs["L0_attn_D"], dtype=np.float32), HD)),
    }
    in_maps = []
    for c in range(8):
        m = dict(common)
        m["x"] = np.ascontiguousarray(x[c])
        in_maps.append(m)
    return in_maps


def kernel(**inputs):
    in_maps = _shard_inputs(inputs)
    nc = build()
    res = run_bass_kernel_spmd(nc, in_maps, core_ids=list(range(8)))
    return np.stack([r["out"] for r in res.results]).astype(np.float32)


# version 08
# speedup vs baseline: 1.0173x; 1.0173x over previous
import contextlib
import numpy as np
import concourse.bass as bass
import concourse.mybir as mybir
from concourse.bass_utils import run_bass_kernel_spmd

F32 = mybir.dt.float32
BF16 = mybir.dt.bfloat16
I32 = mybir.dt.int32
AF = mybir.ActivationFunctionType
ALU = mybir.AluOpType
AXX = mybir.AxisListType.X

S = 2048
HID = 2048
PROJ = 10304
INTER = 4096
CONVD = 6144
NH, HD, NG, SS, CH = 64, 64, 8, 128, 256
NCH = S // CH
I_FFN = 5632
O_FFN = 11264
KT = HID // 128
NEG = -1.0e30
R127 = float(np.float32(1.0 / 127.0))

ENGS = ("tensor", "vector", "scalar", "gpsimd", "sync")


class Sem:
    __slots__ = ("h", "c")

    def __init__(self, h):
        self.h = h
        self.c = 0


class Prog:

    def __init__(self, nc, auto_sems):
        self.nc = nc
        self.ops = {e: [] for e in ENGS}
        self.auto = auto_sems

    def _emit(self, eng, fn, w, extra_inc):
        self.ops[eng].append((tuple((s.h, v) for s, v in w), fn, extra_inc))

    def t(self, fn, w=(), inc=False):
        sem = self.auto["tensor"]
        if inc:
            sem.c += 1
            self._emit("tensor", fn, w, (sem.h, 1))
            return sem.c
        self._emit("tensor", fn, w, None)
        return None

    def _auto(self, eng, fn, w):
        sem = self.auto[eng]
        waits = ((sem.h, sem.c),) + tuple((s.h, v) for s, v in w)
        sem.c += 1
        self.ops[eng].append((waits, fn, (sem.h, 1)))
        return sem.c

    def v(self, fn, w=()):
        return self._auto("vector", fn, w)

    def s(self, fn, w=()):
        return self._auto("scalar", fn, w)

    def g(self, fn, w=()):
        return self._auto("gpsimd", fn, w)

    def dma(self, eng, fn, w=(), dsem=None, n=16):
        dsem.c += n
        self._emit(eng, fn, w, (dsem.h, n))
        return dsem.c

    def y(self, fn, w=(), inc=None, n=16):
        if inc is None:
            self._emit("sync", fn, w, None)
            return None
        return self.dma("sync", fn, w, inc, n)

    def barrier(self, bar, dma_sems=()):
        for e in ("vector", "scalar", "gpsimd", "tensor"):
            sem = self.auto[e]
            self._emit(e, lambda en: en.nop(), ((sem, sem.c),), (bar.h, 1))
        self._emit("sync", lambda en: en.nop(),
                   tuple((s, s.c) for s in dma_sems), (bar.h, 1))
        bar.c += 5
        t = bar.c
        for e in ENGS:
            self._emit(e, lambda en: en.nop(), ((bar, t),), None)

    def replay(self, eng):
        lst = self.ops[eng]

        def f(e):
            for waits, fn, inc in lst:
                for (h, v) in waits:
                    e.wait_ge(h, v)
                inst = fn(e)
                if inc is not None:
                    inst.then_inc(inc[0], inc[1])
        return f


def build(dumps=()):
    nc = bass.Bass("TRN2", target_bir_lowering=False, detect_race_conditions=False)

    def scratch(name, shape, dt):
        kind = "ExternalOutput" if name in dumps else "Internal"
        return nc.dram_tensor(name, shape, dt, kind=kind).ap()

    d_x = nc.dram_tensor("x", [S, HID], F32, kind="ExternalInput").ap()
    d_win = nc.dram_tensor("w_in", [HID, PROJ], F32, kind="ExternalInput").ap()
    d_convw = nc.dram_tensor("conv_w", [4, CONVD], F32, kind="ExternalInput").ap()
    d_convb = nc.dram_tensor("conv_b", [CONVD], F32, kind="ExternalInput").ap()
    d_dtb = nc.dram_tensor("dt_bias", [NH], F32, kind="ExternalInput").ap()
    d_alog = nc.dram_tensor("a_log", [NH], F32, kind="ExternalInput").ap()
    d_dd = nc.dram_tensor("d_d", [NH], F32, kind="ExternalInput").ap()
    d_nw = nc.dram_tensor("norm_w", [INTER], F32, kind="ExternalInput").ap()
    d_wout = nc.dram_tensor("w_out", [INTER, HID], F32, kind="ExternalInput").ap()
    d_w1 = nc.dram_tensor("w1", [O_FFN, HID], F32, kind="ExternalInput").ap()
    d_w2 = nc.dram_tensor("w2", [HID, I_FFN], F32, kind="ExternalInput").ap()
    d_drep = nc.dram_tensor("d_rep", [INTER], F32, kind="ExternalInput").ap()

    d_out = nc.dram_tensor("out", [S, HID], F32, kind="ExternalOutput").ap()

    d_xn = scratch("xn_d", [S, HID], F32) if "xn_d" in dumps else None
    d_xqT = scratch("xqT_d", [HID, S], BF16)
    d_gate = scratch("gate_d", [INTER, S], F32)
    d_hbcc = scratch("hbcc_d", [CONVD, S], F32)
    d_dtdump = scratch("dt_d", [NH, S], F32) if "dt_d" in dumps else None
    d_qw1 = scratch("qw1_d", [O_FFN, HID], BF16)
    d_qw2 = scratch("qw2_d", [HID, I_FFN], BF16)
    d_w1t = scratch("w1t_d", [HID, O_FFN], BF16)
    d_w2t = scratch("w2t_d", [I_FFN, HID], BF16)
    d_h = scratch("h_d", [I_FFN, S], F32)
    d_yf = scratch("yf_d", [INTER, S], F32) if "yf_d" in dumps else None
    d_yn = scratch("yn_d", [INTER, S], F32) if "yn_d" in dumps else None
    d_rs = scratch("rs_d", [128, S], F32) if "rs_d" in dumps else None
    d_dbg2 = scratch("dbg2_d", [384, 1024], F32) if "dbg2_d" in dumps else None
    d_g = scratch("g_d", [HID, S], F32)

    with contextlib.ExitStack() as stk:
        E = stk.enter_context

        sm = {k: Sem(E(nc.semaphore(k))) for k in
              ("bar", "dma", "dmx", "dmw", "dmg", "dms", "dss", "dmq", "te", "ve", "se", "ge")}
        bar, dma, dmx, dmw, dms = sm["bar"], sm["dma"], sm["dmx"], sm["dmw"], sm["dms"]
        dmg = sm["dmg"]
        dss, dmq = sm["dss"], sm["dmq"]
        te, ve, se, ge = sm["te"], sm["ve"], sm["se"], sm["ge"]
        p = Prog(nc, {"tensor": te, "vector": ve, "scalar": se, "gpsimd": ge})

        ident = E(nc.sbuf_tensor("ident", [128, 128], F32)).ap()
        identb = E(nc.sbuf_tensor("identb", [128, 128], BF16)).ap()
        convw_c = E(nc.sbuf_tensor("convw_c", [128, 4 * 48], F32)).ap()
        convb_c = E(nc.sbuf_tensor("convb_c", [128, 48], F32)).ap()
        dtb_c = E(nc.sbuf_tensor("dtb_c", [NH, 1], F32)).ap()
        alog_c = E(nc.sbuf_tensor("alog_c", [NH, 1], F32)).ap()
        acol = E(nc.sbuf_tensor("acol", [NH, 1], F32)).ap()
        dd_c = E(nc.sbuf_tensor("dd_c", [NH, 1], F32)).ap()
        nw_c = E(nc.sbuf_tensor("nw_c", [128, 32], F32)).ap()
        sxcols = E(nc.sbuf_tensor("sxcols", [128, 16], F32)).ap()
        sw1c = E(nc.sbuf_tensor("sw1c", [128, 88], F32)).ap()
        sw2c = E(nc.sbuf_tensor("sw2c", [128, 16], F32)).ap()
        shcols = E(nc.sbuf_tensor("shcols", [128, 16], F32)).ap()
        shrcp = E(nc.sbuf_tensor("shrcp", [128, 16], F32)).ap()
        dt_sb = E(nc.sbuf_tensor("dt_sb", [NH, S], F32)).ap()

        def ld(dst, src, slow=False):
            if slow:
                return lambda e: e.dma_start(out=dst, in_=src, allow_slow_non_contiguous=True)
            return lambda e: e.dma_start(out=dst, in_=src)

        p.y(ld(convw_c[:, :], d_convw.rearrange("j (mt p) -> p (j mt)", p=128), True), inc=dma)
        p.y(ld(convb_c[:, :], d_convb.rearrange("(mt p) -> p mt", p=128), True), inc=dma)
        p.y(ld(dtb_c[:, :], d_dtb.rearrange("(h o) -> h o", o=1), True), inc=dma)
        p.y(ld(alog_c[:, :], d_alog.rearrange("(h o) -> h o", o=1), True), inc=dma)
        p.y(ld(dd_c[:, :], d_dd.rearrange("(h o) -> h o", o=1), True), inc=dma)
        p.y(ld(nw_c[:, :], d_nw.rearrange("(mt p) -> p mt", p=128), True), inc=dma)
        v_consts = dma.c

        tmpones = E(nc.sbuf_tensor("tmpones", [128, 128], F32)).ap()
        p.g(lambda e: e.memset(ident[:, :], 0.0))
        p.g(lambda e: e.memset(tmpones[:, :], 1.0))
        g_consts = p.g(lambda e: e.affine_select(ident[:, :], tmpones[:, :], [[1, 128]],
                                                 ALU.is_equal, 0.0, base=0, channel_multiplier=-1))
        identb_done = p.v(lambda e: e.tensor_copy(identb[:, :], ident[:, :]), w=[(ge, g_consts)])

        se_exp = p.s(lambda e: e.activation(acol[:, :], alog_c[:, :], AF.Exp),
                     w=[(dma, v_consts)])
        acol_done = p.v(lambda e: e.tensor_scalar(acol[:, :], acol[:, :], -1.0, None, ALU.mult),
                        w=[(se, se_exp)])

        with contextlib.ExitStack() as ph:
            Eh = ph.enter_context
            xnT = Eh(nc.sbuf_tensor("xnT", [128, KT * S], F32)).ap()
            xin = Eh(nc.sbuf_tensor("xin", [128, HID], F32)).ap()
            xnst = Eh(nc.sbuf_tensor("xnst", [128, HID], F32)).ap()
            xqst = Eh(nc.sbuf_tensor("xqst", [128, HID], BF16)).ap()
            qtmp = Eh(nc.sbuf_tensor("qtmp", [128, HID], I32)).ap()
            f32a = Eh(nc.sbuf_tensor("f32a", [128, HID], F32)).ap()
            f32b = Eh(nc.sbuf_tensor("f32b", [128, HID], F32)).ap()
            f32c = xin
            mcol = Eh(nc.sbuf_tensor("mcol", [128, 1], F32)).ap()
            vcol = Eh(nc.sbuf_tensor("vcol", [128, 1], F32)).ap()
            rcol = Eh(nc.sbuf_tensor("rcol", [128, 1], F32)).ap()
            rcol2 = Eh(nc.sbuf_tensor("rcol2", [128, 1], F32)).ap()
            tb_st = Eh(nc.sbuf_tensor("tb_st", [128, 128], BF16)).ap()
            p0ps = contextlib.ExitStack()
            ps_tr = p0ps.enter_context(nc.psum_tensor("ps_tr", [128, 128], F32)).ap()
            ps_trb = p0ps.enter_context(nc.psum_tensor("ps_trb", [128, 128], BF16)).ap()

            trans_done = 0
            prev_tb_dma = 0
            xq_done = 0
            for i in range(16):
                ldtgt = p.y(ld(xin[:, :], d_x[i * 128:(i + 1) * 128, :]),
                            w=[(ve, xq_done)], inc=dmx)
                p.v(lambda e: e.tensor_reduce(mcol[:, :], xin[:, :], AXX, ALU.add),
                    w=[(dmx, ldtgt)])
                p.v(lambda e: e.tensor_scalar(mcol[:, :], mcol[:, :], 1.0 / HID, None, ALU.mult))
                p.v(lambda e: e.tensor_scalar(f32a[:, :], xin[:, :], mcol[:, :], None, ALU.subtract))
                vq = p.v(lambda e: e.activation(None, None, None) if False else e.engine_nop())
                sq = p.s(lambda e: e.activation(f32b[:, :], f32a[:, :], AF.Square), w=[(ve, vq)])
                p.v(lambda e: e.tensor_reduce(vcol[:, :], f32b[:, :], AXX, ALU.add),
                    w=[(se, sq)])
                p.v(lambda e: e.tensor_scalar(vcol[:, :], vcol[:, :], 1.0 / HID, 1e-5, ALU.mult, ALU.add))
                vr = p.v(lambda e: e.reciprocal(rcol[:, :], vcol[:, :]))
                sr = p.s(lambda e: e.activation(rcol2[:, :], rcol[:, :], AF.Sqrt), w=[(ve, vr)])
                xn_done = p.v(lambda e: e.tensor_scalar(xnst[:, :], f32a[:, :], rcol2[:, :], None,
                                                        ALU.mult),
                              w=[(se, sr), (te, trans_done)])
                if d_xn is not None:
                    p.y(lambda e, i=i: e.dma_start(out=d_xn[i * 128:(i + 1) * 128, :], in_=xnst[:, :]),
                        w=[(ve, xn_done)], inc=dms)

                p.v(lambda e: e.tensor_reduce(rcol[:, :], xnst[:, :], AXX, ALU.max,
                                              apply_absolute_value=True))
                p.v(lambda e: e.tensor_scalar(rcol2[:, :], rcol[:, :], R127, None, ALU.mult))
                p.v(lambda e: e.scalar_tensor_tensor(vcol[:, :], rcol2[:, :], -127.0, rcol[:, :],
                                                     ALU.mult, ALU.add))
                p.v(lambda e, i=i: e.scalar_tensor_tensor(sxcols[:, i:i + 1], vcol[:, :], R127,
                                                          rcol2[:, :], ALU.mult, ALU.add))
                p.v(lambda e, i=i: e.tensor_scalar(sxcols[:, i:i + 1], sxcols[:, i:i + 1], 1e-8,
                                                   None, ALU.max))
                p.v(lambda e, i=i: e.reciprocal(rcol[:, :], sxcols[:, i:i + 1]))
                p.v(lambda e: e.tensor_scalar(f32a[:, :], xnst[:, :], rcol[:, :], 127.0,
                                              ALU.mult, ALU.min))
                p.v(lambda e: e.tensor_scalar(qtmp[:, :], f32a[:, :], -127.0, None, ALU.max))
                p.v(lambda e: e.scalar_tensor_tensor(f32b[:, :], f32a[:, :], -127.0, qtmp[:, :],
                                                     ALU.max, ALU.subtract))
                p.v(lambda e: e.tensor_scalar(f32a[:, :], f32b[:, :], 0.5, None, ALU.is_gt))
                p.v(lambda e: e.tensor_scalar(f32c[:, :], f32b[:, :], -0.5, None, ALU.is_lt))
                p.v(lambda e: e.scalar_tensor_tensor(f32b[:, :], f32a[:, :], 1.0, qtmp[:, :],
                                                     ALU.mult, ALU.add))
                xq_done = p.v(lambda e: e.tensor_tensor(xqst[:, :], f32b[:, :], f32c[:, :],
                                                        ALU.subtract))

                for k in range(KT):
                    w0 = [(ve, xn_done)] if k == 0 else []
                    if i == 0 and k == 0:
                        w0.append((ge, g_consts))
                    t1 = p.t(lambda e, k=k: e.transpose(ps_tr[:, :], xnst[:, k * 128:(k + 1) * 128],
                                                        ident[:, :]),
                             w=w0, inc=True)
                    c1 = p.v(lambda e, k=k, i=i: e.tensor_copy(
                        xnT[:, k * S + i * 128: k * S + (i + 1) * 128], ps_tr[:, :]),
                        w=[(te, t1)])
                    w1l = [(ve, xq_done)] if k == 0 else []
                    if i == 0 and k == 0:
                        w1l.append((ve, identb_done))
                    t2 = p.t(lambda e, k=k: e.transpose(ps_trb[:, :], xqst[:, k * 128:(k + 1) * 128],
                                                        identb[:, :]),
                             w=w1l, inc=True)
                    c2 = p.v(lambda e: e.tensor_copy(tb_st[:, :], ps_trb[:, :]),
                             w=[(te, t2), (dss, prev_tb_dma)])
                    prev_tb_dma = p.dma("scalar", lambda e, k=k, i=i: e.dma_start(
                        out=d_xqT[k * 128:(k + 1) * 128, i * 128:(i + 1) * 128], in_=tb_st[:, :]),
                        w=[(ve, c2)], dsem=dss)
                    trans_done = p.t(lambda e: e.nop(), w=[(ve, c2), (ve, c1)], inc=True)

            p0_done_v = ve.c
            p0ps.close()

            wblk = Eh(nc.sbuf_tensor("wblk", [128, HID], F32)).ap()
            wblk2 = Eh(nc.sbuf_tensor("wblk2", [128, HID], F32)).ap()
            evst = Eh(nc.sbuf_tensor("evst", [128, S + 4], F32)).ap()
            dttmp = evst[0:NH, 0:S]
            evout = f32a
            evout2 = f32b
            ps_mm = Eh(nc.psum_tensor("ps_mm", [128, S], F32)).ap()
            ps_mm2 = Eh(nc.psum_tensor("ps_mm2", [128, S], F32)).ap()

            p.v(lambda e: e.memset(evst[:, 0:1], 0.0))
            evst_zeroed = p.v(lambda e: e.memset(evst[:, S + 1:S + 4], 0.0))

            wqa, wqb, wqo = xnst, xin, xqst
            wqi = qtmp
            qc1, qc2, qc3 = mcol, vcol, rcol
            wqst = {"ld_free": 0, "st_free": 0, "qw1_done": 0, "qw2_done": 0}

            def wq_load(src_ap, width, extra=()):
                lt = p.dma("scalar", lambda e: e.dma_start(out=wqa[:, :width], in_=src_ap),
                           w=[(ve, wqst["ld_free"])] + list(extra), dsem=dmq)
                return lt

            def wq_scale_from_qc1(scol):
                p.v(lambda e: e.tensor_scalar(qc2[:, :], qc1[:, :], R127, None, ALU.mult))
                p.v(lambda e: e.scalar_tensor_tensor(qc3[:, :], qc2[:, :], -127.0, qc1[:, :],
                                                     ALU.mult, ALU.add))
                p.v(lambda e: e.scalar_tensor_tensor(scol, qc3[:, :], R127, qc2[:, :],
                                                     ALU.mult, ALU.add))
                p.v(lambda e: e.tensor_scalar(scol, scol, 1e-8, None, ALU.max))
                p.v(lambda e: e.reciprocal(qc3[:, :], scol))

            def wq_quant_sub(dst, width, key):
                p.v(lambda e: e.tensor_scalar(wqb[:, :width], wqa[:, :width], qc3[:, :], 127.0,
                                              ALU.mult, ALU.min))
                p.v(lambda e: e.tensor_scalar(wqi[:, :width], wqb[:, :width],
                                              -127.0, None, ALU.max))
                p.v(lambda e: e.scalar_tensor_tensor(wqa[:, :width], wqb[:, :width], -127.0,
                                                     wqi[:, :width], ALU.max, ALU.subtract))
                p.v(lambda e: e.tensor_scalar(wqb[:, :width], wqa[:, :width], 0.5, None, ALU.is_gt))
                p.v(lambda e: e.tensor_scalar(wqa[:, :width], wqa[:, :width], -0.5, None, ALU.is_lt))
                p.v(lambda e: e.scalar_tensor_tensor(wqb[:, :width], wqb[:, :width], 1.0,
                                                     wqi[:, :width], ALU.mult, ALU.add))
                cq = p.v(lambda e: e.tensor_tensor(wqo[:, :width], wqb[:, :width], wqa[:, :width],
                                                   ALU.subtract), w=[(dss, wqst["st_free"])])
                wqst["ld_free"] = cq
                wqst["st_free"] = p.dma("scalar", lambda e: e.dma_start(out=dst, in_=wqo[:, :width]),
                                        w=[(ve, cq)], dsem=dss)
                wqst[key] = wqst["st_free"]

            def emit_wq_w1(mp, extra=()):
                lt = wq_load(d_w1[mp * 128:(mp + 1) * 128, :], HID, extra)
                p.v(lambda e: e.tensor_reduce(qc1[:, :], wqa[:, :HID], AXX, ALU.max,
                                              apply_absolute_value=True), w=[(dmq, lt)])
                wq_scale_from_qc1(sw1c[:, mp:mp + 1])
                wq_quant_sub(d_qw1[mp * 128:(mp + 1) * 128, :], HID, "qw1_done")

            W2SUB = [(0, 2048), (2048, 2048), (4096, 1536)]

            def emit_wq_w2(m2):
                for si, (off, wd) in enumerate(W2SUB):
                    lt = wq_load(d_w2[m2 * 128:(m2 + 1) * 128, off:off + wd], wd)
                    p.v(lambda e, wd=wd: e.tensor_reduce(qc2[:, :], wqa[:, :wd], AXX, ALU.max,
                                                         apply_absolute_value=True),
                        w=[(dmq, lt)])
                    if si == 0:
                        p.v(lambda e: e.tensor_copy(qc1[:, :], qc2[:, :]))
                    else:
                        p.v(lambda e: e.tensor_tensor(qc1[:, :], qc1[:, :], qc2[:, :], ALU.max))
                    wqst["ld_free"] = ve.c
                wq_scale_from_qc1(sw2c[:, m2:m2 + 1])
                for (off, wd) in W2SUB:
                    lt = wq_load(d_w2[m2 * 128:(m2 + 1) * 128, off:off + wd], wd)
                    p.v(lambda e: e.nop(), w=[(dmq, lt)])
                    wq_quant_sub(d_qw2[m2 * 128:(m2 + 1) * 128, off:off + wd], wd, "qw2_done")

            wq_work = [("w1", mp) for mp in range(O_FFN // 128)] +                       [("w2", m2) for m2 in range(HID // 128)]
            wq_idx = [0]

            def drain_wq(n):
                while n > 0 and wq_idx[0] < len(wq_work):
                    kind, idx = wq_work[wq_idx[0]]
                    extra = [(ve, p0_done_v), (te, trans_done)] if wq_idx[0] == 0 else ()
                    if kind == "w1":
                        emit_wq_w1(idx, extra)
                    else:
                        emit_wq_w2(idx)
                    wq_idx[0] += 1
                    n -= 1

            NM = 81
            ev_done = [0, 0]
            wb_free = [0, 0]
            evout_free = 0
            evout_free_sem = None
            evout2_free = 0
            for m in range(NM):
                mw = 128 if m < 80 else 64
                par = m % 2
                wb = wblk if par == 0 else wblk2
                ps = ps_mm if par == 0 else ps_mm2
                wtgts = []
                for kq in range(4):
                    wl = [(te, wb_free[par])] if kq == 0 else []
                    if m == 0 and kq == 0:
                        wl.append((ve, p0_done_v))
                    wtgts.append(p.y(lambda e, m=m, kq=kq, mw=mw, wb=wb: e.dma_start(
                        out=wb[:, kq * 4 * mw:(kq + 1) * 4 * mw].rearrange("p (k c) -> p k c", k=4),
                        in_=d_win[kq * 512:(kq + 1) * 512, m * 128:m * 128 + mw]
                            .rearrange("(k p) c -> p k c", p=128)),
                        w=wl, inc=dmw))
                last = None
                for k in range(KT):
                    for nn in range(4):
                        wlist = []
                        if nn == 0 and k % 4 == 0:
                            wlist.append((dmw, wtgts[k // 4]))
                        if k == 0 and nn == 0:
                            wlist.append((ve, p0_done_v))
                            wlist.append((ve, ev_done[par]))
                        last = p.t(lambda e, k=k, nn=nn, mw=mw, wb=wb, ps=ps: e.matmul(
                            ps[0:mw, nn * 512:(nn + 1) * 512],
                            wb[:, k * mw:(k + 1) * mw],
                            xnT[:, k * S + nn * 512: k * S + (nn + 1) * 512],
                            start=(k == 0), stop=(k == KT - 1)),
                            w=wlist, inc=True)
                wb_free[par] = last

                if m < 32:
                    wl = [(te, last)]
                    if evout_free_sem is not None:
                        wl.append((evout_free_sem, evout_free))
                    c = p.v(lambda e, ps=ps: e.tensor_copy(evout[:, :], ps[:, :]), w=wl)
                    ev_done[par] = c
                    evout_free = p.dma("scalar", lambda e, m=m: e.dma_start(
                        out=d_gate[m * 128:(m + 1) * 128, :], in_=evout[:, :]),
                        w=[(ve, c)], dsem=dss)
                    evout_free_sem = dss
                elif m < 80:
                    mt = m - 32
                    c = p.v(lambda e, ps=ps: e.tensor_copy(evst[:, 1:S + 1], ps[:, :]),
                            w=[(te, last), (ve, evst_zeroed)])
                    ev_done[par] = c
                    wl = []
                    if evout_free_sem is not None:
                        wl.append((evout_free_sem, evout_free))
                    p.v(lambda e, mt=mt: e.tensor_scalar(
                        evout[:, :], evst[:, 0:S], convw_c[:, mt:mt + 1], None, ALU.mult),
                        w=wl)
                    for j in range(1, 4):
                        p.v(lambda e, mt=mt, j=j: e.scalar_tensor_tensor(
                            evout[:, :], evst[:, j:j + S], convw_c[:, j * 48 + mt:j * 48 + mt + 1],
                            evout[:, :], ALU.mult, ALU.add))
                    cv = ve.c
                    evout_free_sem = se
                    sv = p.s(lambda e, mt=mt: e.activation(
                        evout2[:, :], evout[:, :], AF.Silu, bias=convb_c[:, mt:mt + 1]),
                        w=[(ve, cv), (dss, evout2_free)])
                    evout_free = sv
                    evout2_free = p.dma("scalar", lambda e, mt=mt: e.dma_start(
                        out=d_hbcc[mt * 128:(mt + 1) * 128, :], in_=evout2[:, :]),
                        w=[(se, sv)], dsem=dss)
                else:
                    wl = [(te, last)]
                    if evout_free_sem is not None:
                        wl.append((evout_free_sem, evout_free))
                    c = p.v(lambda e, ps=ps: e.tensor_scalar(evout[0:NH, :], ps[0:NH, :],
                                                             dtb_c[:, :], None, ALU.add), w=wl)
                    ev_done[par] = c
                    p.s(lambda e: e.activation(dttmp[:, :], evout[0:NH, :], AF.Abs),
                        w=[(ve, c)])
                    p.s(lambda e: e.activation(dt_sb[:, :], dttmp[:, :], AF.Exp, scale=-1.0))
                    sv = p.s(lambda e: e.activation(dttmp[:, :], dt_sb[:, :], AF.Ln, bias=1.0))
                    dv = p.v(lambda e: e.scalar_tensor_tensor(dt_sb[:, :], evout[0:NH, :], 0.0,
                                                              dttmp[:, :], ALU.max, ALU.add),
                             w=[(se, sv)])
                    if d_dtdump is not None:
                        p.y(lambda e: e.dma_start(out=d_dtdump[:, :], in_=dt_sb[:, :]),
                            w=[(ve, dv)], inc=dms)
                if m >= 2:
                    drain_wq(2)


            drain_wq(len(wq_work))
            qw1_done = wqst["qw1_done"]
            qw2_done = wqst["qw2_done"]

        xqT_written = prev_tb_dma
        p.barrier(bar, dma_sems=(dma, dmx, dmw, dmg, dms, dss, dmq))

        sh_done = None
        with contextlib.ExitStack() as ph:
            Eh = ph.enter_context
            trst = Eh(nc.sbuf_tensor("trst", [128, O_FFN], BF16)).ap()
            xqT_sb = Eh(nc.sbuf_tensor("xqT_sb", [128, KT * S], BF16)).ap()
            sx_rep = Eh(nc.sbuf_tensor("sx_rep", [128, S], F32)).ap()
            diagst = Eh(nc.sbuf_tensor("diagst", [128, 128], F32)).ap()
            wst_g = Eh(nc.sbuf_tensor("wst_g", [128, 2048], BF16)).ap()
            wst_u = Eh(nc.sbuf_tensor("wst_u", [128, 2048], BF16)).ap()
            evG = Eh(nc.sbuf_tensor("evG", [128, 1024], F32)).ap()
            evU = Eh(nc.sbuf_tensor("evU", [128, 1024], F32)).ap()
            sgout = Eh(nc.sbuf_tensor("sgout", [128, 1024], F32)).ap()
            hstage = Eh(nc.sbuf_tensor("hstage", [128, 1024], F32)).ap()
            habs = Eh(nc.sbuf_tensor("habs", [128, 1024], F32)).ap()
            hmax_acc = Eh(nc.sbuf_tensor("hmax_acc", [128, S], F32)).ap()
            hmcol = Eh(nc.sbuf_tensor("hmcol", [128, 1], F32)).ap()
            hmc2 = Eh(nc.sbuf_tensor("hmc2", [128, 1], F32)).ap()
            hmc3 = Eh(nc.sbuf_tensor("hmc3", [128, 1], F32)).ap()
            p2ps = contextlib.ExitStack()
            ps_g = [p2ps.enter_context(nc.psum_tensor(f"ps_g{i}", [128, 1024], F32)).ap()
                    for i in range(2)]
            ps_u = [p2ps.enter_context(nc.psum_tensor(f"ps_u{i}", [128, 1024], F32)).ap()
                    for i in range(2)]
            ps_rep = ps_g[0][:, 0:128]

            trdone = 0
            for k in range(KT):
                tt_ = p.y(lambda e, k=k: e.dma_start_transpose(
                    trst[:, :O_FFN], d_qw1[:, k * 128:(k + 1) * 128]),
                    w=[(dss, qw1_done), (dms, trdone)], inc=dma)
                trdone = p.y(lambda e, k=k: e.dma_start(
                    out=d_w1t[k * 128:(k + 1) * 128, :], in_=trst[:, :O_FFN]),
                    w=[(dma, tt_)], inc=dms)
            wT_done = trdone

            xq_lds = 0
            for k in range(KT):
                xq_lds = p.y(lambda e, k=k: e.dma_start(
                    out=xqT_sb[:, k * S:(k + 1) * S], in_=d_xqT[k * 128:(k + 1) * 128, :]),
                    w=([(dss, xqT_written)] if k == 0 else []), inc=dma)

            rep_done = 0
            for t in range(16):
                dg = p.v(lambda e, t=t: e.tensor_scalar(diagst[:, :], ident[:, :],
                                                        sxcols[:, t:t + 1], None, ALU.mult),
                         w=[(te, rep_done)] if t else [])
                tm = p.t(lambda e: e.matmul(ps_rep[:, :], tmpones[:, :], diagst[:, :],
                                            start=True, stop=True),
                         w=[(ve, dg)], inc=True)
                rep_done = tm
                p.v(lambda e, t=t: e.tensor_copy(sx_rep[:, t * 128:(t + 1) * 128], ps_rep[:, :]),
                    w=[(te, tm)])
            sxrep_done = p.v(lambda e: e.memset(hmax_acc[:, :], 0.0))

            ev_free = [[0, 0], [0, 0]]

            wst_free = [0, 0]
            hst_free = 0
            habs_free = 0
            sg_free = 0
            evG_free = 0
            wst_dump_gate = 0
            for mp in range(44):
                wts = []
                for half, mrow in ((0, mp), (1, 44 + mp)):
                    wst = wst_g if half == 0 else wst_u
                    wl0 = [(te, wst_free[half])]
                    if mp == 0:
                        wl0.append((dms, wT_done))
                    if mp == 1 and half == 0 and wst_dump_gate:
                        wl0.append((dms, wst_dump_gate))
                    wts.append(p.y(lambda e, mrow=mrow, wst=wst: e.dma_start(
                        out=wst[:, :].rearrange("p (k c) -> p k c", k=KT),
                        in_=d_w1t[:, mrow * 128:(mrow + 1) * 128]
                            .rearrange("(k p) c -> p k c", p=128)),
                        w=wl0, inc=dmw))
                if d_dbg2 is not None and mp == 0:
                    crw = p.v(lambda e: e.tensor_copy(evU[:, :], wst_g[:, 0:1024]),
                              w=[(dmw, wts[0])])
                    wst_dump_gate = p.y(lambda e: e.dma_start(out=d_dbg2[128:256, :], in_=evU[:, :]),
                                        w=[(ve, crw)], inc=dms)
                mm_done = [[None, None], [None, None]]
                for half in range(2):
                    wst = wst_g if half == 0 else wst_u
                    psl = ps_g if half == 0 else ps_u
                    for sc in range(2):
                        for k in range(KT):
                            wl = []
                            if k == 0:
                                wl.append((dmw, wts[half]))
                                wl.append((ve, ev_free[half][sc]))
                                if mp == 0 and half == 0 and sc == 0:
                                    wl.append((dma, xq_lds))
                                    wl.append((ve, sxrep_done))
                            for half2 in range(2):
                                mm_done[half][sc] = p.t(lambda e, k=k, sc=sc, wst=wst, psl=psl,
                                                        h2=half2: e.matmul(
                                    psl[sc][:, h2 * 512:(h2 + 1) * 512],
                                    wst[:, k * 128:(k + 1) * 128],
                                    xqT_sb[:, k * S + sc * 1024 + h2 * 512:
                                           k * S + sc * 1024 + (h2 + 1) * 512],
                                    start=(k == 0), stop=(k == KT - 1)),
                                    w=(wl if half2 == 0 else []), inc=True)
                    wst_free[half] = mm_done[half][1]

                for sc in range(2):
                    ssl = slice(sc * 1024, (sc + 1) * 1024)
                    if d_dbg2 is not None and mp == 0 and sc == 0:
                        cr = p.v(lambda e: e.tensor_copy(evU[:, :], ps_g[0][:, :]),
                                 w=[(te, mm_done[0][0])])
                        p.y(lambda e: e.dma_start(out=d_dbg2[0:128, :], in_=evU[:, :]),
                            w=[(ve, cr)], inc=dms)
                        cr3 = p.v(lambda e: e.tensor_copy(evU[:, :], xqT_sb[:, 0:1024]))
                        p.y(lambda e: e.dma_start(out=d_dbg2[256:384, :], in_=evU[:, :]),
                            w=[(ve, cr3)], inc=dms)
                        p.v(lambda e: e.nop(), w=[(dms, dms.c)])
                    cG = p.v(lambda e, sc=sc, mp=mp, ssl=ssl: e.scalar_tensor_tensor(
                        evG[:, :], ps_g[sc][:, :], sw1c[:, mp:mp + 1], sx_rep[:, ssl],
                        ALU.mult, ALU.mult),
                        w=[(te, mm_done[0][sc]), (se, evG_free)])
                    ev_free[0][sc] = cG
                    sg = p.s(lambda e: e.activation(sgout[:, :], evG[:, :], AF.Silu),
                             w=[(ve, cG), (ve, sg_free)])
                    evG_free = sg
                    cU = p.v(lambda e, sc=sc, mp=mp, ssl=ssl: e.scalar_tensor_tensor(
                        evU[:, :], ps_u[sc][:, :], sw1c[:, 44 + mp:45 + mp], sx_rep[:, ssl],
                        ALU.mult, ALU.mult),
                        w=[(te, mm_done[1][sc])])
                    ev_free[1][sc] = cU
                    hh = p.v(lambda e: e.tensor_tensor(hstage[:, :], sgout[:, :], evU[:, :],
                                                       ALU.mult),
                             w=[(se, sg), (dss, hst_free)])
                    sg_free = hh
                    ab = p.s(lambda e: e.activation(habs[:, :], hstage[:, :], AF.Abs),
                             w=[(ve, hh), (ve, habs_free)])
                    habs_free = p.v(lambda e, ssl=ssl: e.tensor_tensor(
                        hmax_acc[:, ssl], hmax_acc[:, ssl], habs[:, :], ALU.max),
                        w=[(se, ab)])
                    hst_free = p.dma("scalar", lambda e, mp=mp, sc=sc: e.dma_start(
                        out=d_h[mp * 128:(mp + 1) * 128, sc * 1024:(sc + 1) * 1024],
                        in_=hstage[:, :]),
                        w=[(ve, hh)], dsem=dss)

            for k in range(I_FFN // 128):
                tt_ = p.y(lambda e, k=k: e.dma_start_transpose(
                    trst[:, :HID], d_qw2[:, k * 128:(k + 1) * 128]),
                    w=[(dss, qw2_done), (dms, trdone)], inc=dma)
                trdone = p.y(lambda e, k=k: e.dma_start(
                    out=d_w2t[k * 128:(k + 1) * 128, :], in_=trst[:, :HID]),
                    w=[(dma, tt_)], inc=dms)
            w2T_done = trdone

            trp_done = 0
            for t in range(16):
                tq = p.t(lambda e, t=t: e.transpose(ps_rep[:, :], hmax_acc[:, t * 128:(t + 1) * 128],
                                                    ident[:, :]),
                         w=[(ve, habs_free), (te, rep_done)] if t == 0 else [(ve, trp_done)],
                         inc=True)
                p.v(lambda e: e.tensor_reduce(hmcol[:, :], ps_rep[:, :], AXX, ALU.max),
                    w=[(te, tq)])
                p.v(lambda e: e.tensor_scalar(hmc2[:, :], hmcol[:, :], R127, None, ALU.mult))
                p.v(lambda e: e.scalar_tensor_tensor(hmc3[:, :], hmc2[:, :], -127.0, hmcol[:, :],
                                                     ALU.mult, ALU.add))
                p.v(lambda e, t=t: e.scalar_tensor_tensor(shcols[:, t:t + 1], hmc3[:, :], R127,
                                                          hmc2[:, :], ALU.mult, ALU.add))
                p.v(lambda e, t=t: e.tensor_scalar(shcols[:, t:t + 1], shcols[:, t:t + 1], 1e-8,
                                                   None, ALU.max))
                trp_done = p.v(lambda e, t=t: e.reciprocal(shrcp[:, t:t + 1], shcols[:, t:t + 1]))
            sh_done = trp_done
            p2ps.close()

        p.barrier(bar, dma_sems=(dma, dmx, dmw, dmg, dms, dss, dmq))

        with contextlib.ExitStack() as ph:
            Eh = ph.enter_context
            hq_sb = Eh(nc.sbuf_tensor("hq_sb", [128, 44 * 1024], BF16)).ap()
            sh_rep = Eh(nc.sbuf_tensor("sh_rep", [128, S], F32)).ap()
            rcph_rep = Eh(nc.sbuf_tensor("rcph_rep", [128, S], F32)).ap()
            diag2 = Eh(nc.sbuf_tensor("diag2", [128, 128], F32)).ap()
            htile = Eh(nc.sbuf_tensor("htile", [128, 1024], F32)).ap()
            fqa = Eh(nc.sbuf_tensor("fqa", [128, 1024], F32)).ap()
            fqb = Eh(nc.sbuf_tensor("fqb", [128, 1024], F32)).ap()
            hqi = Eh(nc.sbuf_tensor("hqi", [128, 1024], I32)).ap()
            w2st = Eh(nc.sbuf_tensor("w2st", [128, 44 * 128], BF16)).ap()
            w2st2 = Eh(nc.sbuf_tensor("w2st2", [128, 44 * 128], BF16)).ap()
            evf = Eh(nc.sbuf_tensor("evf", [128, 1024], F32)).ap()
            gout = Eh(nc.sbuf_tensor("gout", [128, 1024], F32)).ap()
            p2bps = contextlib.ExitStack()
            ps_rep2 = p2bps.enter_context(nc.psum_tensor("ps_rep2", [128, 128], F32)).ap()
            ps_f = [p2bps.enter_context(nc.psum_tensor(f"ps_f{i}", [128, 1024], F32)).ap()
                    for i in range(2)]

            rep_done = 0
            for t in range(16):
                dg = p.v(lambda e, t=t: e.tensor_scalar(diag2[:, :], ident[:, :],
                                                        shcols[:, t:t + 1], None, ALU.mult),
                         w=[(te, rep_done)])
                tm = p.t(lambda e: e.matmul(ps_rep2[:, :], tmpones[:, :], diag2[:, :],
                                            start=True, stop=True), w=[(ve, dg)], inc=True)
                p.v(lambda e, t=t: e.tensor_copy(sh_rep[:, t * 128:(t + 1) * 128], ps_rep2[:, :]),
                    w=[(te, tm)])
                dg = p.v(lambda e, t=t: e.tensor_scalar(diag2[:, :], ident[:, :],
                                                        shrcp[:, t:t + 1], None, ALU.mult))
                tm = p.t(lambda e: e.matmul(ps_rep2[:, :], tmpones[:, :], diag2[:, :],
                                            start=True, stop=True), w=[(ve, dg)], inc=True)
                rep_done = tm
                p.v(lambda e, t=t: e.tensor_copy(rcph_rep[:, t * 128:(t + 1) * 128], ps_rep2[:, :]),
                    w=[(te, tm)])

            g_stored = 0
            for sc in range(2):
                ssl = slice(sc * 1024, (sc + 1) * 1024)
                ht_free = 0
                hq_done = 0
                hq_done_k = [0] * 44
                for mp in range(44):
                    lt = p.y(lambda e, mp=mp, sc=sc: e.dma_start(
                        out=htile[:, :], in_=d_h[mp * 128:(mp + 1) * 128, sc * 1024:(sc + 1) * 1024]),
                        w=[(ve, ht_free)], inc=dma)
                    p.v(lambda e, ssl=ssl: e.tensor_tensor(fqa[:, :], htile[:, :], rcph_rep[:, ssl],
                                                           ALU.mult), w=[(dma, lt)])
                    ht_free = ve.c
                    p.v(lambda e: e.tensor_scalar(fqa[:, :], fqa[:, :], 127.0, None, ALU.min))
                    p.v(lambda e: e.tensor_scalar(hqi[:, :], fqa[:, :], -127.0, None, ALU.max))
                    p.v(lambda e: e.scalar_tensor_tensor(fqb[:, :], fqa[:, :], -127.0, hqi[:, :],
                                                         ALU.max, ALU.subtract))
                    p.v(lambda e: e.tensor_scalar(fqa[:, :], fqb[:, :], 0.5, None, ALU.is_gt))
                    p.v(lambda e: e.tensor_scalar(fqb[:, :], fqb[:, :], -0.5, None, ALU.is_lt))
                    p.v(lambda e: e.scalar_tensor_tensor(fqa[:, :], fqa[:, :], 1.0, hqi[:, :],
                                                         ALU.mult, ALU.add))
                    hq_done = p.v(lambda e, mp=mp: e.tensor_tensor(
                        hq_sb[:, mp * 1024:(mp + 1) * 1024], fqa[:, :], fqb[:, :], ALU.subtract))
                    hq_done_k[mp] = hq_done

                evf_free = 0
                gout_free = 0
                w2_free = [0, 0]
                mmf_prev = [0, 0]
                for m2 in range(16):
                    par = m2 % 2
                    w2b = w2st if par == 0 else w2st2
                    psf = ps_f[par]
                    wt = p.dma("scalar", lambda e, m2=m2, w2b=w2b: e.dma_start(
                        out=w2b[:, :].rearrange("p (k c) -> p k c", k=44),
                        in_=d_w2t[:, m2 * 128:(m2 + 1) * 128].rearrange("(k p) c -> p k c", p=128)),
                        w=[(te, w2_free[par])] + ([(dms, w2T_done)] if (sc == 0 and m2 < 2) else []),
                        dsem=dmq)
                    mmf = None
                    for k in range(44):
                        wl = []
                        if m2 < 2:
                            wl.append((ve, hq_done_k[k]))
                        elif k == 0:
                            wl.append((ve, hq_done))
                        if k == 0:
                            wl.append((dmq, wt))
                            wl.append((ve, mmf_prev[par]))
                        for half2 in range(2):
                            mmf = p.t(lambda e, k=k, w2b=w2b, psf=psf, h2=half2: e.matmul(
                                psf[:, h2 * 512:(h2 + 1) * 512],
                                w2b[:, k * 128:(k + 1) * 128],
                                hq_sb[:, k * 1024 + h2 * 512: k * 1024 + (h2 + 1) * 512],
                                start=(k == 0), stop=(k == 43)),
                                w=(wl if half2 == 0 else []), inc=True)
                    w2_free[par] = mmf
                    cf = p.v(lambda e, m2=m2, ssl=ssl, psf=psf: e.scalar_tensor_tensor(
                        evf[:, :], psf[:, :], sw2c[:, m2:m2 + 1], sh_rep[:, ssl],
                        ALU.mult, ALU.mult),
                        w=[(te, mmf), (se, evf_free)])
                    mmf_prev[par] = cf
                    sg2 = p.s(lambda e: e.activation(gout[:, :], evf[:, :], AF.Silu),
                              w=[(ve, cf), (dms, gout_free)])
                    evf_free = sg2
                    gout_free = p.y(lambda e, m2=m2, sc=sc: e.dma_start(
                        out=d_g[m2 * 128:(m2 + 1) * 128, sc * 1024:(sc + 1) * 1024],
                        in_=gout[:, :]),
                        w=[(se, sg2)], inc=dms)
                g_stored = gout_free
            p2bps.close()

        p.barrier(bar, dma_sems=(dma, dmx, dmw, dmg, dms, dss, dmq))

        with contextlib.ExitStack() as ph:
            Eh = ph.enter_context
            s_run = Eh(nc.sbuf_tensor("s_run", [128, NH * HD], F32)).ap()
            repbank = Eh(nc.sbuf_tensor("repbank", [64, NH * 64], F32)).ap()
            mask0 = Eh(nc.sbuf_tensor("mask0", [128, 256], F32)).ap()
            mask1 = Eh(nc.sbuf_tensor("mask1", [128, 256], F32)).ap()
            ones64 = Eh(nc.sbuf_tensor("ones64", [64, 256], F32)).ap()
            ddrepc = Eh(nc.sbuf_tensor("ddrepc", [128, 32], F32)).ap()
            ab_c = Eh(nc.sbuf_tensor("ab_c", [64, 256], F32)).ap()
            acum = Eh(nc.sbuf_tensor("acum", [64, 256], F32)).ap()
            eexp = Eh(nc.sbuf_tensor("eexp", [64, 256], F32)).ap()
            decayc = Eh(nc.sbuf_tensor("decayc", [64, 256], F32)).ap()
            acT = Eh(nc.sbuf_tensor("acT", [128, 128], F32)).ap()
            nacT = Eh(nc.sbuf_tensor("nacT", [128, 128], F32)).ap()
            dcT = Eh(nc.sbuf_tensor("dcT", [128, 128], F32)).ap()
            cdecbc = Eh(nc.sbuf_tensor("cdecbc", [128, 64], F32)).ap()
            diag3 = Eh(nc.sbuf_tensor("diag3", [128, 832], F32)).ap()
            hsdt_ct = Eh(nc.sbuf_tensor("hsdt_ct", [128, 2 * 256], F32)).ap()
            hsdtT = Eh(nc.sbuf_tensor("hsdtT", [128, 2 * INTER], F32)).ap()
            hsr = Eh(nc.sbuf_tensor("hsr", [128, 2 * 256], F32)).ap()
            hse = Eh(nc.sbuf_tensor("hse", [128, 2 * 256], F32)).ap()
            b_ch = Eh(nc.sbuf_tensor("b_ch", [128, 8 * 256], F32)).ap()
            c_ch = Eh(nc.sbuf_tensor("c_ch", [128, 8 * 256], F32)).ap()
            btsb = Eh(nc.sbuf_tensor("btsb", [128, 2 * 128], F32)).ap()
            gsb = Eh(nc.sbuf_tensor("gsb", [128, 512], F32)).ap()
            dmst = Eh(nc.sbuf_tensor("dmst", [128, 2 * 512], F32)).ap()
            ltsb = Eh(nc.sbuf_tensor("ltsb", [128, 2 * 512], F32)).ap()
            mtsb = Eh(nc.sbuf_tensor("mtsb", [128, 2 * 512], F32)).ap()
            bdst = Eh(nc.sbuf_tensor("bdst", [128, 2 * 256], F32)).ap()
            yfall = Eh(nc.sbuf_tensor("yfall", [128, 32 * 256], F32)).ap()
            gate_ct = Eh(nc.sbuf_tensor("gate_ct", [128, 2 * 256], F32)).ap()
            ebsb = Eh(nc.sbuf_tensor("ebsb", [128, 256], F32)).ap()
            yofe = Eh(nc.sbuf_tensor("yofe", [128, 256], F32)).ap()
            ysb = Eh(nc.sbuf_tensor("ysb", [128, 256], F32)).ap()
            sgate = Eh(nc.sbuf_tensor("sgate", [128, 256], F32)).ap()
            rstdb = Eh(nc.sbuf_tensor("rstdb", [128, 256], F32)).ap()
            wopst = Eh(nc.sbuf_tensor("wopst", [128, 32 * 128], F32)).ap()
            wopst2 = Eh(nc.sbuf_tensor("wopst2", [128, 32 * 128], F32)).ap()
            g_m = Eh(nc.sbuf_tensor("g_m", [128, 256], F32)).ap()
            x_m = Eh(nc.sbuf_tensor("x_m", [128, 2 * 128], F32)).ap()
            zsb = Eh(nc.sbuf_tensor("zsb", [128, 256], F32)).ap()
            osb = Eh(nc.sbuf_tensor("osb", [128, 2 * 128], F32)).ap()

            p3ps = contextlib.ExitStack()
            PS_dt = p3ps.enter_context(nc.psum_tensor("PS_dt", [128, 512], F32)).ap()
            PS_trA = p3ps.enter_context(nc.psum_tensor("PS_trA", [128, 512], F32)).ap()
            PS_trB = p3ps.enter_context(nc.psum_tensor("PS_trB", [128, 512], F32)).ap()
            PS_G = p3ps.enter_context(nc.psum_tensor("PS_G", [128, 512], F32)).ap()
            PS_sel = p3ps.enter_context(nc.psum_tensor("PS_sel", [128, 512], F32)).ap()
            PS_acc = p3ps.enter_context(nc.psum_tensor("PS_acc", [128, 512], F32)).ap()
            PS_yd = p3ps.enter_context(nc.psum_tensor("PS_yd", [128, 512], F32)).ap()
            PS_yo = p3ps.enter_context(nc.psum_tensor("PS_yo", [128, 512], F32)).ap()

            p.v(lambda e: e.memset(s_run[:, :], 0.0))
            p.v(lambda e: e.memset(ones64[:, :], 1.0))
            p.g(lambda e: e.memset(repbank[:, :], 1.0))
            p.g(lambda e: e.affine_select(repbank[:, :], repbank[:, :], [[1, 64], [0, 64]],
                                          ALU.is_equal, 0.0, base=0, channel_multiplier=-1))
            p.g(lambda e: e.memset(mask0[:, :], 0.0))
            p.g(lambda e: e.affine_select(mask0[:, :], mask0[:, :], [[1, 256]],
                                          ALU.is_ge, NEG, base=0, channel_multiplier=-1))
            p.g(lambda e: e.memset(mask1[:, :], 0.0))
            gcst = p.g(lambda e: e.affine_select(mask1[:, :], mask1[:, :], [[1, 256]],
                                                 ALU.is_ge, NEG, base=-128, channel_multiplier=-1))
            ddrep_ld = p.y(ld(ddrepc[:, :], d_drep.rearrange("(mt p) -> p mt", p=128), True),
                           inc=dma)
            consts3 = p.v(lambda e: e.nop(), w=[(ge, gcst), (dma, ddrep_ld)])

            diag_cons = 0
            dg6_cons = [0] * 6
            dgv_done = [0] * 6
            lt2_last = [0, 0]
            yd2_last = [0, 0]
            sel2_free = [0, 0]
            yd_evac2 = [0, 0]
            st2_cons = [0, 0]
            mt2_done = [0, 0]
            bd2_done = [0, 0]
            hse_free = [0, 0]
            sqrd_done0 = 0
            sqrd_done1 = 0
            bt_done = 0
            lt_last = 0
            yd_last = 0
            hs_free = 0
            gate_free = [0, 0]
            sgate_free = 0
            gm_free = [0, 0]
            xm_free = [0, 0]
            bc_free = 0
            acum_free = 0
            hsdtT_cons = 0
            yo_evac = 0
            st_upd = 0
            sel_free = 0
            acc_free = 0
            tr_free = [0, 0]
            out_dma = 0
            for c in range(NCH):
                csl = slice(c * CH, (c + 1) * CH)
                wl = [(te, acum_free)]
                if c == 0:
                    wl.append((ve, consts3))
                av = p.v(lambda e, csl=csl: e.tensor_scalar(ab_c[:, :], dt_sb[:, csl], acol[:, :],
                                                            None, ALU.mult), w=wl)
                scan_done = p.v(lambda e: e.tensor_tensor_scan(acum[:, :], ones64[:, :], ab_c[:, :],
                                                               0.0, ALU.mult, ALU.add))
                sc1 = p.s(lambda e: e.activation(eexp[:, :], acum[:, :], AF.Exp), w=[(ve, scan_done)])
                dv1 = p.v(lambda e: e.tensor_scalar(decayc[:, :], acum[:, :],
                                                    acum[:, 255:256], None, ALU.subtract))
                sc2 = p.s(lambda e: e.activation(decayc[:, :], decayc[:, :], AF.Exp, scale=-1.0),
                          w=[(ve, dv1)])
                for t in range(2):
                    tq = p.t(lambda e, t=t: e.transpose(PS_trA[:, 0:64],
                                                        acum[0:64, t * 128:(t + 1) * 128],
                                                        ident[0:64, 0:64]),
                             w=[(ve, tr_free[0]), (ve, scan_done)], inc=True)
                    tr_free[0] = p.v(lambda e, t=t: e.tensor_copy(acT[:, t * 64:(t + 1) * 64],
                                                                  PS_trA[:, 0:64]),
                                     w=[(te, tq)])
                    tq = p.t(lambda e, t=t: e.transpose(PS_trB[:, 0:64],
                                                        decayc[0:64, t * 128:(t + 1) * 128],
                                                        ident[0:64, 0:64]),
                             w=[(ve, tr_free[1]), (se, sc2)], inc=True)
                    tr_free[1] = p.v(lambda e, t=t: e.tensor_copy(dcT[:, t * 64:(t + 1) * 64],
                                                                  PS_trB[:, 0:64]),
                                     w=[(te, tq)])
                acT_ok = p.v(lambda e: e.tensor_scalar(nacT[:, :], acT[:, :], -1.0, None, ALU.mult))
                dgv = p.v(lambda e: e.tensor_scalar(diag3[0:64, 768:832], ident[0:64, 0:64],
                                                    eexp[:, 255:256], None, ALU.mult),
                          w=[(se, sc1), (te, diag_cons)])
                tcd = p.t(lambda e: e.matmul(PS_acc[:, 256:320], tmpones[0:64, :],
                                             diag3[0:64, 768:832], start=True, stop=True),
                          w=[(ve, dgv), (ve, acc_free)], inc=True)
                diag_cons = tcd
                cdec_done = p.v(lambda e: e.tensor_copy(cdecbc[:, :], PS_acc[:, 256:320]),
                                w=[(te, tcd)])

                def emit_dg(hx):
                    for t in range(2):
                        sl = (hx % 3) * 2 + t
                        dgv_done[sl] = p.v(lambda e, hx=hx, t=t, sl=sl: e.tensor_scalar(
                            diag3[:, sl * 128:(sl + 1) * 128], ident[:, :],
                            acT[:, t * 64 + hx: t * 64 + hx + 1], None, ALU.mult),
                            w=[(te, dg6_cons[sl])])
                emit_dg(0)
                emit_dg(1)
                ldb = p.y(lambda e, c=c: e.dma_start(
                    out=b_ch[:, :].rearrange("p (g l) -> p g l", g=8),
                    in_=d_hbcc[INTER:INTER + 1024, c * CH:(c + 1) * CH]
                        .rearrange("(g p) l -> p g l", p=128)),
                    w=[(te, bc_free)], inc=dma)
                ldc = p.y(lambda e, c=c: e.dma_start(
                    out=c_ch[:, :].rearrange("p (g l) -> p g l", g=8),
                    in_=d_hbcc[INTER + 1024:INTER + 2048, c * CH:(c + 1) * CH]
                        .rearrange("(g p) l -> p g l", p=128)),
                    inc=dma)
                hdv_hist = [0, 0]
                trc_hist = [0, 0]
                for ct in range(32):
                    par = ct % 2
                    psdt = PS_dt if par == 0 else PS_yo
                    pstr3 = PS_trA if par == 0 else PS_trB
                    lh = p.y(lambda e, ct=ct, c=c, par=par: e.dma_start(
                        out=hsr[:, par * 256:(par + 1) * 256],
                        in_=d_hbcc[ct * 128:(ct + 1) * 128, c * CH:(c + 1) * CH]),
                        w=[(ve, hdv_hist[par])], inc=dma)
                    tdt = p.t(lambda e, ct=ct, c=c, psdt=psdt: e.matmul(
                        psdt[:, 0:256], repbank[:, ct * 128:(ct + 1) * 128],
                        dt_sb[0:64, c * CH:(c + 1) * CH],
                        start=True, stop=True),
                        w=[(ve, hdv_hist[par]), (ve, av)], inc=True)
                    hdv = p.v(lambda e, ct=ct, par=par, psdt=psdt: e.tensor_tensor(
                        hsdt_ct[:, par * 256:(par + 1) * 256],
                        hsr[:, par * 256:(par + 1) * 256], psdt[:, 0:256],
                        ALU.mult),
                        w=[(te, tdt), (dma, lh)])
                    hdv_hist[par] = hdv
                    tqs = []
                    for t in range(2):
                        wl0 = [(ve, trc_hist[par]), (ve, hdv), (te, hsdtT_cons)]
                        if ct < 2:
                            wl0.append((ve, sel2_free[1] if par == 0 else yd_evac2[1]))
                        tqs.append(p.t(lambda e, t=t, par=par, pstr3=pstr3: e.transpose(
                            pstr3[:, 128 + t * 128:256 + t * 128],
                            hsdt_ct[:, par * 256 + t * 128: par * 256 + (t + 1) * 128],
                            ident[:, :]),
                            w=(wl0 if t == 0 else []), inc=True))
                    cps = None
                    for t in range(2):
                        cps = p.v(lambda e, ct=ct, t=t, pstr3=pstr3: e.tensor_copy(
                            hsdtT[:, t * INTER + ct * 128: t * INTER + (ct + 1) * 128],
                            pstr3[:, 128 + t * 128:256 + t * 128]),
                            w=[(te, tqs[1])])
                    trc_hist[par] = cps
                hsdtT_done = p.v(lambda e: e.nop())

                yf_lasts = [0] * 32
                last_states_mm = 0
                last_yoff_mm = 0
                last_ebig_mm = 0
                gv_done = 0
                pair_idx = 0
                for g in range(8):
                    wl = [(ve, st_upd), (dma, ldc)]
                    tg_ = None
                    for t in range(2):
                        tg_ = p.t(lambda e, g=g, t=t: e.matmul(
                            PS_G[:, t * 256:(t + 1) * 256],
                            b_ch[:, g * 256 + t * 128: g * 256 + (t + 1) * 128],
                            c_ch[:, g * 256:(g + 1) * 256], start=True, stop=True),
                            w=(wl if t == 0 else []), inc=True)
                    gv_done = p.v(lambda e: e.tensor_copy(gsb[:, :], PS_G[:, :]),
                                  w=[(te, tg_)])
                    tq2 = None
                    for t in range(2):
                        tq2 = p.t(lambda e, g=g, t=t: e.transpose(
                            PS_G[:, t * 128:(t + 1) * 128],
                            b_ch[:, g * 256 + t * 128: g * 256 + (t + 1) * 128],
                            ident[:, :]),
                            w=[(ve, gv_done)] if t == 0 else [], inc=True)
                    bt_done = None
                    for t in range(2):
                        bt_done = p.v(lambda e, t=t: e.tensor_copy(
                            btsb[:, t * 128:(t + 1) * 128], PS_G[:, t * 128:(t + 1) * 128]),
                            w=[(te, tq2)])

                    def emit_lchain(h):
                        hp = h % 2
                        ps_ar = PS_sel if hp == 0 else PS_trA
                        ta = None
                        if h + 2 < 64:
                            emit_dg(h + 2)
                        for t in range(2):
                            sl = (h % 3) * 2 + t
                            ta = p.t(lambda e, t=t, hp=hp, sl=sl, ps_ar=ps_ar: e.matmul(
                                ps_ar[:, t * 128:(t + 1) * 128], tmpones[:, :],
                                diag3[:, sl * 128:(sl + 1) * 128],
                                start=True, stop=True),
                                w=[(ve, dgv_done[sl])] + ([(ve, sel2_free[hp])] if t == 0 else []),
                                inc=True)
                            dg6_cons[sl] = ta
                        p.v(lambda e, h=h, hp=hp: e.tensor_scalar(
                            bdst[:, hp * 256: hp * 256 + 128], btsb[:, 0:128],
                            dcT[:, 0 * 64 + h: 0 * 64 + h + 1], None, ALU.mult),
                            w=[(te, st2_cons[hp])] + ([(ve, bt_done)] if h % 8 == 0 else []))
                        bd2_done[hp] = p.v(lambda e, h=h, hp=hp: e.tensor_scalar(
                            bdst[:, hp * 256 + 128: hp * 256 + 256], btsb[:, 128:256],
                            dcT[:, 1 * 64 + h: 1 * 64 + h + 1], None, ALU.mult))
                        mt_done = 0
                        for t in range(2):
                            msk = mask0 if t == 0 else mask1
                            dm = p.v(lambda e, h=h, t=t, msk=msk, hp=hp, ps_ar=ps_ar:
                                     e.scalar_tensor_tensor(
                                dmst[:, hp * 512 + t * 256: hp * 512 + (t + 1) * 256],
                                ps_ar[:, 0:256],
                                nacT[:, t * 64 + h: t * 64 + h + 1],
                                msk[:, :], ALU.add, ALU.add),
                                w=[(te, ta), (se, lt2_last[hp])])
                            sel2_free[hp] = dm
                            lt_ = p.s(lambda e, t=t, hp=hp: e.activation(
                                ltsb[:, hp * 512 + t * 256: hp * 512 + (t + 1) * 256],
                                dmst[:, hp * 512 + t * 256: hp * 512 + (t + 1) * 256],
                                AF.Exp), w=[(ve, dm)])
                            lt2_last[hp] = lt_
                            mt_done = p.v(lambda e, t=t, hp=hp, g=h // 8: e.tensor_tensor(
                                mtsb[:, hp * 512 + t * 256: hp * 512 + (t + 1) * 256],
                                ltsb[:, hp * 512 + t * 256: hp * 512 + (t + 1) * 256],
                                gsb[:, t * 256:(t + 1) * 256], ALU.mult),
                                w=[(se, lt_), (ve, gv_done), (te, yd2_last[hp])])
                        mt2_done[hp] = mt_done

                    def emit_consume(h):
                        nonlocal last_states_mm, last_yoff_mm, last_ebig_mm, yo_evac
                        nonlocal sgate_free
                        nonlocal pair_idx, cdec_done
                        g = h // 8
                        hh = h % 8
                        hp = h % 2
                        pp = (h // 2) % 2
                        ps_ydx = PS_yd if pp == 0 else PS_trB
                        for t in range(2):
                            yd2_last[hp] = p.t(lambda e, h=h, t=t, hh=hh, hp=hp, ps_ydx=ps_ydx:
                                               e.matmul(
                                ps_ydx[(hh % 2) * 64:(hh % 2) * 64 + 64, 0:256],
                                hsdtT[:, t * INTER + h * 64:t * INTER + (h + 1) * 64],
                                mtsb[:, hp * 512 + t * 256: hp * 512 + (t + 1) * 256],
                                start=(t == 0), stop=(t == 1)),
                                w=([(ve, mt2_done[hp])] if t == 0 else []) +
                                  ([(ve, yd_evac2[pp])] if (hh % 2 == 0 and t == 0) else []),
                                inc=True)
                        for t in range(2):
                            last_states_mm = p.t(lambda e, h=h, t=t, hh=hh, hp=hp: e.matmul(
                                PS_G[:, hh * 64:(hh + 1) * 64],
                                bdst[:, hp * 256 + t * 128: hp * 256 + (t + 1) * 128],
                                hsdtT[:, t * INTER + h * 64:t * INTER + (h + 1) * 64],
                                start=(hh == 0 and t == 0), stop=(hh == 7 and t == 1)),
                                w=([(ve, bd2_done[hp])] if t == 0 else []), inc=True)
                        st2_cons[hp] = last_states_mm

                        if hh % 2 == 1:
                            ct = g * 4 + hh // 2
                            last_yoff_mm = p.t(lambda e, ct=ct, g=g: e.matmul(
                                PS_yo[:, 0:256], s_run[:, ct * 128:(ct + 1) * 128],
                                c_ch[:, g * 256:(g + 1) * 256], start=True, stop=True),
                                w=[(ve, yo_evac)] + ([(ve, hsdtT_done)] if ct == 0 else []),
                                inc=True)
                            last_ebig_mm = p.t(lambda e, ct=ct, c=c: e.matmul(
                                PS_dt[:, 256:512], repbank[:, ct * 128:(ct + 1) * 128],
                                eexp[:, :], start=True, stop=True),
                                w=[(ve, yo_evac)] + ([(ve, hsdtT_done)] if ct == 0 else []),
                                inc=True)
                            ev1 = p.v(lambda e: e.tensor_copy(ebsb[:, :], PS_dt[:, 256:512]),
                                      w=[(te, last_ebig_mm)])
                            yo_evac = p.v(lambda e: e.tensor_tensor(yofe[:, :], PS_yo[:, 0:256],
                                                                    ebsb[:, :], ALU.mult),
                                          w=[(te, last_yoff_mm), (te, sqrd_done1)])
                            gring = ct % 2
                            lg = p.y(lambda e, ct=ct, c=c, gring=gring: e.dma_start(
                                out=gate_ct[:, gring * 256:(gring + 1) * 256],
                                in_=d_gate[ct * 128:(ct + 1) * 128, c * CH:(c + 1) * CH]),
                                w=[(ve, gate_free[gring])], inc=dma)
                            eg = p.s(lambda e, gring=gring: e.activation(
                                sgate[:, :], gate_ct[:, gring * 256:(gring + 1) * 256], AF.Exp,
                                scale=-1.0),
                                w=[(dma, lg), (ve, sgate_free)])
                            p.v(lambda e: e.tensor_scalar(sgate[:, :], sgate[:, :], 1.0, None,
                                                          ALU.add), w=[(se, eg)])
                            p.v(lambda e: e.reciprocal(sgate[:, :], sgate[:, :]))
                            gate_free[gring] = p.v(lambda e, gring=gring: e.tensor_tensor(
                                sgate[:, :], gate_ct[:, gring * 256:(gring + 1) * 256],
                                sgate[:, :], ALU.mult))
                            lhs2 = p.y(lambda e, ct=ct, c=c, gring=gring: e.dma_start(
                                out=hse[:, gring * 256:(gring + 1) * 256],
                                in_=d_hbcc[ct * 128:(ct + 1) * 128, c * CH:(c + 1) * CH]),
                                w=[(ve, hse_free[gring])], inc=dma)
                            t1 = p.v(lambda e, ct=ct, ps_ydx=ps_ydx, gring=gring:
                                     e.scalar_tensor_tensor(
                                ysb[:, :], hse[:, gring * 256:(gring + 1) * 256],
                                ddrepc[:, ct:ct + 1], ps_ydx[:, 0:256], ALU.mult, ALU.add),
                                w=[(te, te.c), (te, sqrd_done0), (dma, lhs2)])
                            yd_evac2[pp] = t1
                            hse_free[gring] = t1
                            p.v(lambda e: e.tensor_tensor(ysb[:, :], ysb[:, :], yofe[:, :],
                                                          ALU.add))
                            yf = p.v(lambda e, ct=ct: e.tensor_tensor(
                                yfall[:, ct * 256:(ct + 1) * 256], ysb[:, :], sgate[:, :],
                                ALU.mult))
                            yf_lasts[ct] = yf
                            sgate_free = yf
                            pair_idx += 1

                    for hh in range(8):
                        emit_lchain(g * 8 + hh)
                        if hh > 0:
                            emit_consume(g * 8 + hh - 1)
                    emit_consume(g * 8 + 7)
                    for hh in range(8):
                        h = g * 8 + hh
                        st_upd = p.v(lambda e, h=h, hh=hh: e.scalar_tensor_tensor(
                            s_run[:, h * 64:(h + 1) * 64], s_run[:, h * 64:(h + 1) * 64],
                            cdecbc[:, (h % 64):(h % 64) + 1], PS_G[:, hh * 64:(hh + 1) * 64],
                            ALU.mult, ALU.add),
                            w=[(te, last_states_mm), (ve, cdec_done)] if hh == 0 else
                              [(te, last_states_mm)])
                bc_free = last_yoff_mm
                acum_free = last_ebig_mm
                hsdtT_cons = last_states_mm

                sq_reads = [ysb, yofe]
                sqrd_done = [sqrd_done0, sqrd_done1]
                for ct in range(32):
                    buf = sq_reads[ct % 2]
                    wl5 = [(ve, yf_lasts[ct]), (te, sqrd_done[ct % 2])]
                    if ct < 2:
                        wl5.append((ve, yf_lasts[31]))
                    sq = p.s(lambda e, ct=ct, buf=buf: e.activation(
                        buf[:, :], yfall[:, ct * 256:(ct + 1) * 256], AF.Square),
                        w=wl5)
                    sqrd_done[ct % 2] = sq_mm = p.t(lambda e, ct=ct, buf=buf: e.matmul(
                        PS_acc[:, 0:256], tmpones[:, :], buf[:, :],
                        start=(ct == 0), stop=(ct == 31)),
                        w=[(se, sq)] + ([(ve, acc_free)] if ct == 0 else []),
                        inc=True)

                sqrd_done0, sqrd_done1 = sqrd_done

                yf_dump = 0
                if d_yf is not None:
                    for ct in range(32):
                        yf_dump = p.y(lambda e, ct=ct, c=c: e.dma_start(
                            out=d_yf[ct * 128:(ct + 1) * 128, c * CH:(c + 1) * CH],
                            in_=yfall[:, ct * 256:(ct + 1) * 256]),
                            w=[(ve, ve.c)], inc=dms)
                sumsq_mm = te.c
                rv = p.v(lambda e: e.tensor_scalar(rstdb[:, :], PS_acc[:, 0:256],
                                                   1.0 / INTER, 1e-6, ALU.mult, ALU.add),
                         w=[(te, sumsq_mm)])
                acc_free = rv
                p.v(lambda e: e.reciprocal(rstdb[:, :], rstdb[:, :]))
                sqr = p.s(lambda e: e.activation(rstdb[:, :], rstdb[:, :], AF.Sqrt),
                          w=[(ve, ve.c)])
                yn_done = 0
                for ct in range(32):
                    wl4 = [(se, sqr)] if ct == 0 else []
                    if d_yf is not None and ct == 0:
                        wl4.append((dms, yf_dump))
                    yn_done = p.v(lambda e, ct=ct: e.scalar_tensor_tensor(
                        yfall[:, ct * 256:(ct + 1) * 256], yfall[:, ct * 256:(ct + 1) * 256],
                        nw_c[:, ct:ct + 1], rstdb[:, :], ALU.mult, ALU.mult),
                        w=wl4)

                yn_dump = 0
                if d_yn is not None:
                    for ct in range(32):
                        yn_dump = p.y(lambda e, ct=ct, c=c: e.dma_start(
                            out=d_yn[ct * 128:(ct + 1) * 128, c * CH:(c + 1) * CH],
                            in_=yfall[:, ct * 256:(ct + 1) * 256]),
                            w=[(ve, yn_done)], inc=dms)
                    yn_done = p.v(lambda e: e.nop(), w=[(dms, yn_dump)])
                if d_rs is not None:
                    p.y(lambda e, c=c: e.dma_start(
                        out=d_rs[:, c * CH:(c + 1) * CH], in_=rstdb[:, :]),
                        w=[(ve, yn_done)], inc=dms)

                op_banks = [PS_sel, PS_yo]
                op_evac2 = [sel2_free[0], yo_evac]
                wop_bufs = [wopst, wopst2]
                wop_cons = [0, 0]
                for m in range(16):
                    psop = op_banks[m % 2]
                    wop = wop_bufs[m % 2]
                    wts2 = []
                    for half3 in range(2):
                        wts2.append(p.dma("scalar", lambda e, m=m, half3=half3, wop=wop: e.dma_start(
                            out=wop[:, half3 * 16 * 128:(half3 + 1) * 16 * 128]
                                .rearrange("p (k c) -> p k c", k=16),
                            in_=d_wout[half3 * 2048:(half3 + 1) * 2048, m * 128:(m + 1) * 128]
                                .rearrange("(k p) c -> p k c", p=128)),
                            w=[(te, wop_cons[m % 2])] if half3 == 0 else [], dsem=dmg))
                    lgm = p.y(lambda e, m=m, c=c: e.dma_start(
                        out=g_m[:, :],
                        in_=d_g[m * 128:(m + 1) * 128, c * CH:(c + 1) * CH]),
                        w=[(ve, gm_free[0])], inc=dma)
                    lxm = p.y(lambda e, m=m, c=c: e.dma_start(
                        out=x_m[:, :].rearrange("p (lb cc) -> p lb cc", lb=2),
                        in_=d_x[c * CH:(c + 1) * CH, m * 128:(m + 1) * 128]
                            .rearrange("(lb p) cc -> p lb cc", p=128)),
                        w=[(ve, xm_free[0])], inc=dma)
                    mmop = None
                    for kt in range(32):
                        wl3 = []
                        if kt == 0:
                            wl3 = [(dmg, wts2[0]), (ve, op_evac2[m % 2]), (ve, yn_done)]
                        elif kt == 16:
                            wl3 = [(dmg, wts2[1])]
                        mmop = p.t(lambda e, kt=kt, m=m, psop=psop, wop=wop: e.matmul(
                            psop[:, 0:256], wop[:, kt * 128:(kt + 1) * 128],
                            yfall[:, kt * 256:(kt + 1) * 256],
                            start=(kt == 0), stop=(kt == 31)),
                            w=wl3, inc=True)
                    wop_cons[m % 2] = mmop
                    zv = p.v(lambda e, m=m, psop=psop: e.tensor_tensor(
                        zsb[:, :], psop[:, 0:256], g_m[:, :], ALU.mult),
                        w=[(te, mmop), (dma, lgm)])
                    op_evac2[m % 2] = zv
                    gm_free[0] = zv
                    ob_done = 0
                    for lb in range(2):
                        par = lb
                        pstr3 = PS_trA if par == 0 else PS_trB
                        tz = p.t(lambda e, lb=lb, pstr3=pstr3: e.transpose(
                            pstr3[:, 256:384], zsb[:, lb * 128:(lb + 1) * 128], ident[:, :]),
                            w=[(ve, tr_free[par]), (ve, zv)], inc=True)
                        tr_free[par] = p.v(lambda e, lb=lb, pstr3=pstr3: e.tensor_tensor(
                            osb[:, lb * 128:(lb + 1) * 128], pstr3[:, 256:384],
                            x_m[:, lb * 128:(lb + 1) * 128], ALU.add),
                            w=[(te, tz), (dma, lxm), (dms, out_dma)])
                        ob_done = tr_free[par]
                    xm_free[0] = ob_done
                    out_dma = p.y(lambda e, m=m, c=c: e.dma_start(
                        out=d_out[c * CH:(c + 1) * CH, m * 128:(m + 1) * 128]
                            .rearrange("(lb p) cc -> p lb cc", p=128),
                        in_=osb[:, :].rearrange("p (lb cc) -> p lb cc", lb=2)),
                        w=[(ve, ob_done)], inc=dms)
                sel2_free[0] = op_evac2[0]
                yo_evac = op_evac2[1]

        p.y(lambda e: e.nop(), w=[(dms, dms.c), (dma, dma.c), (dmw, dmw.c), (dmx, dmx.c), (dmg, dmg.c), (dss, dss.c), (dmq, dmq.c)], inc=None)

        with nc.Block() as block:
            block.tensor(p.replay("tensor"))
            block.vector(p.replay("vector"))
            block.scalar(p.replay("scalar"))
            block.gpsimd(p.replay("gpsimd"))
            block.sync(p.replay("sync"))

    return nc


def _shard_inputs(inputs):
    x = np.ascontiguousarray(inputs["L0_attn_input_states"], dtype=np.float32)
    common = {
        "w_in": np.ascontiguousarray(inputs["L0_attn_in_proj_w"], dtype=np.float32),
        "conv_w": np.ascontiguousarray(np.asarray(inputs["L0_attn_conv_w"]).reshape(4, CONVD), dtype=np.float32),
        "conv_b": np.ascontiguousarray(inputs["L0_attn_conv_b"], dtype=np.float32),
        "dt_bias": np.ascontiguousarray(inputs["L0_attn_dt_bias"], dtype=np.float32),
        "a_log": np.ascontiguousarray(inputs["L0_attn_A_log"], dtype=np.float32),
        "d_d": np.ascontiguousarray(inputs["L0_attn_D"], dtype=np.float32),
        "norm_w": np.ascontiguousarray(inputs["L0_attn_norm_w"], dtype=np.float32),
        "w_out": np.ascontiguousarray(inputs["L0_attn_out_proj_w"], dtype=np.float32),
        "w1": np.ascontiguousarray(inputs["L0_ffn_w_gate_up"], dtype=np.float32),
        "w2": np.ascontiguousarray(inputs["L0_ffn_w_down"], dtype=np.float32),
        "d_rep": np.ascontiguousarray(np.repeat(np.asarray(inputs["L0_attn_D"], dtype=np.float32), HD)),
    }
    in_maps = []
    for c in range(8):
        m = dict(common)
        m["x"] = np.ascontiguousarray(x[c])
        in_maps.append(m)
    return in_maps


def kernel(**inputs):
    in_maps = _shard_inputs(inputs)
    nc = build()
    res = run_bass_kernel_spmd(nc, in_maps, core_ids=list(range(8)))
    return np.stack([r["out"] for r in res.results]).astype(np.float32)


# version 09 (final)
# speedup vs baseline: 1.0436x; 1.0259x over previous
import contextlib
import numpy as np
import concourse.bass as bass
import concourse.mybir as mybir
from concourse.bass_utils import run_bass_kernel_spmd

F32 = mybir.dt.float32
BF16 = mybir.dt.bfloat16
I32 = mybir.dt.int32
AF = mybir.ActivationFunctionType
ALU = mybir.AluOpType
AXX = mybir.AxisListType.X

S = 2048
HID = 2048
PROJ = 10304
INTER = 4096
CONVD = 6144
NH, HD, NG, SS, CH = 64, 64, 8, 128, 256
NCH = S // CH
I_FFN = 5632
O_FFN = 11264
KT = HID // 128
NEG = -1.0e30
R127 = float(np.float32(1.0 / 127.0))

ENGS = ("tensor", "vector", "scalar", "gpsimd", "sync")


class Sem:
    __slots__ = ("h", "c")

    def __init__(self, h):
        self.h = h
        self.c = 0


class Prog:

    def __init__(self, nc, auto_sems):
        self.nc = nc
        self.ops = {e: [] for e in ENGS}
        self.auto = auto_sems

    def _emit(self, eng, fn, w, extra_inc):
        self.ops[eng].append((tuple((s.h, v) for s, v in w), fn, extra_inc))

    def t(self, fn, w=(), inc=False):
        sem = self.auto["tensor"]
        if inc:
            sem.c += 1
            self._emit("tensor", fn, w, (sem.h, 1))
            return sem.c
        self._emit("tensor", fn, w, None)
        return None

    def _auto(self, eng, fn, w):
        sem = self.auto[eng]
        waits = ((sem.h, sem.c),) + tuple((s.h, v) for s, v in w)
        sem.c += 1
        self.ops[eng].append((waits, fn, (sem.h, 1)))
        return sem.c

    def v(self, fn, w=()):
        return self._auto("vector", fn, w)

    def s(self, fn, w=()):
        return self._auto("scalar", fn, w)

    def g(self, fn, w=()):
        return self._auto("gpsimd", fn, w)

    def dma(self, eng, fn, w=(), dsem=None, n=16):
        dsem.c += n
        self._emit(eng, fn, w, (dsem.h, n))
        return dsem.c

    def y(self, fn, w=(), inc=None, n=16):
        if inc is None:
            self._emit("sync", fn, w, None)
            return None
        return self.dma("sync", fn, w, inc, n)

    def barrier(self, bar, dma_sems=()):
        for e in ("vector", "scalar", "gpsimd", "tensor"):
            sem = self.auto[e]
            self._emit(e, lambda en: en.nop(), ((sem, sem.c),), (bar.h, 1))
        self._emit("sync", lambda en: en.nop(),
                   tuple((s, s.c) for s in dma_sems), (bar.h, 1))
        bar.c += 5
        t = bar.c
        for e in ENGS:
            self._emit(e, lambda en: en.nop(), ((bar, t),), None)

    def replay(self, eng):
        lst = self.ops[eng]

        def f(e):
            for waits, fn, inc in lst:
                for (h, v) in waits:
                    e.wait_ge(h, v)
                inst = fn(e)
                if inc is not None:
                    inst.then_inc(inc[0], inc[1])
        return f


def build(dumps=()):
    nc = bass.Bass("TRN2", target_bir_lowering=False, detect_race_conditions=False)

    def scratch(name, shape, dt):
        kind = "ExternalOutput" if name in dumps else "Internal"
        return nc.dram_tensor(name, shape, dt, kind=kind).ap()

    d_x = nc.dram_tensor("x", [S, HID], F32, kind="ExternalInput").ap()
    d_win = nc.dram_tensor("w_in", [HID, PROJ], F32, kind="ExternalInput").ap()
    d_convw = nc.dram_tensor("conv_w", [4, CONVD], F32, kind="ExternalInput").ap()
    d_convb = nc.dram_tensor("conv_b", [CONVD], F32, kind="ExternalInput").ap()
    d_dtb = nc.dram_tensor("dt_bias", [NH], F32, kind="ExternalInput").ap()
    d_alog = nc.dram_tensor("a_log", [NH], F32, kind="ExternalInput").ap()
    d_dd = nc.dram_tensor("d_d", [NH], F32, kind="ExternalInput").ap()
    d_nw = nc.dram_tensor("norm_w", [INTER], F32, kind="ExternalInput").ap()
    d_wout = nc.dram_tensor("w_out", [INTER, HID], F32, kind="ExternalInput").ap()
    d_w1 = nc.dram_tensor("w1", [O_FFN, HID], F32, kind="ExternalInput").ap()
    d_w2 = nc.dram_tensor("w2", [HID, I_FFN], F32, kind="ExternalInput").ap()
    d_drep = nc.dram_tensor("d_rep", [INTER], F32, kind="ExternalInput").ap()

    d_out = nc.dram_tensor("out", [S, HID], F32, kind="ExternalOutput").ap()

    d_xn = scratch("xn_d", [S, HID], F32) if "xn_d" in dumps else None
    d_xqT = scratch("xqT_d", [HID, S], BF16)
    d_gate = scratch("gate_d", [INTER, S], F32)
    d_hbcc = scratch("hbcc_d", [CONVD, S], F32)
    d_dtdump = scratch("dt_d", [NH, S], F32) if "dt_d" in dumps else None
    d_qw1 = scratch("qw1_d", [O_FFN, HID], BF16)
    d_qw2 = scratch("qw2_d", [HID, I_FFN], BF16)
    d_w1t = scratch("w1t_d", [HID, O_FFN], BF16)
    d_w2t = scratch("w2t_d", [I_FFN, HID], BF16)
    d_h = scratch("h_d", [I_FFN, S], F32)
    d_yf = scratch("yf_d", [INTER, S], F32) if "yf_d" in dumps else None
    d_yn = scratch("yn_d", [INTER, S], F32) if "yn_d" in dumps else None
    d_rs = scratch("rs_d", [128, S], F32) if "rs_d" in dumps else None
    d_dbg2 = scratch("dbg2_d", [384, 1024], F32) if "dbg2_d" in dumps else None
    d_g = scratch("g_d", [HID, S], F32)

    with contextlib.ExitStack() as stk:
        E = stk.enter_context

        sm = {k: Sem(E(nc.semaphore(k))) for k in
              ("bar", "dma", "dmx", "dmw", "dmg", "dms", "dss", "dmq", "te", "ve", "se", "ge")}
        bar, dma, dmx, dmw, dms = sm["bar"], sm["dma"], sm["dmx"], sm["dmw"], sm["dms"]
        dmg = sm["dmg"]
        dss, dmq = sm["dss"], sm["dmq"]
        te, ve, se, ge = sm["te"], sm["ve"], sm["se"], sm["ge"]
        p = Prog(nc, {"tensor": te, "vector": ve, "scalar": se, "gpsimd": ge})

        ident = E(nc.sbuf_tensor("ident", [128, 128], F32)).ap()
        identb = E(nc.sbuf_tensor("identb", [128, 128], BF16)).ap()
        convw_c = E(nc.sbuf_tensor("convw_c", [128, 4 * 48], F32)).ap()
        convb_c = E(nc.sbuf_tensor("convb_c", [128, 48], F32)).ap()
        dtb_c = E(nc.sbuf_tensor("dtb_c", [NH, 1], F32)).ap()
        alog_c = E(nc.sbuf_tensor("alog_c", [NH, 1], F32)).ap()
        acol = E(nc.sbuf_tensor("acol", [NH, 1], F32)).ap()
        dd_c = E(nc.sbuf_tensor("dd_c", [NH, 1], F32)).ap()
        nw_c = E(nc.sbuf_tensor("nw_c", [128, 32], F32)).ap()
        sxcols = E(nc.sbuf_tensor("sxcols", [128, 16], F32)).ap()
        sw1c = E(nc.sbuf_tensor("sw1c", [128, 88], F32)).ap()
        sw2c = E(nc.sbuf_tensor("sw2c", [128, 16], F32)).ap()
        shcols = E(nc.sbuf_tensor("shcols", [128, 16], F32)).ap()
        shrcp = E(nc.sbuf_tensor("shrcp", [128, 16], F32)).ap()
        dt_sb = E(nc.sbuf_tensor("dt_sb", [NH, S], F32)).ap()

        def ld(dst, src, slow=False):
            if slow:
                return lambda e: e.dma_start(out=dst, in_=src, allow_slow_non_contiguous=True)
            return lambda e: e.dma_start(out=dst, in_=src)

        p.y(ld(convw_c[:, :], d_convw.rearrange("j (mt p) -> p (j mt)", p=128), True), inc=dma)
        p.y(ld(convb_c[:, :], d_convb.rearrange("(mt p) -> p mt", p=128), True), inc=dma)
        p.y(ld(dtb_c[:, :], d_dtb.rearrange("(h o) -> h o", o=1), True), inc=dma)
        p.y(ld(alog_c[:, :], d_alog.rearrange("(h o) -> h o", o=1), True), inc=dma)
        p.y(ld(dd_c[:, :], d_dd.rearrange("(h o) -> h o", o=1), True), inc=dma)
        p.y(ld(nw_c[:, :], d_nw.rearrange("(mt p) -> p mt", p=128), True), inc=dma)
        v_consts = dma.c

        tmpones = E(nc.sbuf_tensor("tmpones", [128, 128], F32)).ap()
        p.g(lambda e: e.memset(ident[:, :], 0.0))
        p.g(lambda e: e.memset(tmpones[:, :], 1.0))
        g_consts = p.g(lambda e: e.affine_select(ident[:, :], tmpones[:, :], [[1, 128]],
                                                 ALU.is_equal, 0.0, base=0, channel_multiplier=-1))
        identb_done = p.v(lambda e: e.tensor_copy(identb[:, :], ident[:, :]), w=[(ge, g_consts)])

        se_exp = p.s(lambda e: e.activation(acol[:, :], alog_c[:, :], AF.Exp),
                     w=[(dma, v_consts)])
        acol_done = p.v(lambda e: e.tensor_scalar(acol[:, :], acol[:, :], -1.0, None, ALU.mult),
                        w=[(se, se_exp)])

        with contextlib.ExitStack() as ph:
            Eh = ph.enter_context
            xnT = Eh(nc.sbuf_tensor("xnT", [128, KT * S], F32)).ap()
            xin = Eh(nc.sbuf_tensor("xin", [128, HID], F32)).ap()
            xnst = Eh(nc.sbuf_tensor("xnst", [128, HID], F32)).ap()
            xqst = Eh(nc.sbuf_tensor("xqst", [128, HID], BF16)).ap()
            qtmp = Eh(nc.sbuf_tensor("qtmp", [128, HID], I32)).ap()
            f32a = Eh(nc.sbuf_tensor("f32a", [128, HID], F32)).ap()
            f32b = Eh(nc.sbuf_tensor("f32b", [128, HID], F32)).ap()
            f32c = xin
            mcol = Eh(nc.sbuf_tensor("mcol", [128, 1], F32)).ap()
            vcol = Eh(nc.sbuf_tensor("vcol", [128, 1], F32)).ap()
            rcol = Eh(nc.sbuf_tensor("rcol", [128, 1], F32)).ap()
            rcol2 = Eh(nc.sbuf_tensor("rcol2", [128, 1], F32)).ap()
            tb_st = Eh(nc.sbuf_tensor("tb_st", [128, 128], BF16)).ap()
            p0ps = contextlib.ExitStack()
            ps_tr = p0ps.enter_context(nc.psum_tensor("ps_tr", [128, 128], F32)).ap()
            ps_trb = p0ps.enter_context(nc.psum_tensor("ps_trb", [128, 128], BF16)).ap()

            trans_done = 0
            prev_tb_dma = 0
            xq_done = 0
            for i in range(16):
                ldtgt = p.y(ld(xin[:, :], d_x[i * 128:(i + 1) * 128, :]),
                            w=[(ve, xq_done)], inc=dmx)
                p.v(lambda e: e.tensor_reduce(mcol[:, :], xin[:, :], AXX, ALU.add),
                    w=[(dmx, ldtgt)])
                p.v(lambda e: e.tensor_scalar(mcol[:, :], mcol[:, :], 1.0 / HID, None, ALU.mult))
                p.v(lambda e: e.tensor_scalar(f32a[:, :], xin[:, :], mcol[:, :], None, ALU.subtract))
                vq = p.v(lambda e: e.activation(None, None, None) if False else e.engine_nop())
                sq = p.s(lambda e: e.activation(f32b[:, :], f32a[:, :], AF.Square), w=[(ve, vq)])
                p.v(lambda e: e.tensor_reduce(vcol[:, :], f32b[:, :], AXX, ALU.add),
                    w=[(se, sq)])
                p.v(lambda e: e.tensor_scalar(vcol[:, :], vcol[:, :], 1.0 / HID, 1e-5, ALU.mult, ALU.add))
                vr = p.v(lambda e: e.reciprocal(rcol[:, :], vcol[:, :]))
                sr = p.s(lambda e: e.activation(rcol2[:, :], rcol[:, :], AF.Sqrt), w=[(ve, vr)])
                xn_done = p.v(lambda e: e.tensor_scalar(xnst[:, :], f32a[:, :], rcol2[:, :], None,
                                                        ALU.mult),
                              w=[(se, sr), (te, trans_done)])
                if d_xn is not None:
                    p.y(lambda e, i=i: e.dma_start(out=d_xn[i * 128:(i + 1) * 128, :], in_=xnst[:, :]),
                        w=[(ve, xn_done)], inc=dms)

                p.v(lambda e: e.tensor_reduce(rcol[:, :], xnst[:, :], AXX, ALU.max,
                                              apply_absolute_value=True))
                p.v(lambda e: e.tensor_scalar(rcol2[:, :], rcol[:, :], R127, None, ALU.mult))
                p.v(lambda e: e.scalar_tensor_tensor(vcol[:, :], rcol2[:, :], -127.0, rcol[:, :],
                                                     ALU.mult, ALU.add))
                p.v(lambda e, i=i: e.scalar_tensor_tensor(sxcols[:, i:i + 1], vcol[:, :], R127,
                                                          rcol2[:, :], ALU.mult, ALU.add))
                p.v(lambda e, i=i: e.tensor_scalar(sxcols[:, i:i + 1], sxcols[:, i:i + 1], 1e-8,
                                                   None, ALU.max))
                p.v(lambda e, i=i: e.reciprocal(rcol[:, :], sxcols[:, i:i + 1]))
                p.v(lambda e: e.tensor_scalar(f32a[:, :], xnst[:, :], rcol[:, :], 127.0,
                                              ALU.mult, ALU.min))
                p.v(lambda e: e.tensor_scalar(qtmp[:, :], f32a[:, :], -127.0, None, ALU.max))
                p.v(lambda e: e.scalar_tensor_tensor(f32b[:, :], f32a[:, :], -127.0, qtmp[:, :],
                                                     ALU.max, ALU.subtract))
                p.v(lambda e: e.tensor_scalar(f32a[:, :], f32b[:, :], 0.5, None, ALU.is_gt))
                p.v(lambda e: e.tensor_scalar(f32c[:, :], f32b[:, :], -0.5, None, ALU.is_lt))
                p.v(lambda e: e.scalar_tensor_tensor(f32b[:, :], f32a[:, :], 1.0, qtmp[:, :],
                                                     ALU.mult, ALU.add))
                xq_done = p.v(lambda e: e.tensor_tensor(xqst[:, :], f32b[:, :], f32c[:, :],
                                                        ALU.subtract))

                for k in range(KT):
                    w0 = [(ve, xn_done)] if k == 0 else []
                    if i == 0 and k == 0:
                        w0.append((ge, g_consts))
                    t1 = p.t(lambda e, k=k: e.transpose(ps_tr[:, :], xnst[:, k * 128:(k + 1) * 128],
                                                        ident[:, :]),
                             w=w0, inc=True)
                    c1 = p.v(lambda e, k=k, i=i: e.tensor_copy(
                        xnT[:, k * S + i * 128: k * S + (i + 1) * 128], ps_tr[:, :]),
                        w=[(te, t1)])
                    w1l = [(ve, xq_done)] if k == 0 else []
                    if i == 0 and k == 0:
                        w1l.append((ve, identb_done))
                    t2 = p.t(lambda e, k=k: e.transpose(ps_trb[:, :], xqst[:, k * 128:(k + 1) * 128],
                                                        identb[:, :]),
                             w=w1l, inc=True)
                    c2 = p.v(lambda e: e.tensor_copy(tb_st[:, :], ps_trb[:, :]),
                             w=[(te, t2), (dss, prev_tb_dma)])
                    prev_tb_dma = p.dma("scalar", lambda e, k=k, i=i: e.dma_start(
                        out=d_xqT[k * 128:(k + 1) * 128, i * 128:(i + 1) * 128], in_=tb_st[:, :]),
                        w=[(ve, c2)], dsem=dss)
                    trans_done = p.t(lambda e: e.nop(), w=[(ve, c2), (ve, c1)], inc=True)

            p0_done_v = ve.c
            p0ps.close()

            wblk = Eh(nc.sbuf_tensor("wblk", [128, HID], F32)).ap()
            wblk2 = Eh(nc.sbuf_tensor("wblk2", [128, HID], F32)).ap()
            evst = Eh(nc.sbuf_tensor("evst", [128, S + 4], F32)).ap()
            dttmp = evst[0:NH, 0:S]
            evout = f32a
            evout2 = f32b
            ps_mm = Eh(nc.psum_tensor("ps_mm", [128, S], F32)).ap()
            ps_mm2 = Eh(nc.psum_tensor("ps_mm2", [128, S], F32)).ap()

            p.v(lambda e: e.memset(evst[:, 0:1], 0.0))
            evst_zeroed = p.v(lambda e: e.memset(evst[:, S + 1:S + 4], 0.0))

            wqa, wqb, wqo = xnst, xin, xqst
            wqi = qtmp
            qc1, qc2, qc3 = mcol, vcol, rcol
            wqst = {"ld_free": 0, "st_free": 0, "qw1_done": 0, "qw2_done": 0}

            def wq_load(src_ap, width, extra=()):
                lt = p.dma("scalar", lambda e: e.dma_start(out=wqa[:, :width], in_=src_ap),
                           w=[(ve, wqst["ld_free"])] + list(extra), dsem=dmq)
                return lt

            def wq_scale_from_qc1(scol):
                p.v(lambda e: e.tensor_scalar(qc2[:, :], qc1[:, :], R127, None, ALU.mult))
                p.v(lambda e: e.scalar_tensor_tensor(qc3[:, :], qc2[:, :], -127.0, qc1[:, :],
                                                     ALU.mult, ALU.add))
                p.v(lambda e: e.scalar_tensor_tensor(scol, qc3[:, :], R127, qc2[:, :],
                                                     ALU.mult, ALU.add))
                p.v(lambda e: e.tensor_scalar(scol, scol, 1e-8, None, ALU.max))
                p.v(lambda e: e.reciprocal(qc3[:, :], scol))

            def wq_quant_sub(dst, width, key):
                p.v(lambda e: e.tensor_scalar(wqb[:, :width], wqa[:, :width], qc3[:, :], 127.0,
                                              ALU.mult, ALU.min))
                p.v(lambda e: e.tensor_scalar(wqi[:, :width], wqb[:, :width],
                                              -127.0, None, ALU.max))
                p.v(lambda e: e.scalar_tensor_tensor(wqa[:, :width], wqb[:, :width], -127.0,
                                                     wqi[:, :width], ALU.max, ALU.subtract))
                p.v(lambda e: e.tensor_scalar(wqb[:, :width], wqa[:, :width], 0.5, None, ALU.is_gt))
                p.v(lambda e: e.tensor_scalar(wqa[:, :width], wqa[:, :width], -0.5, None, ALU.is_lt))
                p.v(lambda e: e.scalar_tensor_tensor(wqb[:, :width], wqb[:, :width], 1.0,
                                                     wqi[:, :width], ALU.mult, ALU.add))
                cq = p.v(lambda e: e.tensor_tensor(wqo[:, :width], wqb[:, :width], wqa[:, :width],
                                                   ALU.subtract), w=[(dss, wqst["st_free"])])
                wqst["ld_free"] = cq
                wqst["st_free"] = p.dma("scalar", lambda e: e.dma_start(out=dst, in_=wqo[:, :width]),
                                        w=[(ve, cq)], dsem=dss)
                wqst[key] = wqst["st_free"]

            def emit_wq_w1(mp, extra=()):
                lt = wq_load(d_w1[mp * 128:(mp + 1) * 128, :], HID, extra)
                p.v(lambda e: e.tensor_reduce(qc1[:, :], wqa[:, :HID], AXX, ALU.max,
                                              apply_absolute_value=True), w=[(dmq, lt)])
                wq_scale_from_qc1(sw1c[:, mp:mp + 1])
                wq_quant_sub(d_qw1[mp * 128:(mp + 1) * 128, :], HID, "qw1_done")

            W2SUB = [(0, 2048), (2048, 2048), (4096, 1536)]

            def emit_wq_w2(m2):
                for si, (off, wd) in enumerate(W2SUB):
                    lt = wq_load(d_w2[m2 * 128:(m2 + 1) * 128, off:off + wd], wd)
                    p.v(lambda e, wd=wd: e.tensor_reduce(qc2[:, :], wqa[:, :wd], AXX, ALU.max,
                                                         apply_absolute_value=True),
                        w=[(dmq, lt)])
                    if si == 0:
                        p.v(lambda e: e.tensor_copy(qc1[:, :], qc2[:, :]))
                    else:
                        p.v(lambda e: e.tensor_tensor(qc1[:, :], qc1[:, :], qc2[:, :], ALU.max))
                    wqst["ld_free"] = ve.c
                wq_scale_from_qc1(sw2c[:, m2:m2 + 1])
                for (off, wd) in W2SUB:
                    lt = wq_load(d_w2[m2 * 128:(m2 + 1) * 128, off:off + wd], wd)
                    p.v(lambda e: e.nop(), w=[(dmq, lt)])
                    wq_quant_sub(d_qw2[m2 * 128:(m2 + 1) * 128, off:off + wd], wd, "qw2_done")

            wq_work = [("w1", mp) for mp in range(O_FFN // 128)] +                       [("w2", m2) for m2 in range(HID // 128)]
            wq_idx = [0]

            def drain_wq(n):
                while n > 0 and wq_idx[0] < len(wq_work):
                    kind, idx = wq_work[wq_idx[0]]
                    extra = [(ve, p0_done_v), (te, trans_done)] if wq_idx[0] == 0 else ()
                    if kind == "w1":
                        emit_wq_w1(idx, extra)
                    else:
                        emit_wq_w2(idx)
                    wq_idx[0] += 1
                    n -= 1

            NM = 81
            ev_done = [0, 0]
            wb_free = [0, 0]
            evout_free = 0
            evout_free_sem = None
            evout2_free = 0
            for m in range(NM):
                mw = 128 if m < 80 else 64
                par = m % 2
                wb = wblk if par == 0 else wblk2
                ps = ps_mm if par == 0 else ps_mm2
                wtgts = []
                for kq in range(4):
                    wl = [(te, wb_free[par])] if kq == 0 else []
                    if m == 0 and kq == 0:
                        wl.append((ve, p0_done_v))
                    wtgts.append(p.y(lambda e, m=m, kq=kq, mw=mw, wb=wb: e.dma_start(
                        out=wb[:, kq * 4 * mw:(kq + 1) * 4 * mw].rearrange("p (k c) -> p k c", k=4),
                        in_=d_win[kq * 512:(kq + 1) * 512, m * 128:m * 128 + mw]
                            .rearrange("(k p) c -> p k c", p=128)),
                        w=wl, inc=dmw))
                last = None
                for k in range(KT):
                    for nn in range(4):
                        wlist = []
                        if nn == 0 and k % 4 == 0:
                            wlist.append((dmw, wtgts[k // 4]))
                        if k == 0 and nn == 0:
                            wlist.append((ve, p0_done_v))
                            wlist.append(ev_done[par] if isinstance(ev_done[par], tuple)
                                         else (ve, ev_done[par]))
                        last = p.t(lambda e, k=k, nn=nn, mw=mw, wb=wb, ps=ps: e.matmul(
                            ps[0:mw, nn * 512:(nn + 1) * 512],
                            wb[:, k * mw:(k + 1) * mw],
                            xnT[:, k * S + nn * 512: k * S + (nn + 1) * 512],
                            start=(k == 0), stop=(k == KT - 1)),
                            w=wlist, inc=True)
                wb_free[par] = last

                if m < 32:
                    wl = [(te, last)]
                    if evout_free_sem is not None:
                        wl.append((evout_free_sem, evout_free))
                    c = p.s(lambda e, ps=ps: e.activation(evout[:, :], ps[:, :], AF.Silu), w=wl)
                    ev_done[par] = (se, c)
                    evout_free = p.dma("scalar", lambda e, m=m: e.dma_start(
                        out=d_gate[m * 128:(m + 1) * 128, :], in_=evout[:, :]),
                        w=[(se, c)], dsem=dss)
                    evout_free_sem = dss
                elif m < 80:
                    mt = m - 32
                    c = p.v(lambda e, ps=ps: e.tensor_copy(evst[:, 1:S + 1], ps[:, :]),
                            w=[(te, last), (ve, evst_zeroed)])
                    ev_done[par] = c
                    wl = []
                    if evout_free_sem is not None:
                        wl.append((evout_free_sem, evout_free))
                    p.v(lambda e, mt=mt: e.tensor_scalar(
                        evout[:, :], evst[:, 0:S], convw_c[:, mt:mt + 1], None, ALU.mult),
                        w=wl)
                    for j in range(1, 4):
                        p.v(lambda e, mt=mt, j=j: e.scalar_tensor_tensor(
                            evout[:, :], evst[:, j:j + S], convw_c[:, j * 48 + mt:j * 48 + mt + 1],
                            evout[:, :], ALU.mult, ALU.add))
                    cv = ve.c
                    evout_free_sem = se
                    sv = p.s(lambda e, mt=mt: e.activation(
                        evout2[:, :], evout[:, :], AF.Silu, bias=convb_c[:, mt:mt + 1]),
                        w=[(ve, cv), (dss, evout2_free)])
                    evout_free = sv
                    evout2_free = p.dma("scalar", lambda e, mt=mt: e.dma_start(
                        out=d_hbcc[mt * 128:(mt + 1) * 128, :], in_=evout2[:, :]),
                        w=[(se, sv)], dsem=dss)
                else:
                    wl = [(te, last)]
                    if evout_free_sem is not None:
                        wl.append((evout_free_sem, evout_free))
                    c = p.v(lambda e, ps=ps: e.tensor_scalar(evout[0:NH, :], ps[0:NH, :],
                                                             dtb_c[:, :], None, ALU.add), w=wl)
                    ev_done[par] = c
                    p.s(lambda e: e.activation(dttmp[:, :], evout[0:NH, :], AF.Abs),
                        w=[(ve, c)])
                    p.s(lambda e: e.activation(dt_sb[:, :], dttmp[:, :], AF.Exp, scale=-1.0))
                    sv = p.s(lambda e: e.activation(dttmp[:, :], dt_sb[:, :], AF.Ln, bias=1.0))
                    dv = p.v(lambda e: e.scalar_tensor_tensor(dt_sb[:, :], evout[0:NH, :], 0.0,
                                                              dttmp[:, :], ALU.max, ALU.add),
                             w=[(se, sv)])
                    if d_dtdump is not None:
                        p.y(lambda e: e.dma_start(out=d_dtdump[:, :], in_=dt_sb[:, :]),
                            w=[(ve, dv)], inc=dms)
                if m >= 2:
                    drain_wq(2)


            drain_wq(len(wq_work))
            qw1_done = wqst["qw1_done"]
            qw2_done = wqst["qw2_done"]

        xqT_written = prev_tb_dma
        p.barrier(bar, dma_sems=(dma, dmx, dmw, dmg, dms, dss, dmq))

        sh_done = None
        with contextlib.ExitStack() as ph:
            Eh = ph.enter_context
            trst = Eh(nc.sbuf_tensor("trst", [128, O_FFN], BF16)).ap()
            xqT_sb = Eh(nc.sbuf_tensor("xqT_sb", [128, KT * S], BF16)).ap()
            sx_rep = Eh(nc.sbuf_tensor("sx_rep", [128, S], F32)).ap()
            diagst = Eh(nc.sbuf_tensor("diagst", [128, 128], F32)).ap()
            wst_g = Eh(nc.sbuf_tensor("wst_g", [128, 2048], BF16)).ap()
            wst_u = Eh(nc.sbuf_tensor("wst_u", [128, 2048], BF16)).ap()
            evG = Eh(nc.sbuf_tensor("evG", [128, 1024], F32)).ap()
            evU = Eh(nc.sbuf_tensor("evU", [128, 1024], F32)).ap()
            sgout = Eh(nc.sbuf_tensor("sgout", [128, 1024], F32)).ap()
            hstage = Eh(nc.sbuf_tensor("hstage", [128, 1024], F32)).ap()
            habs = Eh(nc.sbuf_tensor("habs", [128, 1024], F32)).ap()
            hmax_acc = Eh(nc.sbuf_tensor("hmax_acc", [128, S], F32)).ap()
            hmcol = Eh(nc.sbuf_tensor("hmcol", [128, 1], F32)).ap()
            hmc2 = Eh(nc.sbuf_tensor("hmc2", [128, 1], F32)).ap()
            hmc3 = Eh(nc.sbuf_tensor("hmc3", [128, 1], F32)).ap()
            p2ps = contextlib.ExitStack()
            ps_g = [p2ps.enter_context(nc.psum_tensor(f"ps_g{i}", [128, 1024], F32)).ap()
                    for i in range(2)]
            ps_u = [p2ps.enter_context(nc.psum_tensor(f"ps_u{i}", [128, 1024], F32)).ap()
                    for i in range(2)]
            ps_rep = ps_g[0][:, 0:128]

            trdone = 0
            for k in range(KT):
                tt_ = p.y(lambda e, k=k: e.dma_start_transpose(
                    trst[:, :O_FFN], d_qw1[:, k * 128:(k + 1) * 128]),
                    w=[(dss, qw1_done), (dms, trdone)], inc=dma)
                trdone = p.y(lambda e, k=k: e.dma_start(
                    out=d_w1t[k * 128:(k + 1) * 128, :], in_=trst[:, :O_FFN]),
                    w=[(dma, tt_)], inc=dms)
            wT_done = trdone

            xq_lds = 0
            for k in range(KT):
                xq_lds = p.y(lambda e, k=k: e.dma_start(
                    out=xqT_sb[:, k * S:(k + 1) * S], in_=d_xqT[k * 128:(k + 1) * 128, :]),
                    w=([(dss, xqT_written)] if k == 0 else []), inc=dma)

            rep_done = 0
            for t in range(16):
                dg = p.v(lambda e, t=t: e.tensor_scalar(diagst[:, :], ident[:, :],
                                                        sxcols[:, t:t + 1], None, ALU.mult),
                         w=[(te, rep_done)] if t else [])
                tm = p.t(lambda e: e.matmul(ps_rep[:, :], tmpones[:, :], diagst[:, :],
                                            start=True, stop=True),
                         w=[(ve, dg)], inc=True)
                rep_done = tm
                p.v(lambda e, t=t: e.tensor_copy(sx_rep[:, t * 128:(t + 1) * 128], ps_rep[:, :]),
                    w=[(te, tm)])
            sxrep_done = p.v(lambda e: e.memset(hmax_acc[:, :], 0.0))

            ev_free = [[0, 0], [0, 0]]

            wst_free = [0, 0]
            hst_free = 0
            habs_free = 0
            sg_free = 0
            evG_free = 0
            wst_dump_gate = 0
            for mp in range(44):
                wts = []
                for half, mrow in ((0, mp), (1, 44 + mp)):
                    wst = wst_g if half == 0 else wst_u
                    wl0 = [(te, wst_free[half])]
                    if mp == 0:
                        wl0.append((dms, wT_done))
                    if mp == 1 and half == 0 and wst_dump_gate:
                        wl0.append((dms, wst_dump_gate))
                    wts.append(p.y(lambda e, mrow=mrow, wst=wst: e.dma_start(
                        out=wst[:, :].rearrange("p (k c) -> p k c", k=KT),
                        in_=d_w1t[:, mrow * 128:(mrow + 1) * 128]
                            .rearrange("(k p) c -> p k c", p=128)),
                        w=wl0, inc=dmw))
                if d_dbg2 is not None and mp == 0:
                    crw = p.v(lambda e: e.tensor_copy(evU[:, :], wst_g[:, 0:1024]),
                              w=[(dmw, wts[0])])
                    wst_dump_gate = p.y(lambda e: e.dma_start(out=d_dbg2[128:256, :], in_=evU[:, :]),
                                        w=[(ve, crw)], inc=dms)
                mm_done = [[None, None], [None, None]]
                for half in range(2):
                    wst = wst_g if half == 0 else wst_u
                    psl = ps_g if half == 0 else ps_u
                    for sc in range(2):
                        for k in range(KT):
                            wl = []
                            if k == 0:
                                wl.append((dmw, wts[half]))
                                wl.append((ve, ev_free[half][sc]))
                                if mp == 0 and half == 0 and sc == 0:
                                    wl.append((dma, xq_lds))
                                    wl.append((ve, sxrep_done))
                            for half2 in range(2):
                                mm_done[half][sc] = p.t(lambda e, k=k, sc=sc, wst=wst, psl=psl,
                                                        h2=half2: e.matmul(
                                    psl[sc][:, h2 * 512:(h2 + 1) * 512],
                                    wst[:, k * 128:(k + 1) * 128],
                                    xqT_sb[:, k * S + sc * 1024 + h2 * 512:
                                           k * S + sc * 1024 + (h2 + 1) * 512],
                                    start=(k == 0), stop=(k == KT - 1)),
                                    w=(wl if half2 == 0 else []), inc=True)
                    wst_free[half] = mm_done[half][1]

                for sc in range(2):
                    ssl = slice(sc * 1024, (sc + 1) * 1024)
                    if d_dbg2 is not None and mp == 0 and sc == 0:
                        cr = p.v(lambda e: e.tensor_copy(evU[:, :], ps_g[0][:, :]),
                                 w=[(te, mm_done[0][0])])
                        p.y(lambda e: e.dma_start(out=d_dbg2[0:128, :], in_=evU[:, :]),
                            w=[(ve, cr)], inc=dms)
                        cr3 = p.v(lambda e: e.tensor_copy(evU[:, :], xqT_sb[:, 0:1024]))
                        p.y(lambda e: e.dma_start(out=d_dbg2[256:384, :], in_=evU[:, :]),
                            w=[(ve, cr3)], inc=dms)
                        p.v(lambda e: e.nop(), w=[(dms, dms.c)])
                    cG = p.v(lambda e, sc=sc, mp=mp, ssl=ssl: e.scalar_tensor_tensor(
                        evG[:, :], ps_g[sc][:, :], sw1c[:, mp:mp + 1], sx_rep[:, ssl],
                        ALU.mult, ALU.mult),
                        w=[(te, mm_done[0][sc]), (se, evG_free)])
                    ev_free[0][sc] = cG
                    sg = p.s(lambda e: e.activation(sgout[:, :], evG[:, :], AF.Silu),
                             w=[(ve, cG), (ve, sg_free)])
                    evG_free = sg
                    cU = p.v(lambda e, sc=sc, mp=mp, ssl=ssl: e.scalar_tensor_tensor(
                        evU[:, :], ps_u[sc][:, :], sw1c[:, 44 + mp:45 + mp], sx_rep[:, ssl],
                        ALU.mult, ALU.mult),
                        w=[(te, mm_done[1][sc])])
                    ev_free[1][sc] = cU
                    hh = p.v(lambda e: e.tensor_tensor(hstage[:, :], sgout[:, :], evU[:, :],
                                                       ALU.mult),
                             w=[(se, sg), (dss, hst_free)])
                    sg_free = hh
                    ab = p.s(lambda e: e.activation(habs[:, :], hstage[:, :], AF.Abs),
                             w=[(ve, hh), (ve, habs_free)])
                    habs_free = p.v(lambda e, ssl=ssl: e.tensor_tensor(
                        hmax_acc[:, ssl], hmax_acc[:, ssl], habs[:, :], ALU.max),
                        w=[(se, ab)])
                    hst_free = p.dma("scalar", lambda e, mp=mp, sc=sc: e.dma_start(
                        out=d_h[mp * 128:(mp + 1) * 128, sc * 1024:(sc + 1) * 1024],
                        in_=hstage[:, :]),
                        w=[(ve, hh)], dsem=dss)

            for k in range(I_FFN // 128):
                tt_ = p.y(lambda e, k=k: e.dma_start_transpose(
                    trst[:, :HID], d_qw2[:, k * 128:(k + 1) * 128]),
                    w=[(dss, qw2_done), (dms, trdone)], inc=dma)
                trdone = p.y(lambda e, k=k: e.dma_start(
                    out=d_w2t[k * 128:(k + 1) * 128, :], in_=trst[:, :HID]),
                    w=[(dma, tt_)], inc=dms)
            w2T_done = trdone

            trp_done = 0
            for t in range(16):
                tq = p.t(lambda e, t=t: e.transpose(ps_rep[:, :], hmax_acc[:, t * 128:(t + 1) * 128],
                                                    ident[:, :]),
                         w=[(ve, habs_free), (te, rep_done)] if t == 0 else [(ve, trp_done)],
                         inc=True)
                p.v(lambda e: e.tensor_reduce(hmcol[:, :], ps_rep[:, :], AXX, ALU.max),
                    w=[(te, tq)])
                p.v(lambda e: e.tensor_scalar(hmc2[:, :], hmcol[:, :], R127, None, ALU.mult))
                p.v(lambda e: e.scalar_tensor_tensor(hmc3[:, :], hmc2[:, :], -127.0, hmcol[:, :],
                                                     ALU.mult, ALU.add))
                p.v(lambda e, t=t: e.scalar_tensor_tensor(shcols[:, t:t + 1], hmc3[:, :], R127,
                                                          hmc2[:, :], ALU.mult, ALU.add))
                p.v(lambda e, t=t: e.tensor_scalar(shcols[:, t:t + 1], shcols[:, t:t + 1], 1e-8,
                                                   None, ALU.max))
                trp_done = p.v(lambda e, t=t: e.reciprocal(shrcp[:, t:t + 1], shcols[:, t:t + 1]))
            sh_done = trp_done
            p2ps.close()

        p.barrier(bar, dma_sems=(dma, dmx, dmw, dmg, dms, dss, dmq))

        with contextlib.ExitStack() as ph:
            Eh = ph.enter_context
            hq_sb = Eh(nc.sbuf_tensor("hq_sb", [128, 44 * 1024], BF16)).ap()
            sh_rep = Eh(nc.sbuf_tensor("sh_rep", [128, S], F32)).ap()
            rcph_rep = Eh(nc.sbuf_tensor("rcph_rep", [128, S], F32)).ap()
            diag2 = Eh(nc.sbuf_tensor("diag2", [128, 128], F32)).ap()
            htile = Eh(nc.sbuf_tensor("htile", [128, 1024], F32)).ap()
            fqa = Eh(nc.sbuf_tensor("fqa", [128, 1024], F32)).ap()
            fqb = Eh(nc.sbuf_tensor("fqb", [128, 1024], F32)).ap()
            hqi = Eh(nc.sbuf_tensor("hqi", [128, 1024], I32)).ap()
            w2st = Eh(nc.sbuf_tensor("w2st", [128, 44 * 128], BF16)).ap()
            w2st2 = Eh(nc.sbuf_tensor("w2st2", [128, 44 * 128], BF16)).ap()
            evf = Eh(nc.sbuf_tensor("evf", [128, 1024], F32)).ap()
            gout = Eh(nc.sbuf_tensor("gout", [128, 1024], F32)).ap()
            p2bps = contextlib.ExitStack()
            ps_rep2 = p2bps.enter_context(nc.psum_tensor("ps_rep2", [128, 128], F32)).ap()
            ps_f = [p2bps.enter_context(nc.psum_tensor(f"ps_f{i}", [128, 1024], F32)).ap()
                    for i in range(2)]

            rep_done = 0
            for t in range(16):
                dg = p.v(lambda e, t=t: e.tensor_scalar(diag2[:, :], ident[:, :],
                                                        shcols[:, t:t + 1], None, ALU.mult),
                         w=[(te, rep_done)])
                tm = p.t(lambda e: e.matmul(ps_rep2[:, :], tmpones[:, :], diag2[:, :],
                                            start=True, stop=True), w=[(ve, dg)], inc=True)
                p.v(lambda e, t=t: e.tensor_copy(sh_rep[:, t * 128:(t + 1) * 128], ps_rep2[:, :]),
                    w=[(te, tm)])
                dg = p.v(lambda e, t=t: e.tensor_scalar(diag2[:, :], ident[:, :],
                                                        shrcp[:, t:t + 1], None, ALU.mult))
                tm = p.t(lambda e: e.matmul(ps_rep2[:, :], tmpones[:, :], diag2[:, :],
                                            start=True, stop=True), w=[(ve, dg)], inc=True)
                rep_done = tm
                p.v(lambda e, t=t: e.tensor_copy(rcph_rep[:, t * 128:(t + 1) * 128], ps_rep2[:, :]),
                    w=[(te, tm)])

            g_stored = 0
            for sc in range(2):
                ssl = slice(sc * 1024, (sc + 1) * 1024)
                ht_free = 0
                hq_done = 0
                hq_done_k = [0] * 44
                for mp in range(44):
                    lt = p.y(lambda e, mp=mp, sc=sc: e.dma_start(
                        out=htile[:, :], in_=d_h[mp * 128:(mp + 1) * 128, sc * 1024:(sc + 1) * 1024]),
                        w=[(ve, ht_free)], inc=dma)
                    p.v(lambda e, ssl=ssl: e.tensor_tensor(fqa[:, :], htile[:, :], rcph_rep[:, ssl],
                                                           ALU.mult), w=[(dma, lt)])
                    ht_free = ve.c
                    p.v(lambda e: e.tensor_scalar(fqa[:, :], fqa[:, :], 127.0, None, ALU.min))
                    p.v(lambda e: e.tensor_scalar(hqi[:, :], fqa[:, :], -127.0, None, ALU.max))
                    p.v(lambda e: e.scalar_tensor_tensor(fqb[:, :], fqa[:, :], -127.0, hqi[:, :],
                                                         ALU.max, ALU.subtract))
                    p.v(lambda e: e.tensor_scalar(fqa[:, :], fqb[:, :], 0.5, None, ALU.is_gt))
                    p.v(lambda e: e.tensor_scalar(fqb[:, :], fqb[:, :], -0.5, None, ALU.is_lt))
                    p.v(lambda e: e.scalar_tensor_tensor(fqa[:, :], fqa[:, :], 1.0, hqi[:, :],
                                                         ALU.mult, ALU.add))
                    hq_done = p.v(lambda e, mp=mp: e.tensor_tensor(
                        hq_sb[:, mp * 1024:(mp + 1) * 1024], fqa[:, :], fqb[:, :], ALU.subtract))
                    hq_done_k[mp] = hq_done

                evf_free = 0
                gout_free = 0
                w2_free = [0, 0]
                mmf_prev = [0, 0]
                for m2 in range(16):
                    par = m2 % 2
                    w2b = w2st if par == 0 else w2st2
                    psf = ps_f[par]
                    wt = p.dma("scalar", lambda e, m2=m2, w2b=w2b: e.dma_start(
                        out=w2b[:, :].rearrange("p (k c) -> p k c", k=44),
                        in_=d_w2t[:, m2 * 128:(m2 + 1) * 128].rearrange("(k p) c -> p k c", p=128)),
                        w=[(te, w2_free[par])] + ([(dms, w2T_done)] if (sc == 0 and m2 < 2) else []),
                        dsem=dmq)
                    mmf = None
                    for k in range(44):
                        wl = []
                        if m2 < 2:
                            wl.append((ve, hq_done_k[k]))
                        elif k == 0:
                            wl.append((ve, hq_done))
                        if k == 0:
                            wl.append((dmq, wt))
                            wl.append((ve, mmf_prev[par]))
                        for half2 in range(2):
                            mmf = p.t(lambda e, k=k, w2b=w2b, psf=psf, h2=half2: e.matmul(
                                psf[:, h2 * 512:(h2 + 1) * 512],
                                w2b[:, k * 128:(k + 1) * 128],
                                hq_sb[:, k * 1024 + h2 * 512: k * 1024 + (h2 + 1) * 512],
                                start=(k == 0), stop=(k == 43)),
                                w=(wl if half2 == 0 else []), inc=True)
                    w2_free[par] = mmf
                    cf = p.v(lambda e, m2=m2, ssl=ssl, psf=psf: e.scalar_tensor_tensor(
                        evf[:, :], psf[:, :], sw2c[:, m2:m2 + 1], sh_rep[:, ssl],
                        ALU.mult, ALU.mult),
                        w=[(te, mmf), (se, evf_free)])
                    mmf_prev[par] = cf
                    sg2 = p.s(lambda e: e.activation(gout[:, :], evf[:, :], AF.Silu),
                              w=[(ve, cf), (dms, gout_free)])
                    evf_free = sg2
                    gout_free = p.y(lambda e, m2=m2, sc=sc: e.dma_start(
                        out=d_g[m2 * 128:(m2 + 1) * 128, sc * 1024:(sc + 1) * 1024],
                        in_=gout[:, :]),
                        w=[(se, sg2)], inc=dms)
                g_stored = gout_free
            p2bps.close()

        p.barrier(bar, dma_sems=(dma, dmx, dmw, dmg, dms, dss, dmq))

        with contextlib.ExitStack() as ph:
            Eh = ph.enter_context
            s_run = Eh(nc.sbuf_tensor("s_run", [128, NH * HD], F32)).ap()
            repbank = Eh(nc.sbuf_tensor("repbank", [64, NH * 64], F32)).ap()
            mask0 = Eh(nc.sbuf_tensor("mask0", [128, 256], F32)).ap()
            mask1 = Eh(nc.sbuf_tensor("mask1", [128, 256], F32)).ap()
            ones64 = Eh(nc.sbuf_tensor("ones64", [64, 256], F32)).ap()
            ddrepc = Eh(nc.sbuf_tensor("ddrepc", [128, 32], F32)).ap()
            ab_c = Eh(nc.sbuf_tensor("ab_c", [64, 256], F32)).ap()
            acum = Eh(nc.sbuf_tensor("acum", [64, 256], F32)).ap()
            eexp = Eh(nc.sbuf_tensor("eexp", [64, 256], F32)).ap()
            decayc = Eh(nc.sbuf_tensor("decayc", [64, 256], F32)).ap()
            acT = Eh(nc.sbuf_tensor("acT", [128, 128], F32)).ap()
            nacT = Eh(nc.sbuf_tensor("nacT", [128, 128], F32)).ap()
            dcT = Eh(nc.sbuf_tensor("dcT", [128, 128], F32)).ap()
            cdecbc = Eh(nc.sbuf_tensor("cdecbc", [128, 64], F32)).ap()
            diag3 = Eh(nc.sbuf_tensor("diag3", [128, 832], F32)).ap()
            hsdt_ct = Eh(nc.sbuf_tensor("hsdt_ct", [128, 2 * 256], F32)).ap()
            hsdtT = Eh(nc.sbuf_tensor("hsdtT", [128, 2 * INTER], F32)).ap()
            hsr = Eh(nc.sbuf_tensor("hsr", [128, 2 * 256], F32)).ap()
            hse = Eh(nc.sbuf_tensor("hse", [128, 2 * 256], F32)).ap()
            b_ch = Eh(nc.sbuf_tensor("b_ch", [128, 8 * 256], F32)).ap()
            c_ch = Eh(nc.sbuf_tensor("c_ch", [128, 8 * 256], F32)).ap()
            btsb = Eh(nc.sbuf_tensor("btsb", [128, 2 * 128], F32)).ap()
            gsb = Eh(nc.sbuf_tensor("gsb", [128, 512], F32)).ap()
            dmst = Eh(nc.sbuf_tensor("dmst", [128, 2 * 512], F32)).ap()
            ltsb = Eh(nc.sbuf_tensor("ltsb", [128, 2 * 512], F32)).ap()
            mtsb = Eh(nc.sbuf_tensor("mtsb", [128, 2 * 512], F32)).ap()
            bdst = Eh(nc.sbuf_tensor("bdst", [128, 2 * 256], F32)).ap()
            yfall = Eh(nc.sbuf_tensor("yfall", [128, 32 * 256], F32)).ap()
            gate_ct = Eh(nc.sbuf_tensor("gate_ct", [128, 2 * 256], F32)).ap()
            ebsb = Eh(nc.sbuf_tensor("ebsb", [128, 256], F32)).ap()
            yofe = Eh(nc.sbuf_tensor("yofe", [128, 256], F32)).ap()
            ysb = Eh(nc.sbuf_tensor("ysb", [128, 256], F32)).ap()
            sgate = Eh(nc.sbuf_tensor("sgate", [128, 256], F32)).ap()
            rstdb = Eh(nc.sbuf_tensor("rstdb", [128, 256], F32)).ap()
            wopst = Eh(nc.sbuf_tensor("wopst", [128, 32 * 128], F32)).ap()
            wopst2 = Eh(nc.sbuf_tensor("wopst2", [128, 32 * 128], F32)).ap()
            g_m = Eh(nc.sbuf_tensor("g_m", [128, 256], F32)).ap()
            x_m = Eh(nc.sbuf_tensor("x_m", [128, 2 * 128], F32)).ap()
            zsb = Eh(nc.sbuf_tensor("zsb", [128, 256], F32)).ap()
            osb = Eh(nc.sbuf_tensor("osb", [128, 2 * 128], F32)).ap()

            p3ps = contextlib.ExitStack()
            PS_dt = p3ps.enter_context(nc.psum_tensor("PS_dt", [128, 512], F32)).ap()
            PS_trA = p3ps.enter_context(nc.psum_tensor("PS_trA", [128, 512], F32)).ap()
            PS_trB = p3ps.enter_context(nc.psum_tensor("PS_trB", [128, 512], F32)).ap()
            PS_G = p3ps.enter_context(nc.psum_tensor("PS_G", [128, 512], F32)).ap()
            PS_sel = p3ps.enter_context(nc.psum_tensor("PS_sel", [128, 512], F32)).ap()
            PS_acc = p3ps.enter_context(nc.psum_tensor("PS_acc", [128, 512], F32)).ap()
            PS_yd = p3ps.enter_context(nc.psum_tensor("PS_yd", [128, 512], F32)).ap()
            PS_yo = p3ps.enter_context(nc.psum_tensor("PS_yo", [128, 512], F32)).ap()

            p.v(lambda e: e.memset(s_run[:, :], 0.0))
            p.v(lambda e: e.memset(ones64[:, :], 1.0))
            p.g(lambda e: e.memset(repbank[:, :], 1.0))
            p.g(lambda e: e.affine_select(repbank[:, :], repbank[:, :], [[1, 64], [0, 64]],
                                          ALU.is_equal, 0.0, base=0, channel_multiplier=-1))
            p.g(lambda e: e.memset(mask0[:, :], 0.0))
            p.g(lambda e: e.affine_select(mask0[:, :], mask0[:, :], [[1, 256]],
                                          ALU.is_ge, NEG, base=0, channel_multiplier=-1))
            p.g(lambda e: e.memset(mask1[:, :], 0.0))
            gcst = p.g(lambda e: e.affine_select(mask1[:, :], mask1[:, :], [[1, 256]],
                                                 ALU.is_ge, NEG, base=-128, channel_multiplier=-1))
            ddrep_ld = p.y(ld(ddrepc[:, :], d_drep.rearrange("(mt p) -> p mt", p=128), True),
                           inc=dma)
            consts3 = p.v(lambda e: e.nop(), w=[(ge, gcst), (dma, ddrep_ld)])

            diag_cons = 0
            dg6_cons = [0] * 6
            dgv_done = [0] * 6
            lt2_last = [0, 0]
            yd2_last = [0, 0]
            sel2_free = [0, 0]
            yd_evac2 = [0, 0]
            st2_cons = [0, 0]
            mt2_done = [0, 0]
            bd2_done = [0, 0]
            hse_free = [0, 0]
            sqrd_done0 = 0
            sqrd_done1 = 0
            bt_done = 0
            lt_last = 0
            yd_last = 0
            hs_free = 0
            gate_free = [0, 0]
            sgate_free = 0
            gm_free = [0, 0]
            xm_free = [0, 0]
            bc_free = 0
            acum_free = 0
            hsdtT_cons = 0
            yo_evac = 0
            st_upd = 0
            sel_free = 0
            acc_free = 0
            tr_free = [0, 0]
            out_dma = 0
            for c in range(NCH):
                csl = slice(c * CH, (c + 1) * CH)
                wl = [(te, acum_free)]
                if c == 0:
                    wl.append((ve, consts3))
                av = p.v(lambda e, csl=csl: e.tensor_scalar(ab_c[:, :], dt_sb[:, csl], acol[:, :],
                                                            None, ALU.mult), w=wl)
                scan_done = p.v(lambda e: e.tensor_tensor_scan(acum[:, :], ones64[:, :], ab_c[:, :],
                                                               0.0, ALU.mult, ALU.add))
                sc1 = p.s(lambda e: e.activation(eexp[:, :], acum[:, :], AF.Exp), w=[(ve, scan_done)])
                dv1 = p.v(lambda e: e.tensor_scalar(decayc[:, :], acum[:, :],
                                                    acum[:, 255:256], None, ALU.subtract))
                sc2 = p.s(lambda e: e.activation(decayc[:, :], decayc[:, :], AF.Exp, scale=-1.0),
                          w=[(ve, dv1)])
                for t in range(2):
                    tq = p.t(lambda e, t=t: e.transpose(PS_trA[:, 0:64],
                                                        acum[0:64, t * 128:(t + 1) * 128],
                                                        ident[0:64, 0:64]),
                             w=[(ve, tr_free[0]), (ve, scan_done)], inc=True)
                    tr_free[0] = p.v(lambda e, t=t: e.tensor_copy(acT[:, t * 64:(t + 1) * 64],
                                                                  PS_trA[:, 0:64]),
                                     w=[(te, tq)])
                    tq = p.t(lambda e, t=t: e.transpose(PS_trB[:, 0:64],
                                                        decayc[0:64, t * 128:(t + 1) * 128],
                                                        ident[0:64, 0:64]),
                             w=[(ve, tr_free[1]), (se, sc2)], inc=True)
                    tr_free[1] = p.v(lambda e, t=t: e.tensor_copy(dcT[:, t * 64:(t + 1) * 64],
                                                                  PS_trB[:, 0:64]),
                                     w=[(te, tq)])
                acT_ok = p.v(lambda e: e.tensor_scalar(nacT[:, :], acT[:, :], -1.0, None, ALU.mult))
                dgv = p.v(lambda e: e.tensor_scalar(diag3[0:64, 768:832], ident[0:64, 0:64],
                                                    eexp[:, 255:256], None, ALU.mult),
                          w=[(se, sc1), (te, diag_cons)])
                tcd = p.t(lambda e: e.matmul(PS_acc[:, 256:320], tmpones[0:64, :],
                                             diag3[0:64, 768:832], start=True, stop=True),
                          w=[(ve, dgv), (ve, acc_free)], inc=True)
                diag_cons = tcd
                cdec_done = p.v(lambda e: e.tensor_copy(cdecbc[:, :], PS_acc[:, 256:320]),
                                w=[(te, tcd)])

                def emit_dg(hx):
                    for t in range(2):
                        sl = (hx % 3) * 2 + t
                        dgv_done[sl] = p.v(lambda e, hx=hx, t=t, sl=sl: e.tensor_scalar(
                            diag3[:, sl * 128:(sl + 1) * 128], ident[:, :],
                            acT[:, t * 64 + hx: t * 64 + hx + 1], None, ALU.mult),
                            w=[(te, dg6_cons[sl])])
                emit_dg(0)
                emit_dg(1)
                ldb = p.y(lambda e, c=c: e.dma_start(
                    out=b_ch[:, :].rearrange("p (g l) -> p g l", g=8),
                    in_=d_hbcc[INTER:INTER + 1024, c * CH:(c + 1) * CH]
                        .rearrange("(g p) l -> p g l", p=128)),
                    w=[(te, bc_free)], inc=dma)
                ldc = p.y(lambda e, c=c: e.dma_start(
                    out=c_ch[:, :].rearrange("p (g l) -> p g l", g=8),
                    in_=d_hbcc[INTER + 1024:INTER + 2048, c * CH:(c + 1) * CH]
                        .rearrange("(g p) l -> p g l", p=128)),
                    inc=dma)
                hdv_hist = [0, 0]
                trc_hist = [0, 0]
                for ct in range(32):
                    par = ct % 2
                    psdt = PS_dt if par == 0 else PS_yo
                    pstr3 = PS_trA if par == 0 else PS_trB
                    lh = p.y(lambda e, ct=ct, c=c, par=par: e.dma_start(
                        out=hsr[:, par * 256:(par + 1) * 256],
                        in_=d_hbcc[ct * 128:(ct + 1) * 128, c * CH:(c + 1) * CH]),
                        w=[(ve, hdv_hist[par])], inc=dma)
                    tdt = p.t(lambda e, ct=ct, c=c, psdt=psdt: e.matmul(
                        psdt[:, 0:256], repbank[:, ct * 128:(ct + 1) * 128],
                        dt_sb[0:64, c * CH:(c + 1) * CH],
                        start=True, stop=True),
                        w=[(ve, hdv_hist[par]), (ve, av)], inc=True)
                    hdv = p.v(lambda e, ct=ct, par=par, psdt=psdt: e.tensor_tensor(
                        hsdt_ct[:, par * 256:(par + 1) * 256],
                        hsr[:, par * 256:(par + 1) * 256], psdt[:, 0:256],
                        ALU.mult),
                        w=[(te, tdt), (dma, lh)])
                    hdv_hist[par] = hdv
                    tqs = []
                    for t in range(2):
                        wl0 = [(ve, trc_hist[par]), (ve, hdv), (te, hsdtT_cons)]
                        if ct < 2:
                            wl0.append((ve, sel2_free[1] if par == 0 else yd_evac2[1]))
                        tqs.append(p.t(lambda e, t=t, par=par, pstr3=pstr3: e.transpose(
                            pstr3[:, 128 + t * 128:256 + t * 128],
                            hsdt_ct[:, par * 256 + t * 128: par * 256 + (t + 1) * 128],
                            ident[:, :]),
                            w=(wl0 if t == 0 else []), inc=True))
                    cps = None
                    for t in range(2):
                        cps = p.v(lambda e, ct=ct, t=t, pstr3=pstr3: e.tensor_copy(
                            hsdtT[:, t * INTER + ct * 128: t * INTER + (ct + 1) * 128],
                            pstr3[:, 128 + t * 128:256 + t * 128]),
                            w=[(te, tqs[1])])
                    trc_hist[par] = cps
                hsdtT_done = p.v(lambda e: e.nop())

                yf_lasts = [0] * 32
                last_states_mm = 0
                last_yoff_mm = 0
                last_ebig_mm = 0
                gv_done = 0
                pair_idx = 0
                for g in range(8):
                    wl = [(ve, st_upd), (dma, ldc)]
                    tg_ = None
                    for t in range(2):
                        tg_ = p.t(lambda e, g=g, t=t: e.matmul(
                            PS_G[:, t * 256:(t + 1) * 256],
                            b_ch[:, g * 256 + t * 128: g * 256 + (t + 1) * 128],
                            c_ch[:, g * 256:(g + 1) * 256], start=True, stop=True),
                            w=(wl if t == 0 else []), inc=True)
                    gv_done = p.v(lambda e: e.tensor_copy(gsb[:, :], PS_G[:, :]),
                                  w=[(te, tg_)])
                    tq2 = None
                    for t in range(2):
                        tq2 = p.t(lambda e, g=g, t=t: e.transpose(
                            PS_G[:, t * 128:(t + 1) * 128],
                            b_ch[:, g * 256 + t * 128: g * 256 + (t + 1) * 128],
                            ident[:, :]),
                            w=[(ve, gv_done)] if t == 0 else [], inc=True)
                    bt_done = None
                    for t in range(2):
                        bt_done = p.v(lambda e, t=t: e.tensor_copy(
                            btsb[:, t * 128:(t + 1) * 128], PS_G[:, t * 128:(t + 1) * 128]),
                            w=[(te, tq2)])

                    def emit_lchain(h):
                        hp = h % 2
                        ps_ar = PS_sel if hp == 0 else PS_trA
                        ta = None
                        if h + 2 < 64:
                            emit_dg(h + 2)
                        for t in range(2):
                            sl = (h % 3) * 2 + t
                            ta = p.t(lambda e, t=t, hp=hp, sl=sl, ps_ar=ps_ar: e.matmul(
                                ps_ar[:, t * 128:(t + 1) * 128], tmpones[:, :],
                                diag3[:, sl * 128:(sl + 1) * 128],
                                start=True, stop=True),
                                w=[(ve, dgv_done[sl])] + ([(ve, sel2_free[hp])] if t == 0 else []),
                                inc=True)
                            dg6_cons[sl] = ta
                        p.v(lambda e, h=h, hp=hp: e.tensor_scalar(
                            bdst[:, hp * 256: hp * 256 + 128], btsb[:, 0:128],
                            dcT[:, 0 * 64 + h: 0 * 64 + h + 1], None, ALU.mult),
                            w=[(te, st2_cons[hp])] + ([(ve, bt_done)] if h % 8 == 0 else []))
                        bd2_done[hp] = p.v(lambda e, h=h, hp=hp: e.tensor_scalar(
                            bdst[:, hp * 256 + 128: hp * 256 + 256], btsb[:, 128:256],
                            dcT[:, 1 * 64 + h: 1 * 64 + h + 1], None, ALU.mult))
                        mt_done = 0
                        for t in range(2):
                            msk = mask0 if t == 0 else mask1
                            dm = p.v(lambda e, h=h, t=t, msk=msk, hp=hp, ps_ar=ps_ar:
                                     e.scalar_tensor_tensor(
                                dmst[:, hp * 512 + t * 256: hp * 512 + (t + 1) * 256],
                                ps_ar[:, 0:256],
                                nacT[:, t * 64 + h: t * 64 + h + 1],
                                msk[:, :], ALU.add, ALU.add),
                                w=[(te, ta), (se, lt2_last[hp])])
                            sel2_free[hp] = dm
                            lt_ = p.s(lambda e, t=t, hp=hp: e.activation(
                                ltsb[:, hp * 512 + t * 256: hp * 512 + (t + 1) * 256],
                                dmst[:, hp * 512 + t * 256: hp * 512 + (t + 1) * 256],
                                AF.Exp), w=[(ve, dm)])
                            lt2_last[hp] = lt_
                            mt_done = p.v(lambda e, t=t, hp=hp, g=h // 8: e.tensor_tensor(
                                mtsb[:, hp * 512 + t * 256: hp * 512 + (t + 1) * 256],
                                ltsb[:, hp * 512 + t * 256: hp * 512 + (t + 1) * 256],
                                gsb[:, t * 256:(t + 1) * 256], ALU.mult),
                                w=[(se, lt_), (ve, gv_done), (te, yd2_last[hp])])
                        mt2_done[hp] = mt_done

                    def emit_consume(h):
                        nonlocal last_states_mm, last_yoff_mm, last_ebig_mm, yo_evac
                        nonlocal pair_idx, cdec_done
                        g = h // 8
                        hh = h % 8
                        hp = h % 2
                        pp = (h // 2) % 2
                        ps_ydx = PS_yd if pp == 0 else PS_trB
                        for t in range(2):
                            yd2_last[hp] = p.t(lambda e, h=h, t=t, hh=hh, hp=hp, ps_ydx=ps_ydx:
                                               e.matmul(
                                ps_ydx[(hh % 2) * 64:(hh % 2) * 64 + 64, 0:256],
                                hsdtT[:, t * INTER + h * 64:t * INTER + (h + 1) * 64],
                                mtsb[:, hp * 512 + t * 256: hp * 512 + (t + 1) * 256],
                                start=(t == 0), stop=(t == 1)),
                                w=([(ve, mt2_done[hp])] if t == 0 else []) +
                                  ([(ve, yd_evac2[pp])] if (hh % 2 == 0 and t == 0) else []),
                                inc=True)
                        for t in range(2):
                            last_states_mm = p.t(lambda e, h=h, t=t, hh=hh, hp=hp: e.matmul(
                                PS_G[:, hh * 64:(hh + 1) * 64],
                                bdst[:, hp * 256 + t * 128: hp * 256 + (t + 1) * 128],
                                hsdtT[:, t * INTER + h * 64:t * INTER + (h + 1) * 64],
                                start=(hh == 0 and t == 0), stop=(hh == 7 and t == 1)),
                                w=([(ve, bd2_done[hp])] if t == 0 else []), inc=True)
                        st2_cons[hp] = last_states_mm

                        if hh % 2 == 1:
                            ct = g * 4 + hh // 2
                            last_yoff_mm = p.t(lambda e, ct=ct, g=g: e.matmul(
                                PS_yo[:, 0:256], s_run[:, ct * 128:(ct + 1) * 128],
                                c_ch[:, g * 256:(g + 1) * 256], start=True, stop=True),
                                w=[(ve, yo_evac)] + ([(ve, hsdtT_done)] if ct == 0 else []),
                                inc=True)
                            last_ebig_mm = p.t(lambda e, ct=ct, c=c: e.matmul(
                                PS_dt[:, 256:512], repbank[:, ct * 128:(ct + 1) * 128],
                                eexp[:, :], start=True, stop=True),
                                w=[(ve, yo_evac)] + ([(ve, hsdtT_done)] if ct == 0 else []),
                                inc=True)
                            ev1 = p.v(lambda e: e.tensor_copy(ebsb[:, :], PS_dt[:, 256:512]),
                                      w=[(te, last_ebig_mm)])
                            yo_evac = p.v(lambda e: e.tensor_tensor(yofe[:, :], PS_yo[:, 0:256],
                                                                    ebsb[:, :], ALU.mult),
                                          w=[(te, last_yoff_mm), (te, sqrd_done1)])
                            gring = ct % 2
                            lg = p.y(lambda e, ct=ct, c=c, gring=gring: e.dma_start(
                                out=gate_ct[:, gring * 256:(gring + 1) * 256],
                                in_=d_gate[ct * 128:(ct + 1) * 128, c * CH:(c + 1) * CH]),
                                w=[(ve, gate_free[gring])], inc=dma)

                            lhs2 = p.y(lambda e, ct=ct, c=c, gring=gring: e.dma_start(
                                out=hse[:, gring * 256:(gring + 1) * 256],
                                in_=d_hbcc[ct * 128:(ct + 1) * 128, c * CH:(c + 1) * CH]),
                                w=[(ve, hse_free[gring])], inc=dma)
                            t1 = p.v(lambda e, ct=ct, ps_ydx=ps_ydx, gring=gring:
                                     e.scalar_tensor_tensor(
                                ysb[:, :], hse[:, gring * 256:(gring + 1) * 256],
                                ddrepc[:, ct:ct + 1], ps_ydx[:, 0:256], ALU.mult, ALU.add),
                                w=[(te, te.c), (te, sqrd_done0), (dma, lhs2)])
                            yd_evac2[pp] = t1
                            hse_free[gring] = t1
                            p.v(lambda e: e.tensor_tensor(ysb[:, :], ysb[:, :], yofe[:, :],
                                                          ALU.add))
                            yf = p.v(lambda e, ct=ct, gring=gring: e.tensor_tensor(
                                yfall[:, ct * 256:(ct + 1) * 256], ysb[:, :],
                                gate_ct[:, gring * 256:(gring + 1) * 256], ALU.mult),
                                w=[(dma, lg)])
                            yf_lasts[ct] = yf
                            gate_free[gring] = yf
                            pair_idx += 1

                    for hh in range(8):
                        emit_lchain(g * 8 + hh)
                        if hh > 0:
                            emit_consume(g * 8 + hh - 1)
                    emit_consume(g * 8 + 7)
                    for hh in range(8):
                        h = g * 8 + hh
                        st_upd = p.v(lambda e, h=h, hh=hh: e.scalar_tensor_tensor(
                            s_run[:, h * 64:(h + 1) * 64], s_run[:, h * 64:(h + 1) * 64],
                            cdecbc[:, (h % 64):(h % 64) + 1], PS_G[:, hh * 64:(hh + 1) * 64],
                            ALU.mult, ALU.add),
                            w=[(te, last_states_mm), (ve, cdec_done)] if hh == 0 else
                              [(te, last_states_mm)])
                bc_free = last_yoff_mm
                acum_free = last_ebig_mm
                hsdtT_cons = last_states_mm

                sq_reads = [ysb, yofe]
                sqrd_done = [sqrd_done0, sqrd_done1]
                for ct in range(32):
                    buf = sq_reads[ct % 2]
                    wl5 = [(ve, yf_lasts[ct]), (te, sqrd_done[ct % 2])]
                    if ct < 2:
                        wl5.append((ve, yf_lasts[31]))
                    sq = p.s(lambda e, ct=ct, buf=buf: e.activation(
                        buf[:, :], yfall[:, ct * 256:(ct + 1) * 256], AF.Square),
                        w=wl5)
                    sqrd_done[ct % 2] = sq_mm = p.t(lambda e, ct=ct, buf=buf: e.matmul(
                        PS_acc[:, 0:256], tmpones[:, :], buf[:, :],
                        start=(ct == 0), stop=(ct == 31)),
                        w=[(se, sq)] + ([(ve, acc_free)] if ct == 0 else []),
                        inc=True)

                sqrd_done0, sqrd_done1 = sqrd_done

                yf_dump = 0
                if d_yf is not None:
                    for ct in range(32):
                        yf_dump = p.y(lambda e, ct=ct, c=c: e.dma_start(
                            out=d_yf[ct * 128:(ct + 1) * 128, c * CH:(c + 1) * CH],
                            in_=yfall[:, ct * 256:(ct + 1) * 256]),
                            w=[(ve, ve.c)], inc=dms)
                sumsq_mm = te.c
                rv = p.v(lambda e: e.tensor_scalar(rstdb[:, :], PS_acc[:, 0:256],
                                                   1.0 / INTER, 1e-6, ALU.mult, ALU.add),
                         w=[(te, sumsq_mm)])
                acc_free = rv
                p.v(lambda e: e.reciprocal(rstdb[:, :], rstdb[:, :]))
                sqr = p.s(lambda e: e.activation(rstdb[:, :], rstdb[:, :], AF.Sqrt),
                          w=[(ve, ve.c)])
                yn_done = 0
                for ct in range(32):
                    wl4 = [(se, sqr)] if ct == 0 else []
                    if d_yf is not None and ct == 0:
                        wl4.append((dms, yf_dump))
                    yn_done = p.v(lambda e, ct=ct: e.scalar_tensor_tensor(
                        yfall[:, ct * 256:(ct + 1) * 256], yfall[:, ct * 256:(ct + 1) * 256],
                        nw_c[:, ct:ct + 1], rstdb[:, :], ALU.mult, ALU.mult),
                        w=wl4)

                yn_dump = 0
                if d_yn is not None:
                    for ct in range(32):
                        yn_dump = p.y(lambda e, ct=ct, c=c: e.dma_start(
                            out=d_yn[ct * 128:(ct + 1) * 128, c * CH:(c + 1) * CH],
                            in_=yfall[:, ct * 256:(ct + 1) * 256]),
                            w=[(ve, yn_done)], inc=dms)
                    yn_done = p.v(lambda e: e.nop(), w=[(dms, yn_dump)])
                if d_rs is not None:
                    p.y(lambda e, c=c: e.dma_start(
                        out=d_rs[:, c * CH:(c + 1) * CH], in_=rstdb[:, :]),
                        w=[(ve, yn_done)], inc=dms)

                op_banks = [PS_sel, PS_yo]
                op_evac2 = [sel2_free[0], yo_evac]
                wop_bufs = [wopst, wopst2]
                wop_cons = [0, 0]
                for m in range(16):
                    psop = op_banks[m % 2]
                    wop = wop_bufs[m % 2]
                    wts2 = []
                    for half3 in range(2):
                        wts2.append(p.dma("scalar", lambda e, m=m, half3=half3, wop=wop: e.dma_start(
                            out=wop[:, half3 * 16 * 128:(half3 + 1) * 16 * 128]
                                .rearrange("p (k c) -> p k c", k=16),
                            in_=d_wout[half3 * 2048:(half3 + 1) * 2048, m * 128:(m + 1) * 128]
                                .rearrange("(k p) c -> p k c", p=128)),
                            w=[(te, wop_cons[m % 2])] if half3 == 0 else [], dsem=dmg))
                    lgm = p.y(lambda e, m=m, c=c: e.dma_start(
                        out=g_m[:, :],
                        in_=d_g[m * 128:(m + 1) * 128, c * CH:(c + 1) * CH]),
                        w=[(ve, gm_free[0])], inc=dma)
                    lxm = p.y(lambda e, m=m, c=c: e.dma_start(
                        out=x_m[:, :].rearrange("p (lb cc) -> p lb cc", lb=2),
                        in_=d_x[c * CH:(c + 1) * CH, m * 128:(m + 1) * 128]
                            .rearrange("(lb p) cc -> p lb cc", p=128)),
                        w=[(ve, xm_free[0])], inc=dma)
                    mmop = None
                    for kt in range(32):
                        wl3 = []
                        if kt == 0:
                            wl3 = [(dmg, wts2[0]), (ve, op_evac2[m % 2]), (ve, yn_done)]
                        elif kt == 16:
                            wl3 = [(dmg, wts2[1])]
                        mmop = p.t(lambda e, kt=kt, m=m, psop=psop, wop=wop: e.matmul(
                            psop[:, 0:256], wop[:, kt * 128:(kt + 1) * 128],
                            yfall[:, kt * 256:(kt + 1) * 256],
                            start=(kt == 0), stop=(kt == 31)),
                            w=wl3, inc=True)
                    wop_cons[m % 2] = mmop
                    zv = p.v(lambda e, m=m, psop=psop: e.tensor_tensor(
                        zsb[:, :], psop[:, 0:256], g_m[:, :], ALU.mult),
                        w=[(te, mmop), (dma, lgm)])
                    op_evac2[m % 2] = zv
                    gm_free[0] = zv
                    ob_done = 0
                    for lb in range(2):
                        par = lb
                        pstr3 = PS_trA if par == 0 else PS_trB
                        tz = p.t(lambda e, lb=lb, pstr3=pstr3: e.transpose(
                            pstr3[:, 256:384], zsb[:, lb * 128:(lb + 1) * 128], ident[:, :]),
                            w=[(ve, tr_free[par]), (ve, zv)], inc=True)
                        tr_free[par] = p.v(lambda e, lb=lb, pstr3=pstr3: e.tensor_tensor(
                            osb[:, lb * 128:(lb + 1) * 128], pstr3[:, 256:384],
                            x_m[:, lb * 128:(lb + 1) * 128], ALU.add),
                            w=[(te, tz), (dma, lxm), (dms, out_dma)])
                        ob_done = tr_free[par]
                    xm_free[0] = ob_done
                    out_dma = p.y(lambda e, m=m, c=c: e.dma_start(
                        out=d_out[c * CH:(c + 1) * CH, m * 128:(m + 1) * 128]
                            .rearrange("(lb p) cc -> p lb cc", p=128),
                        in_=osb[:, :].rearrange("p (lb cc) -> p lb cc", lb=2)),
                        w=[(ve, ob_done)], inc=dms)
                sel2_free[0] = op_evac2[0]
                yo_evac = op_evac2[1]

        p.y(lambda e: e.nop(), w=[(dms, dms.c), (dma, dma.c), (dmw, dmw.c), (dmx, dmx.c), (dmg, dmg.c), (dss, dss.c), (dmq, dmq.c)], inc=None)

        with nc.Block() as block:
            block.tensor(p.replay("tensor"))
            block.vector(p.replay("vector"))
            block.scalar(p.replay("scalar"))
            block.gpsimd(p.replay("gpsimd"))
            block.sync(p.replay("sync"))

    return nc


def _shard_inputs(inputs):
    x = np.ascontiguousarray(inputs["L0_attn_input_states"], dtype=np.float32)
    common = {
        "w_in": np.ascontiguousarray(inputs["L0_attn_in_proj_w"], dtype=np.float32),
        "conv_w": np.ascontiguousarray(np.asarray(inputs["L0_attn_conv_w"]).reshape(4, CONVD), dtype=np.float32),
        "conv_b": np.ascontiguousarray(inputs["L0_attn_conv_b"], dtype=np.float32),
        "dt_bias": np.ascontiguousarray(inputs["L0_attn_dt_bias"], dtype=np.float32),
        "a_log": np.ascontiguousarray(inputs["L0_attn_A_log"], dtype=np.float32),
        "d_d": np.ascontiguousarray(inputs["L0_attn_D"], dtype=np.float32),
        "norm_w": np.ascontiguousarray(inputs["L0_attn_norm_w"], dtype=np.float32),
        "w_out": np.ascontiguousarray(inputs["L0_attn_out_proj_w"], dtype=np.float32),
        "w1": np.ascontiguousarray(inputs["L0_ffn_w_gate_up"], dtype=np.float32),
        "w2": np.ascontiguousarray(inputs["L0_ffn_w_down"], dtype=np.float32),
        "d_rep": np.ascontiguousarray(np.repeat(np.asarray(inputs["L0_attn_D"], dtype=np.float32), HD)),
    }
    in_maps = []
    for c in range(8):
        m = dict(common)
        m["x"] = np.ascontiguousarray(x[c])
        in_maps.append(m)
    return in_maps


def kernel(**inputs):
    in_maps = _shard_inputs(inputs)
    nc = build()
    res = run_bass_kernel_spmd(nc, in_maps, core_ids=list(range(8)))
    return np.stack([r["out"] for r in res.results]).astype(np.float32)
